# version 07
# baseline (speedup 1.0000x reference)
import math
from contextlib import ExitStack

import numpy as np
import ml_dtypes

import concourse.bass as bass
import concourse.mybir as mybir
from concourse.bass_utils import run_bass_kernel_spmd

F32 = mybir.dt.float32
BF16 = mybir.dt.bfloat16
I32 = mybir.dt.int32
AF = mybir.ActivationFunctionType
ALU = mybir.AluOpType
AX = mybir.AxisListType
BF = ml_dtypes.bfloat16

B, S, D = 4, 4096, 4096
TOK = 2048
NHEAD = 32
HD = 128
HPC = 16
BLK = 256
NB = S // BLK
EPS_LN = 1e-5
EPS_RMS = 1e-5
NCORE = 8



def _l0_slopes():
    base = 1.0 / 2.0 ** (8.0 / NHEAD)
    exponent = np.arange(NHEAD, dtype=np.float32) + 1.0
    factor = 1.0 - 1 / (32 - 1 + 1e-5) + 1e-5
    return (base ** exponent * factor).astype(np.float32)


def _l1_slopes():
    n = 32
    cp2 = 2 ** math.floor(math.log2(n))
    base = 2 ** (-(2 ** -(math.log2(cp2) - 3)))
    powers = np.arange(1, 1 + cp2, dtype=np.float32)
    return np.power(base, powers).astype(np.float32)


def host_tables(core):
    h = core % 2
    my_heads = list(range(h * HPC, h * HPC + HPC))
    sl0 = _l0_slopes()
    r = np.arange(BLK, dtype=np.float32) + 1.0
    t = {}
    qd = np.exp(-sl0[my_heads][:, None] * r[None, :]).astype(np.float32)
    kd = np.exp(-sl0[my_heads][:, None] * (BLK - r)[None, :]).astype(np.float32)
    bd = np.exp(-sl0[my_heads] * BLK).astype(np.float32)
    diff = r[:, None] - r[None, :]
    ddT = []
    for hh in my_heads:
        dg = sl0[hh] * (-diff)
        dmat = np.where(diff >= 0, np.exp(-sl0[hh] * diff, dtype=np.float32), 0.0)
        ddT.append(dmat.T.astype(np.float32))
    t["qd"] = qd
    t["kd"] = np.ascontiguousarray(kd.reshape(16, 2, 128).transpose(0, 2, 1))
    t["bd"] = np.repeat(bd[:, None], 128, 1)[:, :, None].astype(np.float32)
    t["ddT"] = np.stack(ddT)
    t["qd_bc"] = np.repeat(qd[:, None, :], 128, 1).astype(np.float32)
    heads_l1 = [2 * j + h for j in range(HPC)]
    sl1 = _l1_slopes()[heads_l1]
    kpos = np.arange(S, dtype=np.float32).reshape(32, 128)
    t["sl_kpos"] = np.ascontiguousarray(
        (sl1[:, None, None] * kpos[None]).transpose(0, 2, 1)).astype(np.float32)
    t["slqp"] = (sl1[:, None] * np.arange(S, dtype=np.float32)[None, :]).astype(np.float32)
    masks = np.zeros((4, 128, 512), np.float32)
    for a in range(4):
        kk = a * 128 + np.arange(128)
        qq = np.arange(512)
        masks[a] = np.where(kk[:, None] <= qq[None, :], 0.0, -1.0e8)
    t["dmask"] = masks
    t["ident"] = np.eye(128, dtype=BF)
    par = (np.arange(128) % 2 == 1).astype(np.float32)[:, None]
    t["odd_par"] = par
    return t


def host_prep(inputs):
    x = np.asarray(inputs["L0_attn_hidden_states"])
    qkv_wT = np.ascontiguousarray(np.asarray(inputs["L0_attn_qkv_w"]).T)
    out_wT = np.ascontiguousarray(np.asarray(inputs["L0_attn_out_w"]).T)
    gate_wT = np.ascontiguousarray(np.asarray(inputs["L0_attn_gate_w"]).T)
    norm_w = np.asarray(inputs["L0_attn_norm_w"])
    packed = np.asarray(inputs["L0_ffn_packed_weight"])
    wscale = np.asarray(inputs["L0_ffn_scale"])
    fbias = np.asarray(inputs["L0_ffn_bias"])
    w_qkv1 = np.asarray(inputs["L1_attn_w_qkv"])
    b_qkv1 = np.asarray(inputs["L1_attn_b_qkv"])
    w_out1 = np.asarray(inputs["L1_attn_w_out"])
    b_out1 = np.asarray(inputs["L1_attn_b_out"])
    W1T = np.ascontiguousarray(np.asarray(inputs["L1_ffn_W"]).T)
    b1 = np.asarray(inputs["L1_ffn_b"])
    A1 = np.asarray(inputs["L1_ffn_A"])
    Bmat = np.asarray(inputs["L1_ffn_Bmat"])
    LiMEs = np.asarray(inputs["L1_ffn_LiMEs"])
    LiMEsT = np.ascontiguousarray(LiMEs.T)

    per_core = []
    for c in range(NCORE):
        b = c // 2
        h = c % 2
        heads = list(range(h * HPC, h * HPC + HPC))
        heads_l1 = [2 * j + h for j in range(HPC)]
        qk_cols = np.concatenate([np.arange(hh * 384, hh * 384 + 256) for hh in heads])
        v_cols = np.concatenate([np.arange(hh * 384 + 256, hh * 384 + 384) for hh in heads])
        d = {
            "x_my": np.ascontiguousarray(x[b, h * TOK:(h + 1) * TOK]),
            "wqk_t": np.ascontiguousarray(qkv_wT[:, qk_cols]),
            "wv_t": np.ascontiguousarray(qkv_wT[:, v_cols]),
            "wg_t": np.ascontiguousarray(gate_wT[:, h * 2048:(h + 1) * 2048]),
            "wo_t": out_wT,
            "norm_w": norm_w[None, :].astype(np.float32),
            "nw_bc": np.ascontiguousarray(np.repeat(
                norm_w[h * 2048:(h + 1) * 2048][None, :], 128, 0).astype(np.float32)),
            "pack_my": np.ascontiguousarray(packed[c * 512:(c + 1) * 512]),
            "wsc_my": np.ascontiguousarray(wscale[c * 512:(c + 1) * 512]),
            "fbias": fbias[None, :],
            "w1qk": np.ascontiguousarray(
                w_qkv1[:, np.concatenate([np.concatenate(
                    [np.arange(hh * 128, hh * 128 + 128),
                     np.arange(4096 + hh * 128, 4096 + hh * 128 + 128)]) for hh in heads_l1])]),
            "w1v": np.ascontiguousarray(
                w_qkv1[:, np.concatenate([np.arange(8192 + hh * 128, 8192 + hh * 128 + 128)
                                          for hh in heads_l1])]),
            "b1qk": None,
            "b1v2": None,
            "w1o": np.ascontiguousarray(w_out1[np.concatenate(
                [np.arange((2 * j + p) * 128, (2 * j + p) * 128 + 128)
                 for p in range(2) for j in range(HPC)])]),
            "b1o": b_out1[None, :],
            "W1T": W1T, "b1": b1[None, :], "A1": A1, "Bmat": Bmat,
            "LiMEs": LiMEs, "LiMEsT": LiMEsT,
        }
        qkb = np.concatenate([np.concatenate(
            [b_qkv1[hh * 128:hh * 128 + 128], b_qkv1[4096 + hh * 128:4096 + hh * 128 + 128]])
            for hh in heads_l1])
        vb = np.concatenate([b_qkv1[8192 + hh * 128:8192 + hh * 128 + 128] for hh in heads_l1])
        d["b1qk"] = np.ascontiguousarray(qkb.reshape(32, 128).T).astype(np.float32)
        d["b1v2"] = np.ascontiguousarray(vb.reshape(16, 128).T).astype(np.float32)
        d["identf"] = np.eye(128, dtype=np.float32)
        d.update(host_tables(c))
        per_core.append(d)
    return per_core



class Sem:

    def __init__(self, handle):
        self.h = handle
        self.n = 0

    def inc(self, inst, k=1):
        inst.then_inc(self.h, k)
        self.n += k
        return self.n


def mk_sems(st, nc, names):
    return {n: Sem(st.enter_context(nc.semaphore(n))) for n in names}


def dinc(eng, sem, k=1):
    sem.inc(eng.drain(), k)



def emit_ln(nc, src, dst, raw_dst=None, n_rows=TOK, tag="ln"):
    NT = n_rows // 128
    with ExitStack() as st:
        def sb(name, shape, dt):
            return st.enter_context(nc.sbuf_tensor(tag + name, shape, dt))
        xt = [sb(f"x{j}", [128, D], F32) for j in range(2)]
        n0t = [sb(f"n{j}", [128, D], BF16) for j in range(2)]
        bstat = sb("bst", [128, 48], F32)
        mvc = sb("mvc", [128, 2 * NT], F32)
        lnv = sb("lnv", [128, NT], F32)
        lnh = sb("lnh", [128, NT], F32)
        rstd = sb("rstd", [128, NT], F32)
        epsc = sb("eps", [128, 1], F32)
        sems = mk_sems(st, nc, [tag + k for k in ["ld", "cv", "ca", "so"]])
        ld, cv, ca, so = (sems[tag + k] for k in ["ld", "cv", "ca", "so"])
        blk = st.enter_context(nc.Block())

        @blk.sync
        def _(sy):
            for j in range(NT):
                if j >= 2:
                    sy.wait_ge(cv.h, j - 1)
                sy.dma_start(out=xt[j % 2][:], in_=src[j * 128:(j + 1) * 128, :]).then_inc(ld.h, 16)
            sy.wait_ge(ca.h, 2)
            for j in range(NT):
                if j >= 2:
                    sy.wait_ge(cv.h, NT + 2 + (j - 1))
                sy.dma_start(out=xt[j % 2][:], in_=src[j * 128:(j + 1) * 128, :]).then_inc(ld.h, 16)
            ld.n = 32 * NT

        @blk.vector
        def _(v):
            for j in range(NT):
                v.wait_ge(ld.h, 16 * (j + 1))
                for c8 in range(8):
                    v.bn_stats(out=bstat[:, c8 * 6:(c8 + 1) * 6],
                               in_=xt[j % 2][:, c8 * 512:(c8 + 1) * 512])
                v.drain()
                v.bn_aggr(out=mvc[:, 2 * j:2 * j + 2], in_=bstat[:])
                dinc(v, cv)
            v.wait_ge(ca.h, 1)
            v.tensor_scalar_mul(out=lnh[:], in0=lnv[:], scalar1=-0.5)
            dinc(v, cv)
            dinc(v, cv)
            v.wait_ge(ca.h, 2)
            for j in range(NT):
                v.wait_ge(ld.h, 16 * (NT + j + 1))
                if j >= 2:
                    v.wait_ge(so.h, 16 * (j - 1))
                v.tensor_scalar(out=n0t[j % 2][:], in0=xt[j % 2][:],
                                scalar1=mvc[:, 2 * j:2 * j + 1],
                                scalar2=rstd[:, j:j + 1],
                                op0=ALU.subtract, op1=ALU.mult)
                dinc(v, cv)

        @blk.scalar
        def _(s):
            s.wait_ge(cv.h, NT)
            s.activation(out=lnv[:], in_=mvc[:].rearrange("p (j k) -> p j k", k=2)[:, :, 1:2],
                         func=AF.Ln, bias=epsc[:])
            dinc(s, ca)
            s.wait_ge(cv.h, NT + 1)
            s.activation(out=rstd[:], in_=lnh[:], func=AF.Exp)
            dinc(s, ca)

        @blk.gpsimd
        def _(g):
            g.memset(epsc[:], EPS_LN)
            nr = 0
            if raw_dst is not None:
                q = n_rows // 4
                for j in range(4):
                    g.dma_start(out=raw_dst[j * q:(j + 1) * q, :],
                                in_=src[j * q:(j + 1) * q, :]).then_inc(so.h, 16)
                nr = 4
            for j in range(NT):
                g.wait_ge(cv.h, NT + 2 + j + 1)
                g.dma_start(out=dst[j * 128:(j + 1) * 128, :], in_=n0t[j % 2][:]).then_inc(so.h, 16)
            so.n = 16 * (NT + nr)

        @blk.sync
        def _(sy):
            sy.wait_ge(so.h, so.n)

    return nc


def emit_dequant(nc, io):
    pk = io["pack_my"]
    sc = io["wsc_my"]
    wq = io["wdeq_my"]
    with ExitStack() as st:
        def sb(name, shape, dt):
            return st.enter_context(nc.sbuf_tensor(name, shape, dt))
        pkt = sb("dqpk", [128, 2048], I32)
        sct = sb("dqsc", [128, 128], F32)
        shi = sb("dqshi", [128, 512], I32)
        wf_ = sb("dqwf", [128, 4096], F32)
        wb_ = sb("dqwb", [128, 4096], BF16)
        sems = mk_sems(st, nc, ["dqld", "dq", "dqs"])
        ld, dq, dqs = (sems[k] for k in ["dqld", "dq", "dqs"])
        blk = st.enter_context(nc.Block())

        @blk.sync
        def _(sy):
            sy.dma_start(out=pkt[:].rearrange("p (c n) -> p c n", c=4),
                         in_=pk.rearrange("(c p) n -> p c n", p=128)).then_inc(ld.h, 16)
            sy.dma_start(out=sct[:].rearrange("p (c n) -> p c n", c=4),
                         in_=sc.rearrange("(c p) n -> p c n", p=128)).then_inc(ld.h, 16)

        @blk.vector
        def _(v):
            v.wait_ge(ld.h, 32)
            for rc in range(4):
                pkv = pkt[:, rc * 512:(rc + 1) * 512]
                for e in range(8):
                    v.tensor_scalar(out=shi[:], in0=pkv, scalar1=4 * e, scalar2=28,
                                    op0=ALU.logical_shift_left, op1=ALU.arith_shift_right)
                    v.tensor_copy(out=wf_[:, e:4096:8], in_=shi[:])
                if rc >= 1:
                    v.wait_ge(dqs.h, 16 * rc)
                for gsc in range(32):
                    v.tensor_scalar(out=wb_[:, gsc * 128:(gsc + 1) * 128],
                                    in0=wf_[:, gsc * 128:(gsc + 1) * 128],
                                    scalar1=sct[:, rc * 32 + gsc:rc * 32 + gsc + 1],
                                    scalar2=None, op0=ALU.mult)
                dinc(v, dq)

        @blk.gpsimd
        def _(g):
            for rc in range(4):
                g.wait_ge(dq.h, rc + 1)
                g.dma_start(out=wq[rc * 128:(rc + 1) * 128, :], in_=wb_[:]).then_inc(dqs.h, 16)

        @blk.sync
        def _(sy):
            sy.wait_ge(dqs.h, 64)
    return nc


def emit_p0(nc, io):
    emit_ln(nc, io["x_my"], io["n0_my"], n_rows=TOK, tag="p0")
    emit_dequant(nc, io)
    return nc



def emit_p1(nc, io):
    n0 = io["n0_full"]
    wqk = io["wqk_t"]
    wv = io["wv_t"]
    wg = io["wg_t"]
    qk0 = io["qk0"]
    v0 = io["v0fm"]
    g0 = io["g0fm"]
    NF = 64

    for th in (0, 1):
        with ExitStack() as st:
            def sb(name, shape, dt):
                return st.enter_context(nc.sbuf_tensor(f"p1{th}{name}", shape, dt))
            rhsf = [sb(f"r{i}", [128, 2048], BF16) for i in range(32)]
            wbuf = [sb(f"w{j}", [128, 4096], BF16) for j in range(2)]
            evb = [sb(f"e{j}", [128, 512], BF16) for j in range(8)]
            ps = [st.enter_context(nc.psum_tensor(f"p1{th}ps{j}", [128, 512], F32))
                  for j in range(8)]
            sems = mk_sems(st, nc, [f"p1{th}{k}" for k in ["ldr", "wl", "wf", "ce", "se"]])
            ldr, wl, wf, ce, se = (sems[f"p1{th}{k}"] for k in ["ldr", "wl", "wf", "ce", "se"])
            blk = st.enter_context(nc.Block())
            tsl = slice(th * 2048, (th + 1) * 2048)

            @blk.sync
            def _(sy):
                for i in range(32):
                    sy.dma_start(out=rhsf[i][:], in_=n0[tsl, i * 128:(i + 1) * 128],
                                 transpose=True).then_inc(ldr.h, 16)

            @blk.gpsimd
            def _(g):
                for f in range(NF):
                    if f < 32:
                        src = wqk[:, f * 128:(f + 1) * 128]
                    elif f < 48:
                        src = wv[:, (f - 32) * 128:(f - 31) * 128]
                    else:
                        src = wg[:, (f - 48) * 128:(f - 47) * 128]
                    if f >= 2:
                        g.wait_ge(wf.h, f - 1)
                    s3 = src.rearrange("(i p) c -> p i c", p=128)
                    d3 = wbuf[f % 2][:].rearrange("p (i c) -> p i c", c=128)
                    for q in range(4):
                        g.dma_start(out=d3[:, q * 8:(q + 1) * 8, :],
                                    in_=s3[:, q * 8:(q + 1) * 8, :]).then_inc(wl.h, 16)

            @blk.tensor
            def _(t):
                t.wait_ge(ldr.h, 16 * 32)
                for f in range(NF):
                    t.wait_ge(wl.h, 64 * (f + 1))
                    if f >= 2:
                        t.wait_ge(ce.h, 4 * (f - 1))
                    for i in range(32):
                        lhsT = wbuf[f % 2][:, i * 128:(i + 1) * 128]
                        for tt in range(4):
                            ti = t.matmul(ps[(f % 2) * 4 + tt][:],
                                          lhsT, rhsf[i][:, tt * 512:(tt + 1) * 512],
                                          start=(i == 0), stop=(i == 31))
                    wf.inc(ti)

            @blk.scalar
            def _(s):
                for f in range(NF):
                    s.wait_ge(wf.h, f + 1)
                    if f >= 2:
                        s.wait_ge(se.h, 16 * 4 * (f - 1))
                    func = AF.Silu if f < 48 else AF.Sigmoid
                    for tt in range(4):
                        si = s.activation(out=evb[(f % 2) * 4 + tt][:],
                                          in_=ps[(f % 2) * 4 + tt][:], func=func)
                        ce.inc(si)
                    for tt in range(4):
                        if f < 32:
                            dst = qk0[f * 128:(f + 1) * 128,
                                      th * 2048 + tt * 512: th * 2048 + (tt + 1) * 512]
                        elif f < 48:
                            dst = v0[(f - 32) * 128:(f - 31) * 128,
                                     th * 2048 + tt * 512: th * 2048 + (tt + 1) * 512]
                        else:
                            dst = g0[(f - 48) * 128:(f - 47) * 128,
                                     th * 2048 + tt * 512: th * 2048 + (tt + 1) * 512]
                        s.dma_start(out=dst, in_=evb[(f % 2) * 4 + tt][:]).then_inc(se.h, 16)

            @blk.sync
            def _(sy):
                sy.wait_ge(se.h, 16 * 4 * NF)
    return nc


def build_p1():
    nc = bass.Bass("TRN2", target_bir_lowering=False, debug=False)
    io = {
        "n0_full": nc.dram_tensor("n0_full", [S, D], BF16, kind="ExternalInput").ap(),
        "wqk_t": nc.dram_tensor("wqk_t", [D, 4096], F32, kind="ExternalInput").ap(),
        "wv_t": nc.dram_tensor("wv_t", [D, 2048], F32, kind="ExternalInput").ap(),
        "wg_t": nc.dram_tensor("wg_t", [D, 2048], F32, kind="ExternalInput").ap(),
        "qk0": nc.dram_tensor("qk0", [4096, S], BF16, kind="ExternalOutput").ap(),
        "v0fm": nc.dram_tensor("v0fm", [2048, S], BF16, kind="ExternalOutput").ap(),
        "g0fm": nc.dram_tensor("g0fm", [2048, S], BF16, kind="ExternalOutput").ap(),
    }
    emit_p1(nc, io)
    return nc



def emit_p2(nc, io):
    qk0 = io["qk0"]
    v0 = io["v0fm"]
    attn0 = io["attn0"]
    qdT = io["qd_bc"]
    kdT = io["kd"]
    bdT = io["bd"]
    ddTT = io["ddT"]
    identb = io["ident"]

    with ExitStack() as st:
        def sb(name, shape, dt):
            return st.enter_context(nc.sbuf_tensor(f"p2{name}", shape, dt))
        qsl = [sb(f"q{j}", [128, S], BF16) for j in range(2)]
        ksl = [sb(f"k{j}", [128, S], BF16) for j in range(2)]
        vsl = [sb(f"v{j}", [128, S], BF16) for j in range(2)]
        qdb = [sb(f"qd{j}", [128, 256], F32) for j in range(2)]
        kdc = [sb(f"kd{j}", [128, 2], F32) for j in range(2)]
        bdc = [sb(f"bd{j}", [128, 1], F32) for j in range(2)]
        dds = [sb(f"dd{j}", [128, 512], F32) for j in range(2)]
        idb = sb("idb", [128, 128], BF16)
        qs_ = [sb(f"qs{j}", [128, 256], BF16) for j in range(2)]
        stb = [sb(f"st{j}", [128, 512], BF16) for j in range(2)]
        kkd = [sb(f"kk{j}", [128, 256], BF16) for j in range(2)]
        vtm = [sb(f"vt{j}", [128, 256], BF16) for j in range(2)]
        atb = [sb(f"at{j}", [128, 256], BF16) for j in range(2)]
        kvf = [sb(f"kvf{j}", [128, 128], F32) for j in range(2)]
        kvbf = sb("kvbf", [128, 128], BF16)
        sqw = sb("sqw", [128, 256], F32)
        ssall = sb("ssall", [128, 512], F32)
        ssout = sb("ssout", [128, 32], F32)
        ps_st = [st.enter_context(nc.psum_tensor(f"p2pst{j}", [128, 256], F32)) for j in range(2)]
        ps_a = [st.enter_context(nc.psum_tensor(f"p2a{j}", [128, 128], F32)) for j in range(2)]
        ps_c = st.enter_context(nc.psum_tensor("p2c", [128, 128], F32))
        ps_tk = st.enter_context(nc.psum_tensor("p2tk", [128, 256], BF16))
        ps_tv = st.enter_context(nc.psum_tensor("p2tv", [128, 256], BF16))
        sems = mk_sems(st, nc, [f"p2{k}" for k in ["ld", "tm", "dv", "ca", "se"]])
        ld, tm, dv, ca, se = (sems[f"p2{k}"] for k in ["ld", "tm", "dv", "ca", "se"])
        blk = st.enter_context(nc.Block())

        def ldslot(s_):
            return 16 * (1 + 7 * (s_ + 1))

        @blk.sync
        def _(sy):
            sy.dma_start(out=idb[:], in_=identb).then_inc(ld.h, 16)
            for s_ in range(16):
                sp = s_ % 2
                if s_ >= 2:
                    sy.wait_ge(tm.h, 5 * 16 * (s_ - 1))
                    sy.wait_ge(dv.h, 5 * 16 * (s_ - 1))
                r0 = s_ * 256
                sy.dma_start(out=qsl[sp][:], in_=qk0[r0:r0 + 128, :]).then_inc(ld.h, 16)
                sy.dma_start(out=ksl[sp][:], in_=qk0[r0 + 128:r0 + 256, :]).then_inc(ld.h, 16)
                sy.dma_start(out=vsl[sp][:], in_=v0[s_ * 128:(s_ + 1) * 128, :]).then_inc(ld.h, 16)
                sy.dma_start(out=qdb[sp][:], in_=qdT[s_]).then_inc(ld.h, 16)
                sy.dma_start(out=kdc[sp][:], in_=kdT[s_]).then_inc(ld.h, 16)
                sy.dma_start(out=bdc[sp][:], in_=bdT[s_]).then_inc(ld.h, 16)
                sy.dma_start(out=dds[sp][:].rearrange("p (c n) -> p c n", c=2),
                             in_=ddTT[s_].rearrange("(c p) n -> p c n", p=128)).then_inc(ld.h, 16)

        @blk.vector
        def _(v):
            for s_ in range(16):
                sp = s_ % 2
                if s_ >= 1:
                    v.wait_ge(ca.h, 3 * 16 * s_)
                v.memset(kvf[0][:], 0.0)
                v.memset(kvbf[:], 0.0)
                for n_ in range(16):
                    u = s_ * 16 + n_
                    p = u % 2
                    c0 = n_ * 256
                    v.wait_ge(ld.h, ldslot(s_))
                    v.tensor_tensor(out=qs_[p][:], in0=qsl[sp][:, c0:c0 + 256],
                                    in1=qdb[sp][:], op=ALU.mult)
                    dinc(v, dv)
                    v.wait_ge(tm.h, 5 * u + 3)
                    v.tensor_tensor(out=stb[p][:, 0:256], in0=ps_st[0][:],
                                    in1=dds[sp][:, 0:256], op=ALU.mult)
                    v.tensor_tensor(out=stb[p][:, 256:512], in0=ps_st[1][:],
                                    in1=dds[sp][:, 256:512], op=ALU.mult)
                    dinc(v, dv)
                    v.wait_ge(tm.h, 5 * u + 1)
                    v.tensor_scalar(out=kkd[p][:, 0:128], in0=ps_tk[:, 0:128],
                                    scalar1=kdc[sp][:, 0:1], scalar2=None, op0=ALU.mult)
                    v.tensor_scalar(out=kkd[p][:, 128:256], in0=ps_tk[:, 128:256],
                                    scalar1=kdc[sp][:, 1:2], scalar2=None, op0=ALU.mult)
                    dinc(v, dv)
                    v.wait_ge(tm.h, 5 * u + 5)
                    v.scalar_tensor_tensor(out=kvf[(n_ + 1) % 2][:], in0=kvf[n_ % 2][:],
                                           scalar=bdc[sp][:], in1=ps_c[:],
                                           op0=ALU.mult, op1=ALU.add)
                    dinc(v, dv)
                    v.wait_ge(ca.h, 3 * u + 3)
                    v.scalar_tensor_tensor(out=sqw[:], in0=atb[p][:], scalar=0.0,
                                           in1=atb[p][:], op0=ALU.add, op1=ALU.mult)
                    for nh in range(2):
                        tc = n_ * 2 + nh
                        v.reduce_sum(out=ssall[:, tc * 16 + s_:tc * 16 + s_ + 1],
                                     in_=sqw[:, nh * 128:(nh + 1) * 128], axis=AX.X)
                    dv.inc(v.engine_nop())

        @blk.tensor
        def _(t):
            t.wait_ge(ld.h, 16)
            for s_ in range(16):
                sp = s_ % 2
                for n_ in range(16):
                    u = s_ * 16 + n_
                    p = u % 2
                    c0 = n_ * 256
                    t.wait_ge(ld.h, ldslot(s_))
                    if u >= 1:
                        t.wait_ge(dv.h, 5 * (u - 1) + 3)
                    t.transpose(ps_tk[:, 0:128], ksl[sp][:, c0:c0 + 128], idb[:])
                    ti = t.transpose(ps_tk[:, 128:256], ksl[sp][:, c0 + 128:c0 + 256], idb[:])
                    tm.inc(ti)
                    if u >= 1:
                        t.wait_ge(ca.h, 3 * (u - 1) + 1)
                    t.transpose(ps_tv[:, 0:128], vsl[sp][:, c0:c0 + 128], idb[:])
                    ti = t.transpose(ps_tv[:, 128:256], vsl[sp][:, c0 + 128:c0 + 256], idb[:])
                    tm.inc(ti)
                    if u >= 1:
                        t.wait_ge(dv.h, 5 * (u - 1) + 2)
                    for mh in range(2):
                        ti = t.matmul(ps_st[mh][:], ksl[sp][:, c0 + mh * 128:c0 + (mh + 1) * 128],
                                      qsl[sp][:, c0:c0 + 256], start=True, stop=True)
                    tm.inc(ti)
                    t.wait_ge(dv.h, 5 * u + 2)
                    t.wait_ge(ca.h, 3 * u + 1)
                    if u >= 1:
                        t.wait_ge(ca.h, 3 * (u - 1) + 3)
                    for nh in range(2):
                        t.matmul(ps_a[nh][:], stb[p][:, nh * 128:nh * 128 + 128],
                                 vtm[p][:, 0:128], start=True, stop=False)
                        t.matmul(ps_a[nh][:], stb[p][:, 256 + nh * 128:256 + nh * 128 + 128],
                                 vtm[p][:, 128:256], start=False, stop=False)
                        ti = t.matmul(ps_a[nh][:], qs_[p][:, nh * 128:(nh + 1) * 128],
                                      kvbf[:], start=False, stop=True)
                    tm.inc(ti)
                    t.wait_ge(dv.h, max(5 * u + 3, 5 * (u - 1) + 4) if u >= 1 else 5 * u + 3)
                    t.matmul(ps_c[:], kkd[p][:, 0:128], vtm[p][:, 0:128],
                             start=True, stop=False)
                    ti = t.matmul(ps_c[:], kkd[p][:, 128:256], vtm[p][:, 128:256],
                                  start=False, stop=True)
                    tm.inc(ti)

        @blk.scalar
        def _(s):
            for s_ in range(16):
                for n_ in range(16):
                    u = s_ * 16 + n_
                    p = u % 2
                    s.wait_ge(tm.h, 5 * u + 2)
                    if u >= 2:
                        s.wait_ge(tm.h, 5 * (u - 2) + 5)
                    si = s.activation(out=vtm[p][:], in_=ps_tv[:], func=AF.Copy)
                    dinc(s, ca)
                    s.wait_ge(dv.h, 5 * u + 4)
                    s.activation(out=kvbf[:], in_=kvf[(n_ + 1) % 2][:], func=AF.Copy)
                    dinc(s, ca)
                    s.wait_ge(tm.h, 5 * u + 4)
                    if u >= 1:
                        s.wait_ge(se.h, 32 * u)
                    if u >= 2:
                        s.wait_ge(dv.h, 5 * (u - 2) + 5)
                    s.activation(out=atb[p][:, 0:128], in_=ps_a[0][:], func=AF.Copy)
                    s.activation(out=atb[p][:, 128:256], in_=ps_a[1][:], func=AF.Copy)
                    dinc(s, ca)
                    for nh in range(2):
                        s.dma_start(out=attn0[n_ * 256 + nh * 128:n_ * 256 + (nh + 1) * 128,
                                             s_ * 128:(s_ + 1) * 128],
                                    in_=atb[p][:, nh * 128:(nh + 1) * 128]).then_inc(se.h, 16)

        @blk.vector
        def _(v):
            v.reduce_sum(out=ssout[:].rearrange("p (a o) -> p a o", o=1),
                         in_=ssall[:].rearrange("p (a s) -> p a s", s=16), axis=AX.X)
            dinc(v, dv)

        @blk.gpsimd
        def _(g):
            g.wait_ge(dv.h, 5 * 256 + 1)
            g.dma_start(out=io["ss_my"], in_=ssout[:]).then_inc(se.h, 16)

        @blk.sync
        def _(sy):
            sy.wait_ge(se.h, 32 * 256 + 16)
    return nc


def build_p2():
    nc = bass.Bass("TRN2", target_bir_lowering=False, debug=False)
    io = {
        "qk0": nc.dram_tensor("qk0", [4096, S], BF16, kind="ExternalInput").ap(),
        "v0fm": nc.dram_tensor("v0fm", [2048, S], BF16, kind="ExternalInput").ap(),
        "qd_bc": nc.dram_tensor("qd_bc", [16, 128, 256], F32, kind="ExternalInput").ap(),
        "kd": nc.dram_tensor("kd", [16, 128, 2], F32, kind="ExternalInput").ap(),
        "bd": nc.dram_tensor("bd", [16, 128, 1], F32, kind="ExternalInput").ap(),
        "ddT": nc.dram_tensor("ddT", [16, 256, 256], F32, kind="ExternalInput").ap(),
        "ident": nc.dram_tensor("ident", [128, 128], BF16, kind="ExternalInput").ap(),
        "attn0": nc.dram_tensor("attn0", [S, 2048], BF16, kind="ExternalOutput").ap(),
        "ss_my": nc.dram_tensor("ss_my", [128, 32], F32, kind="ExternalOutput").ap(),
    }
    emit_p2(nc, io)
    return nc



def emit_p3(nc, io, ss_ag=False):
    attn0 = io["attn0"]
    g0 = io["g0fm"]
    nwb = io["nw_bc"]
    if ss_ag:
        ssm = io["ss_ag"][0:128, :]
        ssp = io["ss_ag"][128:256, :]
    else:
        ssm = io["ss_my"]
        ssp = io["ss_pt"]
    gn0 = io["gn0"]

    with ExitStack() as st:
        def sb(name, shape, dt):
            return st.enter_context(nc.sbuf_tensor(f"p3{name}", shape, dt))
        at_t = [sb(f"a{j}", [128, 2048], BF16) for j in range(2)]
        gt_t = [sb(f"g{j}", [128, 2048], BF16) for j in range(2)]
        t2 = sb("t2", [128, 2048], F32)
        gnt = [sb(f"o{j}", [128, 2048], BF16) for j in range(2)]
        nwt = sb("nw", [128, 2048], F32)
        ssa = sb("ssa", [128, 32], F32)
        ssb = sb("ssb", [128, 32], F32)
        w1 = sb("w1", [128, 32], F32)
        lnv = sb("lnv", [128, 32], F32)
        lnh = sb("lnh", [128, 32], F32)
        rstd = sb("rstd", [128, 32], F32)
        epsc = sb("eps", [128, 1], F32)
        sems = mk_sems(st, nc, [f"p3{k}" for k in ["ld", "cv", "ca", "so"]])
        ld, cv, ca, so = (sems[f"p3{k}"] for k in ["ld", "cv", "ca", "so"])
        blk = st.enter_context(nc.Block())

        @blk.sync
        def _(sy):
            sy.dma_start(out=nwt[:], in_=nwb).then_inc(ld.h, 16)
            sy.dma_start(out=ssa[:], in_=ssm).then_inc(ld.h, 16)
            sy.dma_start(out=ssb[:], in_=ssp).then_inc(ld.h, 16)
            for tc in range(32):
                if tc >= 2:
                    sy.wait_ge(cv.h, 4 + 2 * (tc - 2) + 2)
                sy.dma_start(out=at_t[tc % 2][:],
                             in_=attn0[tc * 128:(tc + 1) * 128, :]).then_inc(ld.h, 16)
                sy.dma_start(out=gt_t[tc % 2][:], in_=g0[:, tc * 128:(tc + 1) * 128],
                             transpose=True).then_inc(ld.h, 16)

        @blk.vector
        def _(v):
            v.memset(epsc[:], EPS_RMS)
            v.wait_ge(ld.h, 48)
            v.tensor_tensor(out=w1[:], in0=ssa[:], in1=ssb[:], op=ALU.add)
            dinc(v, cv)
            v.wait_ge(ca.h, 1)
            v.tensor_scalar_mul(out=lnh[:], in0=lnv[:], scalar1=-0.5)
            dinc(v, cv)
            dinc(v, cv)
            v.wait_ge(ca.h, 2)
            dinc(v, cv)
            for tc in range(32):
                v.wait_ge(ld.h, 48 + 32 * (tc + 1))
                v.scalar_tensor_tensor(out=t2[:], in0=at_t[tc % 2][:],
                                       scalar=rstd[:, tc:tc + 1], in1=nwt[:],
                                       op0=ALU.mult, op1=ALU.mult)
                if tc >= 2:
                    v.wait_ge(so.h, 16 * (tc - 1))
                v.tensor_tensor(out=gnt[tc % 2][:], in0=t2[:], in1=gt_t[tc % 2][:], op=ALU.mult)
                dinc(v, cv)
                dinc(v, cv)

        @blk.scalar
        def _(s):
            s.wait_ge(cv.h, 1)
            s.activation(out=lnv[:], in_=w1[:], func=AF.Ln, scale=1.0 / D, bias=epsc[:])
            dinc(s, ca)
            s.wait_ge(cv.h, 2)
            s.activation(out=rstd[:], in_=lnh[:], func=AF.Exp)
            dinc(s, ca)

        @blk.gpsimd
        def _(g):
            for tc in range(32):
                g.wait_ge(cv.h, 4 + 2 * tc + 2)
                g.dma_start(out=gn0[tc * 128:(tc + 1) * 128, :],
                            in_=gnt[tc % 2][:]).then_inc(so.h, 16)

        @blk.sync
        def _(sy):
            sy.wait_ge(so.h, 16 * 32)
    return nc



def emit_proj_tm(nc, tag, act, w_ap, w_is_bf16_fxi, out, res=None, bias=None,
                 act_mode="tm"):
    with ExitStack() as st:
        def sb(name, shape, dt):
            return st.enter_context(nc.sbuf_tensor(f"{tag}{name}", shape, dt))
        wsb = [sb(f"w{i}", [128, 2048], BF16) for i in range(32)]
        gnf = [[sb(f"a{q}_{i}", [128, 512], BF16) for i in range(32)] for q in range(2)]
        xt = sb("x0", [128, 1024], F32)
        xo = sb("o0", [128, 1024], F32)
        ones1 = sb("one", [1, 128], BF16)
        brow = sb("b", [1, 2048], BF16) if bias is not None else None
        ps = [st.enter_context(nc.psum_tensor(f"{tag}ps{j}", [128, 512], F32))
              for j in range(8)]
        sems = mk_sems(st, nc, [f"{tag}{k}" for k in ["lw", "la", "lx", "mm", "cv", "so", "pr"]])
        lw, la, lx, mm, cv, so, pr = (sems[f"{tag}{k}"]
                                      for k in ["lw", "la", "lx", "mm", "cv", "so", "pr"])
        blk = st.enter_context(nc.Block())

        @blk.vector
        def _(v):
            v.memset(ones1[:], 1.0)
            dinc(v, pr)
            g = 0
            for fh in range(2):
                for tq in range(4):
                    for tc in range(4):
                        g += 1
                        v.wait_ge(mm.h, g)
                        p = (g - 1) % 2
                        for half in range(2):
                            v.wait_ge(lx.h, 32 * g - 16 + 16 * half)
                            if g > 1 or half == 1:
                                v.wait_ge(so.h, 32 * (g - 1) + 16 * half)
                            for f2 in range(2):
                                fo = half * 2 + f2
                                v.tensor_tensor(out=xo[:, f2 * 512:(f2 + 1) * 512],
                                                in0=ps[p * 4 + fo][:],
                                                in1=xt[:, f2 * 512:(f2 + 1) * 512], op=ALU.add)
                            dinc(v, cv)

        @blk.gpsimd
        def _(g_):
            g = 0
            for fh in range(2):
                if fh == 1 and (bias is not None or not w_is_bf16_fxi):
                    g_.wait_ge(mm.h, 16)
                if bias is not None:
                    g_.dma_start(out=brow[:],
                                 in_=bias[:, fh * 2048:(fh + 1) * 2048]).then_inc(lw.h, 16)
                if not w_is_bf16_fxi:
                    for i in range(32):
                        g_.dma_start(out=wsb[i][:],
                                     in_=w_ap[i * 128:(i + 1) * 128, fh * 2048:(fh + 1) * 2048]
                                     ).then_inc(lw.h, 16)
                for tq in range(4):
                    for tc in range(4):
                        g += 1
                        for ho in range(2):
                            g_.wait_ge(cv.h, 2 * (g - 1) + ho + 1)
                            g_.dma_start(
                                out=out[tq * 512 + tc * 128:tq * 512 + (tc + 1) * 128,
                                        fh * 2048 + ho * 1024:fh * 2048 + (ho + 1) * 1024],
                                in_=xo[:]).then_inc(so.h, 16)

        @blk.sync
        def _(sy):
            g = 0
            for fh in range(2):
                if w_is_bf16_fxi:
                    if fh == 1:
                        sy.wait_ge(mm.h, 16)
                    for i in range(32):
                        sy.dma_start(out=wsb[i][:],
                                     in_=w_ap[fh * 2048:(fh + 1) * 2048, i * 128:(i + 1) * 128],
                                     transpose=True).then_inc(lw.h, 16)
                for tq in range(4):
                    gq = fh * 4 + tq
                    if gq >= 2:
                        sy.wait_ge(mm.h, 4 * (gq - 1))
                    for i in range(32):
                        if act_mode == "tm":
                            sy.dma_start(out=gnf[gq % 2][i][:],
                                         in_=act[tq * 512:(tq + 1) * 512, i * 128:(i + 1) * 128],
                                         transpose=True).then_inc(la.h, 16)
                        else:
                            sy.dma_start(out=gnf[gq % 2][i][:],
                                         in_=act[i * 128:(i + 1) * 128, tq * 512:(tq + 1) * 512]
                                         ).then_inc(la.h, 16)
                    for tc in range(4):
                        g += 1
                        for half in range(2):
                            if g > 1 or half == 1:
                                sy.wait_ge(cv.h, 2 * (g - 1) + half)
                            sy.dma_start(
                                out=xt[:],
                                in_=res[tq * 512 + tc * 128:tq * 512 + (tc + 1) * 128,
                                        fh * 2048 + half * 1024:fh * 2048 + (half + 1) * 1024]
                            ).then_inc(lx.h, 16)

        @blk.tensor
        def _(t):
            nb = 0 if bias is None else 1
            nwl = 32 + nb if not w_is_bf16_fxi else 32 + nb
            t.wait_ge(pr.h, 1)
            g = 0
            for fh in range(2):
                t.wait_ge(lw.h, 16 * nwl * (fh + 1))
                for tq in range(4):
                    gq = fh * 4 + tq
                    t.wait_ge(la.h, 16 * 32 * (gq + 1))
                    for tc in range(4):
                        g += 1
                        p = (g - 1) % 2
                        if g > 2:
                            t.wait_ge(cv.h, 2 * (g - 2))
                        for i in range(32):
                            for fo in range(4):
                                ti = t.matmul(ps[p * 4 + fo][:],
                                              gnf[gq % 2][i][:, tc * 128:(tc + 1) * 128],
                                              wsb[i][:, fo * 512:(fo + 1) * 512],
                                              start=(i == 0), stop=(i == 31 and bias is None))
                        if bias is not None:
                            for fo in range(4):
                                ti = t.matmul(ps[p * 4 + fo][:], ones1[:],
                                              brow[:, fo * 512:(fo + 1) * 512],
                                              start=False, stop=True)
                        mm.inc(ti)

        @blk.sync
        def _(sy):
            sy.wait_ge(so.h, 32 * 32)
    return nc



def build_p3():
    nc = bass.Bass("TRN2", target_bir_lowering=False, debug=False)
    io = {
        "attn0": nc.dram_tensor("attn0", [S, 2048], BF16, kind="ExternalInput").ap(),
        "g0fm": nc.dram_tensor("g0fm", [2048, S], BF16, kind="ExternalInput").ap(),
        "nw_bc": nc.dram_tensor("nw_bc", [128, 2048], F32, kind="ExternalInput").ap(),
        "ss_my": nc.dram_tensor("ss_my", [128, 32], F32, kind="ExternalInput").ap(),
        "ss_pt": nc.dram_tensor("ss_pt", [128, 32], F32, kind="ExternalInput").ap(),
        "gn0": nc.dram_tensor("gn0", [S, 2048], BF16, kind="ExternalOutput").ap(),
    }
    emit_p3(nc, io)
    return nc


def build_p4():
    nc = bass.Bass("TRN2", target_bir_lowering=False, debug=False)
    gn_my = nc.dram_tensor("gn_my", [TOK, D], BF16, kind="ExternalInput").ap()
    wo_t = nc.dram_tensor("wo_t", [D, D], F32, kind="ExternalInput").ap()
    x_my = nc.dram_tensor("x_my", [TOK, D], F32, kind="ExternalInput").ap()
    x1_my = nc.dram_tensor("x1_my", [TOK, D], F32, kind="ExternalOutput").ap()
    n1_my = nc.dram_tensor("n1_my", [TOK, D], BF16, kind="ExternalOutput").ap()
    emit_proj_tm(nc, "p4", gn_my, wo_t, False, x1_my, res=x_my)
    emit_ln(nc, x1_my, n1_my, n_rows=TOK, tag="p4ln")
    return nc



def build_p0():
    nc = bass.Bass("TRN2", target_bir_lowering=False, debug=False)
    io = {
        "x_my": nc.dram_tensor("x_my", [TOK, D], F32, kind="ExternalInput").ap(),
        "pack_my": nc.dram_tensor("pack_my", [512, 512], I32, kind="ExternalInput").ap(),
        "wsc_my": nc.dram_tensor("wsc_my", [512, 32], F32, kind="ExternalInput").ap(),
        "n0_my": nc.dram_tensor("n0_my", [TOK, D], BF16, kind="ExternalOutput").ap(),
        "wdeq_my": nc.dram_tensor("wdeq_my", [512, D], BF16, kind="ExternalOutput").ap(),
    }
    emit_p0(nc, io)
    return nc



def emit_p6(nc, io):
    n2 = io["n2_full"]
    wqk = io["w1qk"]
    wv = io["w1v"]
    bqk = io["b1qk"]
    bv = io["b1v2"]
    qk1 = io["qk1"]
    v1 = io["v1fm"]
    NF = 48

    for th in (0, 1):
        with ExitStack() as st:
            def sb(name, shape, dt):
                return st.enter_context(nc.sbuf_tensor(f"p6{th}{name}", shape, dt))
            rhsf = [sb(f"r{i}", [128, 2048], BF16) for i in range(32)]
            wbuf = [sb(f"w{j}", [128, 4096], BF16) for j in range(2)]
            evb = [sb(f"e{j}", [128, 512], BF16) for j in range(8)]
            bcol = sb("bc", [128, 48], F32)
            ps = [st.enter_context(nc.psum_tensor(f"p6{th}ps{j}", [128, 512], F32))
                  for j in range(8)]
            sems = mk_sems(st, nc, [f"p6{th}{k}" for k in ["ldr", "wl", "wf", "ce", "se"]])
            ldr, wl, wf, ce, se = (sems[f"p6{th}{k}"] for k in ["ldr", "wl", "wf", "ce", "se"])
            blk = st.enter_context(nc.Block())
            tsl = slice(th * 2048, (th + 1) * 2048)

            @blk.sync
            def _(sy):
                sy.dma_start(out=bcol[:, 0:32], in_=bqk).then_inc(ldr.h, 16)
                sy.dma_start(out=bcol[:, 32:48], in_=bv).then_inc(ldr.h, 16)
                for i in range(32):
                    sy.dma_start(out=rhsf[i][:], in_=n2[tsl, i * 128:(i + 1) * 128],
                                 transpose=True).then_inc(ldr.h, 16)

            @blk.gpsimd
            def _(g):
                for f in range(NF):
                    src_ = wqk[:, f * 128:(f + 1) * 128] if f < 32 else \
                        wv[:, (f - 32) * 128:(f - 31) * 128]
                    if f >= 2:
                        g.wait_ge(wf.h, f - 1)
                    s3 = src_.rearrange("(i p) c -> p i c", p=128)
                    d3 = wbuf[f % 2][:].rearrange("p (i c) -> p i c", c=128)
                    for q in range(4):
                        g.dma_start(out=d3[:, q * 8:(q + 1) * 8, :],
                                    in_=s3[:, q * 8:(q + 1) * 8, :]).then_inc(wl.h, 16)

            @blk.tensor
            def _(t):
                t.wait_ge(ldr.h, 16 * 34)
                for f in range(NF):
                    t.wait_ge(wl.h, 64 * (f + 1))
                    if f >= 2:
                        t.wait_ge(ce.h, 4 * (f - 1))
                    for i in range(32):
                        lhsT = wbuf[f % 2][:, i * 128:(i + 1) * 128]
                        for tt in range(4):
                            ti = t.matmul(ps[(f % 2) * 4 + tt][:],
                                          lhsT, rhsf[i][:, tt * 512:(tt + 1) * 512],
                                          start=(i == 0), stop=(i == 31))
                    wf.inc(ti)

            @blk.scalar
            def _(s):
                for f in range(NF):
                    s.wait_ge(wf.h, f + 1)
                    if f >= 2:
                        s.wait_ge(se.h, 16 * 4 * (f - 1))
                    for tt in range(4):
                        si = s.activation(out=evb[(f % 2) * 4 + tt][:],
                                          in_=ps[(f % 2) * 4 + tt][:], func=AF.Identity,
                                          bias=bcol[:, f:f + 1])
                        ce.inc(si)
                    for tt in range(4):
                        if f < 32:
                            dst = qk1[f * 128:(f + 1) * 128,
                                      th * 2048 + tt * 512: th * 2048 + (tt + 1) * 512]
                        else:
                            dst = v1[(f - 32) * 128:(f - 31) * 128,
                                     th * 2048 + tt * 512: th * 2048 + (tt + 1) * 512]
                        s.dma_start(out=dst, in_=evb[(f % 2) * 4 + tt][:]).then_inc(se.h, 16)

            @blk.sync
            def _(sy):
                sy.wait_ge(se.h, 16 * 4 * NF)
    return nc



def _p7_windows():
    out = []
    for j in range(16):
        slope = float(_l1_slopes()[2 * j + 1])
        W = int(math.ceil(90.0 / slope))
        kept = {}
        for qt in range(8):
            nkc = 4 * (qt + 1)
            kept[qt] = [kc for kc in range(nkc) if (kc + 1) * 128 - 1 >= qt * 512 - W]
        out.append(kept)
    return out


def _p7_schedule():
    SKEW = 4
    wins = _p7_windows()
    ld = tm = dv = ca = se = 0
    ms = []
    dv += 1
    ld += 16
    for s_ in range(16):
        m = {}
        ld += 16 * 36
        m["ld_slot"] = ld
        ca += 1; m["ca_ksq"] = ca
        m["tm_krow"] = []; m["ca_krow"] = []
        for j in range(8):
            tm += 1; m["tm_krow"].append(tm)
            ca += 1; m["ca_krow"].append(ca)
        ca += 1; m["ca_qsq"] = ca
        m["tm_qrow"] = []; m["ca_qrow"] = []
        for j in range(8):
            tm += 1; m["tm_qrow"].append(tm)
            ca += 1; m["ca_qrow"].append(ca)
        dv += 1; m["dv_km"] = dv
        ca += 1; m["ca_kms"] = ca
        dv += 1; m["dv_trow"] = dv
        ca += 1; m["ca_rq"] = ca
        m["tm_rq"] = {}; m["ca_rqb"] = {}; m["tm_sc"] = {}; m["dv_tmp"] = {}
        m["ca_exp"] = {}; m["tm_cd"] = {}; m["tm_den"] = {}; m["dv_rec"] = {}
        m["tm_rec"] = {}; m["ca_recb"] = {}; m["dv_ctxn"] = {}; m["se_ctx"] = {}
        m["dv_msk"] = {}
        kept_s = wins[s_]
        for qt in range(8):
            kcs = kept_s[qt]
            nk = len(kcs)
            tm += 1; m["tm_rq"][qt] = tm
            ca += 1; m["ca_rqb"][qt] = ca
            dv += 1; m["dv_msk"][qt] = dv
            m["tm_sc"][qt] = [None] * nk; m["dv_tmp"][qt] = []
            m["ca_exp"][qt] = []; m["tm_cd"][qt] = [None] * nk
            for j in range(nk + SKEW):
                if j < nk:
                    tm += 1; m["tm_sc"][qt][j] = tm
                jj = j - SKEW
                if jj >= 0:
                    tm += 1; m["tm_cd"][qt][jj] = tm
            for i2 in range(nk):
                dv += 1; m["dv_tmp"][qt].append(dv)
                ca += 1; m["ca_exp"][qt].append(ca)
            m["tm_den"][qt] = m["tm_cd"][qt][-1]
            dv += 1; m["dv_rec"][qt] = dv
            tm += 1; m["tm_rec"][qt] = tm
            ca += 1; m["ca_recb"][qt] = ca
            dv += 1; m["dv_ctxn"][qt] = dv
            se += 16; m["se_ctx"][qt] = se
        m["tm_end"] = tm
        m["ca_end"] = ca
        m["dv_end"] = dv
        ms.append(m)
    return ms


def emit_p7(nc, io):
    qk1 = io["qk1"]
    v1 = io["v1fm"]
    ctx1 = io["ctx1"]
    slkpT = io["sl_kpos"]
    slqpT = io["slqp"]
    dmaskT = io["dmask"]
    SCAL = float(HD) ** -0.5
    SKEW = 4
    ms = _p7_schedule()
    wins = _p7_windows()

    with ExitStack() as st:
        def sb(name, shape, dt):
            return st.enter_context(nc.sbuf_tensor(f"p7{name}", shape, dt))
        qf = sb("qf", [128, S], BF16)
        kf = sb("kf", [128, S], BF16)
        vts = sb("vts", [128, S], BF16)
        sqb = sb("sqb", [128, S], BF16)
        slkp = sb("slkp", [128, 32], F32)
        slqp = sb("slqp", [1, S], F32)
        dmsk = sb("dmsk", [128, 2048], F32)
        kkrow = sb("kkrow", [1, S], F32)
        qrt = sb("qrt", [1, S], F32)
        km = sb("km", [1, 1], F32)
        kms = sb("kms", [1, 1], F32)
        trow = sb("trow", [1, S], F32)
        rqrow = sb("rqrow", [1, S], F32)
        rqall = sb("rqall", [128, 2560], F32)
        stf = [sb(f"stf{j}", [128, 512], F32) for j in range(4)]
        eb = [sb(f"eb{j}", [128, 512], BF16) for j in range(4)]
        recrow = sb("recrow", [1, 512], F32)
        recb = sb("recb", [128, 512], F32)
        ctxn = [sb(f"ctxn{j}", [128, 512], BF16) for j in range(2)]
        onesb = sb("onesb", [128, 1], BF16)
        onesf = sb("onesf", [1, 128], F32)
        ps_s = [st.enter_context(nc.psum_tensor(f"p7s{j}", [128, 512], F32)) for j in range(4)]
        ps_ctx = st.enter_context(nc.psum_tensor("p7ctx", [128, 512], F32))
        ps_den = st.enter_context(nc.psum_tensor("p7den", [1, 512], F32))
        ps_r = st.enter_context(nc.psum_tensor("p7r", [128, 512], F32))
        sems = mk_sems(st, nc, [f"p7{k}" for k in ["ld", "tm", "dv", "ca", "se"]])
        ld, tm, dv, ca, se = (sems[f"p7{k}"] for k in ["ld", "tm", "dv", "ca", "se"])
        blk = st.enter_context(nc.Block())

        @blk.sync
        def _(sy):
            sy.dma_start(out=dmsk[:].rearrange("p (a n) -> p a n", a=4),
                         in_=dmaskT.rearrange("a p n -> p a n")).then_inc(ld.h, 16)
            for s_ in range(16):
                if s_ >= 1:
                    sy.wait_ge(tm.h, ms[s_ - 1]["tm_end"])
                    sy.wait_ge(ca.h, ms[s_ - 1]["ca_end"])
                    sy.wait_ge(dv.h, ms[s_ - 1]["dv_end"])
                r0 = s_ * 256
                sy.dma_start(out=qf[:], in_=qk1[r0:r0 + 128, :]).then_inc(ld.h, 16)
                sy.dma_start(out=kf[:], in_=qk1[r0 + 128:r0 + 256, :]).then_inc(ld.h, 16)
                for kc in range(32):
                    sy.dma_start(out=vts[:, kc * 128:(kc + 1) * 128],
                                 in_=v1[s_ * 128:(s_ + 1) * 128, kc * 128:(kc + 1) * 128],
                                 transpose=True).then_inc(ld.h, 16)
                sy.dma_start(out=slkp[:], in_=slkpT[s_]).then_inc(ld.h, 16)
                sy.dma_start(out=slqp[:], in_=slqpT[s_:s_ + 1, :]).then_inc(ld.h, 16)

        @blk.vector
        def _(v):
            v.memset(onesb[:], 1.0)
            v.memset(onesf[:], 1.0)
            dinc(v, dv)
            for s_ in range(16):
                m = ms[s_]
                v.wait_ge(ca.h, m["ca_krow"][7])
                v.reduce_max(out=km[:], in_=kkrow[:], axis=AX.X)
                dinc(v, dv)
                v.wait_ge(ca.h, m["ca_kms"])
                v.wait_ge(ca.h, m["ca_qrow"][7])
                v.scalar_tensor_tensor(out=trow[:], in0=qrt[:], scalar=kms[:],
                                       in1=slqp[:], op0=ALU.mult, op1=ALU.add)
                dinc(v, dv)
                kept_s = wins[s_]

                def prev_tile(qt, i2, back):
                    idx = sum(len(kept_s[q2]) for q2 in range(qt)) + i2 - back
                    if idx < 0:
                        return None
                    for q2 in range(8):
                        base2 = sum(len(kept_s[q3]) for q3 in range(q2))
                        if idx < base2 + len(kept_s[q2]):
                            return q2, idx - base2
                    return None

                def gti(qt, i2):
                    return sum(len(kept_s[q2]) for q2 in range(qt)) + i2

                for qt in range(8):
                    kcs = kept_s[qt]
                    v.wait_ge(ca.h, m["ca_rqb"][qt])
                    for a in range(4):
                        v.tensor_tensor(out=rqall[:, (a + 1) * 512:(a + 2) * 512],
                                        in0=rqall[:, 0:512],
                                        in1=dmsk[:, a * 512:(a + 1) * 512], op=ALU.add)
                    dinc(v, dv)
                    for i2, kc in enumerate(kcs):
                        v.wait_ge(tm.h, m["tm_sc"][qt][i2])
                        pv = prev_tile(qt, i2, 4)
                        if pv is not None:
                            v.wait_ge(ca.h, m["ca_exp"][pv[0]][pv[1]])
                        sel = 0 if kc < 4 * qt else 1 + (kc - 4 * qt)
                        p = gti(qt, i2) % 4
                        v.tensor_tensor(out=stf[p][:], in0=ps_s[p][:],
                                        in1=rqall[:, sel * 512:(sel + 1) * 512], op=ALU.add)
                        dinc(v, dv)
                    v.wait_ge(tm.h, m["tm_den"][qt])
                    v.reciprocal(out=recrow[:], in_=ps_den[0:1, :])
                    dinc(v, dv)
                    v.wait_ge(ca.h, m["ca_recb"][qt])
                    if qt >= 2:
                        v.wait_ge(se.h, m["se_ctx"][qt - 2])
                    v.tensor_tensor(out=ctxn[qt % 2][:], in0=ps_ctx[:], in1=recb[:],
                                    op=ALU.mult)
                    dinc(v, dv)

        @blk.scalar
        def _(s):
            for s_ in range(16):
                m = ms[s_]
                s.wait_ge(ld.h, m["ld_slot"])
                s.activation(out=sqb[:], in_=kf[:], func=AF.Square)
                dinc(s, ca)
                for j in range(8):
                    s.wait_ge(tm.h, m["tm_krow"][j])
                    s.activation(out=kkrow[:, j * 512:(j + 1) * 512], in_=ps_den[0:1, :],
                                 func=AF.Copy)
                    dinc(s, ca)
                s.activation(out=sqb[:], in_=qf[:], func=AF.Square)
                dinc(s, ca)
                for j in range(8):
                    s.wait_ge(tm.h, m["tm_qrow"][j])
                    s.activation(out=qrt[:, j * 512:(j + 1) * 512], in_=ps_den[0:1, :],
                                 func=AF.Sqrt)
                    dinc(s, ca)
                s.wait_ge(dv.h, m["dv_km"])
                s.activation(out=kms[:], in_=km[:], func=AF.Sqrt, scale=SCAL * SCAL)
                dinc(s, ca)
                s.wait_ge(dv.h, m["dv_trow"])
                s.activation(out=rqrow[:], in_=trow[:], func=AF.Copy, scale=-1.0 / SCAL)
                dinc(s, ca)
                kept_s = wins[s_]

                def prev_tile(qt, i2, back):
                    idx = sum(len(kept_s[q2]) for q2 in range(qt)) + i2 - back
                    if idx < 0:
                        return None
                    for q2 in range(8):
                        base2 = sum(len(kept_s[q3]) for q3 in range(q2))
                        if idx < base2 + len(kept_s[q2]):
                            return q2, idx - base2
                    return None

                def gti(qt, i2):
                    return sum(len(kept_s[q2]) for q2 in range(qt)) + i2

                for qt in range(8):
                    kcs = kept_s[qt]
                    s.wait_ge(tm.h, m["tm_rq"][qt])
                    if qt >= 1:
                        s.wait_ge(dv.h, m["dv_tmp"][qt - 1][-1])
                    s.activation(out=rqall[:, 0:512], in_=ps_r[:], func=AF.Copy)
                    dinc(s, ca)
                    for i2, kc in enumerate(kcs):
                        s.wait_ge(dv.h, m["dv_tmp"][qt][i2])
                        pv = prev_tile(qt, i2, 4)
                        if pv is not None:
                            cd = m["tm_cd"][pv[0]][pv[1]]
                            s.wait_ge(tm.h, cd)
                        p = gti(qt, i2) % 4
                        s.activation(out=eb[p][:], in_=stf[p][:], func=AF.Exp,
                                     scale=SCAL, bias=slkp[:, kc:kc + 1])
                        dinc(s, ca)
                    s.wait_ge(tm.h, m["tm_rec"][qt])
                    s.activation(out=recb[:], in_=ps_r[:], func=AF.Copy)
                    dinc(s, ca)

        @blk.tensor
        def _(t):
            t.wait_ge(dv.h, 1)
            for s_ in range(16):
                m = ms[s_]
                t.wait_ge(ld.h, m["ld_slot"])
                t.wait_ge(ca.h, m["ca_ksq"])
                for j in range(8):
                    if j >= 1:
                        t.wait_ge(ca.h, m["ca_krow"][j - 1])
                    ti = t.matmul(ps_den[:], onesb[:], sqb[:, j * 512:(j + 1) * 512],
                                  start=True, stop=True)
                    tm.inc(ti)
                t.wait_ge(ca.h, m["ca_qsq"])
                for j in range(8):
                    t.wait_ge(ca.h, m["ca_qrow"][j - 1] if j >= 1 else m["ca_krow"][7])
                    ti = t.matmul(ps_den[:], onesb[:], sqb[:, j * 512:(j + 1) * 512],
                                  start=True, stop=True)
                    tm.inc(ti)
                t.wait_ge(ca.h, m["ca_rq"])
                kept_s = wins[s_]

                def prev_tile(qt, i2, back):
                    idx = sum(len(kept_s[q2]) for q2 in range(qt)) + i2 - back
                    if idx < 0:
                        return None
                    for q2 in range(8):
                        base2 = sum(len(kept_s[q3]) for q3 in range(q2))
                        if idx < base2 + len(kept_s[q2]):
                            return q2, idx - base2
                    return None

                def gti(qt, i2):
                    return sum(len(kept_s[q2]) for q2 in range(qt)) + i2

                for qt in range(8):
                    kcs = kept_s[qt]
                    nk = len(kcs)
                    if qt >= 1:
                        t.wait_ge(ca.h, m["ca_recb"][qt - 1])
                    ti = t.matmul(ps_r[:], onesf[:], rqrow[:, qt * 512:(qt + 1) * 512],
                                  start=True, stop=True)
                    tm.inc(ti)
                    for j in range(nk + SKEW):
                        if j < nk:
                            pv = prev_tile(qt, j, 4)
                            if pv is not None:
                                t.wait_ge(dv.h, m["dv_tmp"][pv[0]][pv[1]])
                            p = gti(qt, j) % 4
                            kc = kcs[j]
                            ti = t.matmul(ps_s[p][:], kf[:, kc * 128:(kc + 1) * 128],
                                          qf[:, qt * 512:(qt + 1) * 512],
                                          start=True, stop=True)
                            tm.inc(ti)
                        jj = j - SKEW
                        if jj >= 0:
                            t.wait_ge(ca.h, m["ca_exp"][qt][jj])
                            if jj == 0 and qt >= 1:
                                t.wait_ge(dv.h, m["dv_ctxn"][qt - 1])
                                t.wait_ge(dv.h, m["dv_rec"][qt - 1])
                            p = gti(qt, jj) % 4
                            kc = kcs[jj]
                            t.matmul(ps_ctx[:], vts[:, kc * 128:(kc + 1) * 128], eb[p][:],
                                     start=(jj == 0), stop=(jj == nk - 1))
                            ti = t.matmul(ps_den[:], onesb[:], eb[p][:],
                                          start=(jj == 0), stop=(jj == nk - 1))
                            tm.inc(ti)
                    t.wait_ge(dv.h, m["dv_rec"][qt])
                    ti = t.matmul(ps_r[:], onesf[:], recrow[:], start=True, stop=True)
                    tm.inc(ti)

        @blk.gpsimd
        def _(g):
            for s_ in range(16):
                m = ms[s_]
                for qt in range(8):
                    g.wait_ge(dv.h, m["dv_ctxn"][qt])
                    g.dma_start(out=ctx1[s_ * 128:(s_ + 1) * 128, qt * 512:(qt + 1) * 512],
                                in_=ctxn[qt % 2][:]).then_inc(se.h, 16)

        @blk.sync
        def _(sy):
            sy.wait_ge(se.h, 16 * 8 * 16)
    return nc


def build_p7():
    nc = bass.Bass("TRN2", target_bir_lowering=False, debug=False)
    io = {
        "qk1": nc.dram_tensor("qk1", [4096, S], BF16, kind="ExternalInput").ap(),
        "v1fm": nc.dram_tensor("v1fm", [2048, S], BF16, kind="ExternalInput").ap(),
        "sl_kpos": nc.dram_tensor("sl_kpos", [16, 128, 32], F32, kind="ExternalInput").ap(),
        "slqp": nc.dram_tensor("slqp", [16, S], F32, kind="ExternalInput").ap(),
        "dmask": nc.dram_tensor("dmask", [4, 128, 512], F32, kind="ExternalInput").ap(),
        "ctx1": nc.dram_tensor("ctx1", [2048, S], BF16, kind="ExternalOutput").ap(),
    }
    emit_p7(nc, io)
    return nc



def emit_reshard(nc, tag, copies, off_ap):
    with ExitStack() as st:
        osb = st.enter_context(nc.sbuf_tensor(f"rs{tag}o", [1, 1], I32))
        sem = st.enter_context(nc.semaphore(f"rs{tag}s"))
        blk = st.enter_context(nc.Block())

        @blk.gpsimd
        def _(g):
            g.dma_start(out=osb[:], in_=off_ap).then_inc(sem, 16)
            g.wait_ge(sem, 16)
            rb = nc.gpsimd.alloc_register(f"rs{tag}r")
            r2 = nc.gpsimd.alloc_register(f"rs{tag}r2")
            g.reg_load(rb, osb[0:1, 0:1])
            n = 16
            for dst, srct, pat, soff in copies:
                g.reg_add(r2, rb, soff)
                g.dma_start(out=dst, in_=bass.AP(srct, r2, pat)).then_inc(sem, 16)
                n += 16
            g.wait_ge(sem, n)
    return nc


def emit_coll(nc, tag, kind, op, groups, in_ap, out_ap):
    with ExitStack() as st:
        sem = st.enter_context(nc.semaphore(f"cc{tag}"))
        blk = st.enter_context(nc.Block())

        @blk.gpsimd
        def _(g):
            g.collective_compute(kind, op, replica_groups=groups,
                                 ins=[in_ap.opt()], outs=[out_ap.opt()]).then_inc(sem)
            g.wait_ge(sem, 1)
    return nc


def emit_p9z(nc, io):
    n3 = io["n3_my"]
    W1T = io["W1T"]
    b1 = io["b1"]
    A1 = io["A1"]
    LiT = io["LiMEsT"]
    identf = io["identf"]
    z_out = io["z_my"]
    h8_out = io["h8_my"]
    ds_out = io["ds_my"]
    mx_out = io["mx_my"]

    with ExitStack() as st:
        def sb(name, shape, dt):
            return st.enter_context(nc.sbuf_tensor(f"p9z{name}", shape, dt))
        n3f = [sb(f"n{i}", [128, 2048], BF16) for i in range(32)]
        a1sb = sb("a1", [128, 1024], BF16)
        w8all = sb("w8", [128, 256], BF16)
        litb = sb("lit", [32, 8], BF16)
        b8row = sb("b8", [1, 8], BF16)
        ones1 = sb("one", [1, 128], BF16)
        idf = sb("idf", [128, 128], F32)
        zfm = sb("zfm", [32, 2048], F32)
        zbf = sb("zbf", [32, 2048], BF16)
        h8s = sb("h8s", [128, 64], F32)
        dss = sb("dss", [128, 64], F32)
        mhd = sb("mhd", [128, 2], F32)
        mx2 = sb("mx2", [2, 1], F32)
        psz = st.enter_context(nc.psum_tensor("p9zz", [32, 512], F32))
        ps8 = st.enter_context(nc.psum_tensor("p9z8", [128, 8], F32))
        pst = st.enter_context(nc.psum_tensor("p9zt", [2, 128], F32))
        sems = mk_sems(st, nc, [f"p9z{k}" for k in ["ld", "tm", "dv", "ca", "se"]])
        ld, tm, dv, ca, se = (sems[f"p9z{k}"] for k in ["ld", "tm", "dv", "ca", "se"])
        blk = st.enter_context(nc.Block())

        @blk.sync
        def _(sy):
            for i in range(32):
                sy.dma_start(out=n3f[i][:], in_=n3[:, i * 128:(i + 1) * 128],
                             transpose=True).then_inc(ld.h, 16)
            sy.dma_start(out=idf[:], in_=identf).then_inc(ld.h, 16)

        @blk.gpsimd
        def _(g):
            for i in range(32):
                g.dma_start(out=a1sb[:, i * 32:(i + 1) * 32],
                            in_=A1[i * 128:(i + 1) * 128, :]).then_inc(ld.h, 16)
                g.dma_start(out=w8all[:, i * 8:(i + 1) * 8],
                            in_=W1T[i * 128:(i + 1) * 128, 0:8]).then_inc(ld.h, 16)
            g.dma_start(out=litb[:], in_=LiT).then_inc(ld.h, 16)
            g.dma_start(out=b8row[:], in_=b1[:, 0:8]).then_inc(ld.h, 16)
            g.wait_ge(ca.h, 8)
            g.dma_start(out=z_out, in_=zfm[:]).then_inc(se.h, 16)
            g.wait_ge(ca.h, 16)
            g.dma_start(out=h8_out, in_=h8s[:]).then_inc(se.h, 16)
            g.wait_ge(ca.h, 24)
            g.dma_start(out=ds_out, in_=dss[:]).then_inc(se.h, 16)
            g.wait_ge(dv.h, 3)
            g.dma_start(out=mx_out, in_=mx2[:]).then_inc(se.h, 16)

        @blk.vector
        def _(v):
            v.memset(ones1[:], 1.0)
            dinc(v, dv)
            v.wait_ge(ca.h, 24)
            v.reduce_max(out=mhd[:, 0:1], in_=h8s[:], axis=AX.X, apply_absolute_value=True)
            v.reduce_max(out=mhd[:, 1:2], in_=dss[:], axis=AX.X, apply_absolute_value=True)
            dinc(v, dv)
            v.wait_ge(tm.h, 21)
            v.reduce_max(out=mx2[:], in_=pst[:], axis=AX.X)
            dinc(v, dv)

        @blk.tensor
        def _(t):
            t.wait_ge(ld.h, 16 * 99)
            t.wait_ge(dv.h, 1)
            for t4 in range(4):
                if t4 >= 1:
                    t.wait_ge(ca.h, 2 * t4)
                for i in range(32):
                    ti = t.matmul(psz[:], a1sb[:, i * 32:(i + 1) * 32],
                                  n3f[i][:, t4 * 512:(t4 + 1) * 512],
                                  start=(i == 0), stop=(i == 31))
                tm.inc(ti)
            for ac in range(8):
                if ac >= 1:
                    t.wait_ge(ca.h, 8 + ac)
                for i in range(32):
                    t.matmul(ps8[:], n3f[i][:, 1 + ac * 256:ac * 256 + 256:2],
                             w8all[:, i * 8:(i + 1) * 8], start=(i == 0), stop=False)
                ti = t.matmul(ps8[:], ones1[:], b8row[:], start=False, stop=True)
                tm.inc(ti)
            for ac in range(8):
                t.wait_ge(ca.h, 16 + ac)
                ti = t.matmul(ps8[:], zbf[:, 1 + ac * 256:ac * 256 + 256:2],
                              litb[:], start=True, stop=True)
                tm.inc(ti)
            t.wait_ge(dv.h, 2)
            ti = t.matmul(pst[:], mhd[:], idf[:], start=True, stop=True)
            tm.inc(ti)

        @blk.scalar
        def _(s):
            for t4 in range(4):
                s.wait_ge(tm.h, t4 + 1)
                s.activation(out=zfm[:, t4 * 512:(t4 + 1) * 512], in_=psz[:], func=AF.Copy)
                dinc(s, ca)
                s.activation(out=zbf[:, t4 * 512:(t4 + 1) * 512], in_=psz[:], func=AF.Copy)
                dinc(s, ca)
            for ac in range(8):
                s.wait_ge(tm.h, 5 + ac)
                s.activation(out=h8s[:, ac * 8:(ac + 1) * 8], in_=ps8[:], func=AF.Copy)
                dinc(s, ca)
            for ac in range(8):
                s.wait_ge(tm.h, 13 + ac)
                s.activation(out=dss[:, ac * 8:(ac + 1) * 8], in_=ps8[:], func=AF.Copy)
                dinc(s, ca)

        @blk.sync
        def _(sy):
            sy.wait_ge(se.h, 64)
    return nc


def emit_p9b(nc, io, mx_pre=False):
    mxall = io["mx_all"]
    h8 = io["h8_my"]
    dsi = io["ds_my"]
    z_in = io["z_my"]
    limes = io["LiMEs"]
    bmat = io["Bmat"]
    x3p = io["x3p_my"]
    identb = io["ident"]
    x4 = io["x4_my"]
    GSC = 0.5

    with ExitStack() as st:
        def sb(name, shape, dt):
            return st.enter_context(nc.sbuf_tensor(f"p9b{name}", shape, dt))
        mxs = sb("mx", [1, 16], F32)
        h8s = sb("h8", [128, 64], F32)
        dss = sb("ds", [128, 64], F32)
        zfm = sb("z", [32, 2048], F32)
        limb = sb("lim", [8, 32], BF16)
        bmb = sb("bm", [32, 4096], BF16)
        idb = sb("idb", [128, 128], BF16)
        onesf = sb("onef", [1, 128], F32)
        mx2 = sb("m2", [1, 2], F32)
        cp1 = sb("cp1", [1, 2], F32)
        mx2c = sb("m2c", [1, 2], F32)
        cp2 = sb("cp2", [1, 2], F32)
        rcp = sb("rcp", [1, 2], F32)
        sc2 = sb("sc2", [1, 2], F32)
        scb = sb("scb", [128, 2], F32)
        ds2 = sb("ds2", [128, 64], F32)
        lg = sb("lg", [128, 64], F32)
        elg = sb("elg", [128, 64], F32)
        sm8 = sb("sm8", [128, 8], F32)
        cp8 = sb("cp8", [128, 8], F32)
        rp8 = sb("rp8", [128, 8], F32)
        probs = sb("pr", [128, 64], F32)
        m1 = sb("m1", [128, 8], F32)
        eqb = sb("eq", [128, 64], F32)
        cnt = sb("cnt", [128, 8], F32)
        mskd = sb("mskd", [128, 64], F32)
        m2b = sb("m2b", [128, 8], F32)
        d12 = sb("d12", [128, 8], F32)
        cond = sb("cond", [128, 8], F32)
        t1b = sb("t1b", [128, 8], F32)
        thr = sb("thr", [128, 8], F32)
        pdf = sb("pd", [128, 64], F32)
        sgm = sb("sg", [128, 64], F32)
        wu = sb("wu", [128, 64], F32)
        ws = sb("ws", [128, 8], F32)
        wsp = sb("wsp", [128, 8], F32)
        cpw = sb("cpw", [128, 8], F32)
        rw = sb("rw", [128, 8], F32)
        wgt = sb("wgt", [128, 64], BF16)
        wfm = sb("wfm", [8, 1024], BF16)
        pm = sb("pm", [32, 1024], F32)
        pf = sb("pf", [32, 2048], F32)
        zpb = sb("zpb", [32, 2048], BF16)
        x3t = [sb(f"x{j}", [128, 4096], F32) for j in range(2)]
        x4t = [sb(f"y{j}", [128, 4096], F32) for j in range(2)]
        psc = st.enter_context(nc.psum_tensor("p9bc", [128, 2], F32))
        pwt = st.enter_context(nc.psum_tensor("p9bw", [8, 128], BF16))
        ppm = st.enter_context(nc.psum_tensor("p9bp", [32, 512], F32))
        psd = [st.enter_context(nc.psum_tensor(f"p9bd{j}", [128, 512], F32)) for j in range(4)]
        sems = mk_sems(st, nc, [f"p9b{k}" for k in ["ld", "tm", "dv", "ca", "se", "lx"]])
        ld, tm, dv, ca, se, lx = (sems[f"p9b{k}"]
                                  for k in ["ld", "tm", "dv", "ca", "se", "lx"])
        blk = st.enter_context(nc.Block())

        def b3(ap):
            return ap.rearrange("p (g o) -> p g o", o=1).broadcast_to([128, 8, 8])

        def g3(ap):
            return ap.rearrange("p (g e) -> p g e", e=8)

        @blk.sync
        def _(sy):
            sy.dma_start(out=mxs[:, 0:2] if mx_pre else mxs[:], in_=mxall).then_inc(ld.h, 16)
            sy.dma_start(out=h8s[:], in_=h8).then_inc(ld.h, 16)
            sy.dma_start(out=dss[:], in_=dsi).then_inc(ld.h, 16)
            sy.dma_start(out=zfm[:], in_=z_in).then_inc(ld.h, 16)
            sy.dma_start(out=idb[:], in_=identb).then_inc(ld.h, 16)
            for tc in range(16):
                if tc >= 2:
                    sy.wait_ge(dv.h, 17 + 2 * (tc - 2) + 2)
                sy.dma_start(out=x3t[tc % 2][:],
                             in_=x3p[tc * 128:(tc + 1) * 128, :]).then_inc(lx.h, 16)

        @blk.gpsimd
        def _(g):
            g.dma_start(out=limb[:], in_=limes).then_inc(ld.h, 16)
            g.dma_start(out=bmb[:], in_=bmat).then_inc(ld.h, 16)
            for tc in range(16):
                g.wait_ge(dv.h, 17 + 2 * tc + 2)
                g.dma_start(out=x4[tc * 128:(tc + 1) * 128, :],
                            in_=x4t[tc % 2][:]).then_inc(se.h, 16)

        @blk.vector
        def _(v):
            v.memset(onesf[:], 1.0)
            v.wait_ge(ld.h, 16 * 7)
            if mx_pre:
                v.tensor_copy(out=mx2[:], in_=mxs[:, 0:2])
            else:
                v.reduce_max(out=mx2[:].rearrange("o (g e) -> o g e", e=1),
                             in_=mxs[:].rearrange("o (g e) -> o g e", e=8), axis=AX.X)
            dinc(v, dv)
            v.wait_ge(ca.h, 1)
            v.tensor_scalar_max(out=mx2c[:], in0=cp1[:], scalar1=1e-6)
            dinc(v, dv)
            v.wait_ge(ca.h, 2)
            v.reciprocal(out=rcp[:], in_=cp2[:])
            dinc(v, dv)
            v.wait_ge(ca.h, 4)
            v.tensor_scalar(out=ds2[:], in0=dss[:], scalar1=scb[:, 1:2], scalar2=None,
                            op0=ALU.mult)
            v.drain()
            v.scalar_tensor_tensor(out=lg[:], in0=h8s[:], scalar=scb[:, 0:1], in1=ds2[:],
                                   op0=ALU.mult, op1=ALU.add)
            dinc(v, dv)
            v.wait_ge(ca.h, 5)
            v.reduce_sum(out=sm8[:].rearrange("p (g o) -> p g o", o=1), in_=g3(elg[:]),
                         axis=AX.X)
            dinc(v, dv)
            v.wait_ge(ca.h, 6)
            v.reciprocal(out=rp8[:], in_=cp8[:])
            v.drain()
            v.tensor_tensor(out=g3(probs[:]), in0=g3(elg[:]), in1=b3(rp8[:]), op=ALU.mult)
            v.drain()
            v.reduce_max(out=m1[:].rearrange("p (g o) -> p g o", o=1), in_=g3(probs[:]),
                         axis=AX.X)
            v.drain()
            v.tensor_tensor(out=g3(eqb[:]), in0=g3(probs[:]), in1=b3(m1[:]), op=ALU.is_ge)
            v.drain()
            v.reduce_sum(out=cnt[:].rearrange("p (g o) -> p g o", o=1), in_=g3(eqb[:]),
                         axis=AX.X)
            v.drain()
            v.scalar_tensor_tensor(out=mskd[:], in0=eqb[:], scalar=-10.0, in1=probs[:],
                                   op0=ALU.mult, op1=ALU.add)
            v.drain()
            v.reduce_max(out=m2b[:].rearrange("p (g o) -> p g o", o=1), in_=g3(mskd[:]),
                         axis=AX.X)
            v.drain()
            v.tensor_sub(out=d12[:], in0=m1[:], in1=m2b[:])
            v.drain()
            v.tensor_scalar(out=cond[:], in0=cnt[:], scalar1=1.5, scalar2=None, op0=ALU.is_gt)
            v.drain()
            v.tensor_mul(out=t1b[:], in0=d12[:], in1=cond[:])
            v.drain()
            v.tensor_add(out=thr[:], in0=t1b[:], in1=m2b[:])
            v.drain()
            v.tensor_tensor(out=g3(pdf[:]), in0=g3(probs[:]), in1=b3(thr[:]), op=ALU.subtract)
            dinc(v, dv)
            v.wait_ge(ca.h, 7)
            v.tensor_mul(out=wu[:], in0=probs[:], in1=sgm[:])
            v.drain()
            v.reduce_sum(out=ws[:].rearrange("p (g o) -> p g o", o=1), in_=g3(wu[:]),
                         axis=AX.X)
            v.drain()
            v.tensor_scalar_add(out=wsp[:], in0=ws[:], scalar1=1e-9)
            dinc(v, dv)
            v.wait_ge(ca.h, 8)
            v.reciprocal(out=rw[:], in_=cpw[:])
            v.drain()
            v.tensor_tensor(out=g3(wgt[:]), in0=g3(wu[:]), in1=b3(rw[:]), op=ALU.mult)
            dinc(v, dv)
            for ac in range(8):
                v.wait_ge(tm.h, 2 + ac)
                v.tensor_copy(out=wfm[:, ac * 128:(ac + 1) * 128], in_=pwt[:])
                dinc(v, dv)
            v.wait_ge(ca.h, 10)
            for par in range(2):
                v.tensor_copy(out=pf[:, par:2048:2], in_=pm[:])
            v.drain()
            v.scalar_tensor_tensor(out=zpb[:], in0=zfm[:], scalar=GSC, in1=pf[:],
                                   op0=ALU.mult, op1=ALU.mult)
            dinc(v, dv)
            for tc in range(16):
                v.wait_ge(lx.h, 16 * (tc + 1))
                if tc >= 2:
                    v.wait_ge(se.h, 16 * (tc - 1))
                for g2 in range(2):
                    v.wait_ge(tm.h, 11 + 2 * tc + g2 + 1)
                    for fo in range(4):
                        f = g2 * 4 + fo
                        v.tensor_tensor(out=x4t[tc % 2][:, f * 512:(f + 1) * 512],
                                        in0=psd[fo][:],
                                        in1=x3t[tc % 2][:, f * 512:(f + 1) * 512],
                                        op=ALU.add)
                    dinc(v, dv)

        @blk.scalar
        def _(s):
            s.wait_ge(dv.h, 1)
            s.activation(out=cp1[:], in_=mx2[:], func=AF.Copy)
            dinc(s, ca)
            s.wait_ge(dv.h, 2)
            s.activation(out=cp2[:], in_=mx2c[:], func=AF.Copy)
            dinc(s, ca)
            s.wait_ge(dv.h, 3)
            s.activation(out=sc2[:], in_=rcp[:], func=AF.Copy, scale=GSC)
            dinc(s, ca)
            s.wait_ge(tm.h, 1)
            s.activation(out=scb[:], in_=psc[:], func=AF.Copy)
            dinc(s, ca)
            s.wait_ge(dv.h, 4)
            s.activation(out=elg[:], in_=lg[:], func=AF.Exp)
            dinc(s, ca)
            s.wait_ge(dv.h, 5)
            s.activation(out=cp8[:], in_=sm8[:], func=AF.Copy)
            dinc(s, ca)
            s.wait_ge(dv.h, 6)
            s.activation(out=sgm[:], in_=pdf[:], func=AF.Sigmoid, scale=2.0)
            dinc(s, ca)
            s.wait_ge(dv.h, 7)
            s.activation(out=cpw[:], in_=wsp[:], func=AF.Copy)
            dinc(s, ca)
            for j in range(2):
                s.wait_ge(tm.h, 10 + j)
                s.activation(out=pm[:, j * 512:(j + 1) * 512], in_=ppm[:], func=AF.Copy)
                dinc(s, ca)

        @blk.tensor
        def _(t):
            t.wait_ge(ld.h, 16 * 7)
            t.wait_ge(ca.h, 3)
            ti = t.matmul(psc[:], onesf[:], sc2[:], start=True, stop=True)
            tm.inc(ti)
            t.wait_ge(dv.h, 8)
            for ac in range(8):
                if ac >= 1:
                    t.wait_ge(dv.h, 8 + ac)
                ti = t.transpose(pwt[:], wgt[:, ac * 8:(ac + 1) * 8], idb[:])
                tm.inc(ti)
            t.wait_ge(dv.h, 16)
            for j in range(2):
                if j >= 1:
                    t.wait_ge(ca.h, 9)
                ti = t.matmul(ppm[:], limb[:], wfm[:, j * 512:(j + 1) * 512],
                              start=True, stop=True)
                tm.inc(ti)
            t.wait_ge(dv.h, 17)
            for tc in range(16):
                for g2 in range(2):
                    if not (tc == 0 and g2 == 0):
                        t.wait_ge(dv.h, 17 + 2 * tc + g2)
                    for fo in range(4):
                        f = g2 * 4 + fo
                        ti = t.matmul(psd[fo][:], zpb[:, tc * 128:(tc + 1) * 128],
                                      bmb[:, f * 512:(f + 1) * 512], start=True, stop=True)
                    tm.inc(ti)

        @blk.sync
        def _(sy):
            sy.wait_ge(se.h, 16 * 16)
    return nc


def build_p8():
    nc = bass.Bass("TRN2", target_bir_lowering=False, debug=False)
    ctx_my = nc.dram_tensor("ctx_my", [TOK, D], BF16, kind="ExternalInput").ap()
    w1o = nc.dram_tensor("w1o", [D, D], F32, kind="ExternalInput").ap()
    b1o = nc.dram_tensor("b1o", [1, D], F32, kind="ExternalInput").ap()
    x2_my = nc.dram_tensor("x2_my", [TOK, D], F32, kind="ExternalInput").ap()
    x3_my = nc.dram_tensor("x3_my", [TOK, D], F32, kind="ExternalOutput").ap()
    n3_my = nc.dram_tensor("n3_my", [TOK, D], BF16, kind="ExternalOutput").ap()
    emit_proj_tm(nc, "p8", ctx_my, w1o, False, x3_my, res=x2_my, bias=b1o)
    emit_ln(nc, x3_my, n3_my, n_rows=TOK, tag="p8ln")
    return nc


def build_p9a():
    nc = bass.Bass("TRN2", target_bir_lowering=False, debug=False)
    io = {
        "n3_my": nc.dram_tensor("n3_my", [TOK, D], BF16, kind="ExternalInput").ap(),
        "W1T": nc.dram_tensor("W1T", [D, D], F32, kind="ExternalInput").ap(),
        "b1": nc.dram_tensor("b1", [1, D], F32, kind="ExternalInput").ap(),
        "A1": nc.dram_tensor("A1", [D, 32], F32, kind="ExternalInput").ap(),
        "LiMEsT": nc.dram_tensor("LiMEsT", [32, 8], F32, kind="ExternalInput").ap(),
        "identf": nc.dram_tensor("identf", [128, 128], F32, kind="ExternalInput").ap(),
        "x3_my": nc.dram_tensor("x3_my", [TOK, D], F32, kind="ExternalInput").ap(),
        "x3p_my": nc.dram_tensor("x3p_my", [TOK, D], F32, kind="ExternalOutput").ap(),
        "z_my": nc.dram_tensor("z_my", [32, TOK], F32, kind="ExternalOutput").ap(),
        "h8_my": nc.dram_tensor("h8_my", [128, 64], F32, kind="ExternalOutput").ap(),
        "ds_my": nc.dram_tensor("ds_my", [128, 64], F32, kind="ExternalOutput").ap(),
        "mx_my": nc.dram_tensor("mx_my", [2, 1], F32, kind="ExternalOutput").ap(),
    }
    emit_proj_tm(nc, "p9h", io["n3_my"], io["W1T"], False, io["x3p_my"],
                 res=io["x3_my"], bias=io["b1"])
    emit_p9z(nc, io)
    return nc


def build_p9b():
    nc = bass.Bass("TRN2", target_bir_lowering=False, debug=False)
    io = {
        "mx_all": nc.dram_tensor("mx_all", [1, 16], F32, kind="ExternalInput").ap(),
        "h8_my": nc.dram_tensor("h8_my", [128, 64], F32, kind="ExternalInput").ap(),
        "ds_my": nc.dram_tensor("ds_my", [128, 64], F32, kind="ExternalInput").ap(),
        "z_my": nc.dram_tensor("z_my", [32, TOK], F32, kind="ExternalInput").ap(),
        "LiMEs": nc.dram_tensor("LiMEs", [8, 32], F32, kind="ExternalInput").ap(),
        "Bmat": nc.dram_tensor("Bmat", [32, D], F32, kind="ExternalInput").ap(),
        "x3p_my": nc.dram_tensor("x3p_my", [TOK, D], F32, kind="ExternalInput").ap(),
        "ident": nc.dram_tensor("ident", [128, 128], BF16, kind="ExternalInput").ap(),
        "x4_my": nc.dram_tensor("x4_my", [TOK, D], F32, kind="ExternalOutput").ap(),
    }
    emit_p9b(nc, io)
    return nc


def build_p6():
    nc = bass.Bass("TRN2", target_bir_lowering=False, debug=False)
    io = {
        "n2_full": nc.dram_tensor("n2_full", [S, D], BF16, kind="ExternalInput").ap(),
        "w1qk": nc.dram_tensor("w1qk", [D, 4096], F32, kind="ExternalInput").ap(),
        "w1v": nc.dram_tensor("w1v", [D, 2048], F32, kind="ExternalInput").ap(),
        "b1qk": nc.dram_tensor("b1qk", [128, 32], F32, kind="ExternalInput").ap(),
        "b1v2": nc.dram_tensor("b1v2", [128, 16], F32, kind="ExternalInput").ap(),
        "qk1": nc.dram_tensor("qk1", [4096, S], BF16, kind="ExternalOutput").ap(),
        "v1fm": nc.dram_tensor("v1fm", [2048, S], BF16, kind="ExternalOutput").ap(),
    }
    emit_p6(nc, io)
    return nc


def build_p5():
    nc = bass.Bass("TRN2", target_bir_lowering=False, debug=False)
    n1_my = nc.dram_tensor("n1_my", [TOK, D], BF16, kind="ExternalInput").ap()
    wdeq = nc.dram_tensor("wdeq", [D, D], BF16, kind="ExternalInput").ap()
    fbias = nc.dram_tensor("fbias", [1, D], F32, kind="ExternalInput").ap()
    x1_my = nc.dram_tensor("x1_my", [TOK, D], F32, kind="ExternalInput").ap()
    x2_my = nc.dram_tensor("x2_my", [TOK, D], F32, kind="ExternalOutput").ap()
    n2_my = nc.dram_tensor("n2_my", [TOK, D], BF16, kind="ExternalOutput").ap()
    emit_proj_tm(nc, "p5", n1_my, wdeq, True, x2_my, res=x1_my, bias=fbias)
    emit_ln(nc, x2_my, n2_my, n_rows=TOK, tag="p5ln")
    return nc



def build_fused():
    nc = bass.Bass("TRN2", target_bir_lowering=False, debug=False)

    def inp(name, shape, dt):
        return nc.dram_tensor(name, shape, dt, kind="ExternalInput").ap()

    def mid(name, shape, dt):
        return nc.dram_tensor(name, shape, dt, kind="Internal").ap()

    io = {}
    io["x_full"] = inp("x_full", [S, D], F32)
    io["x_my"] = inp("x_my", [TOK, D], F32)
    io["pack_my"] = inp("pack_my", [512, 512], I32)
    io["wsc_my"] = inp("wsc_my", [512, 32], F32)
    io["wqk_t"] = inp("wqk_t", [D, 4096], F32)
    io["wv_t"] = inp("wv_t", [D, 2048], F32)
    io["wg_t"] = inp("wg_t", [D, 2048], F32)
    io["nw_bc"] = inp("nw_bc", [128, 2048], F32)
    io["wo_t"] = inp("wo_t", [D, D], F32)
    io["fbias"] = inp("fbias", [1, D], F32)
    io["w1qk"] = inp("w1qk", [D, 4096], F32)
    io["w1v"] = inp("w1v", [D, 2048], F32)
    io["b1qk"] = inp("b1qk", [128, 32], F32)
    io["b1v2"] = inp("b1v2", [128, 16], F32)
    io["sl_kpos"] = inp("sl_kpos", [16, 128, 32], F32)
    io["slqp"] = inp("slqp", [16, S], F32)
    io["dmask"] = inp("dmask", [4, 128, 512], F32)
    io["w1o"] = inp("w1o", [D, D], F32)
    io["b1o"] = inp("b1o", [1, D], F32)
    io["W1T"] = inp("W1T", [D, D], F32)
    io["b1"] = inp("b1", [1, D], F32)
    io["A1"] = inp("A1", [D, 32], F32)
    io["LiMEsT"] = inp("LiMEsT", [32, 8], F32)
    io["LiMEs"] = inp("LiMEs", [8, 32], F32)
    io["Bmat"] = inp("Bmat", [32, D], F32)
    io["identf"] = inp("identf", [128, 128], F32)
    io["ident"] = inp("ident", [128, 128], BF16)
    io["qd_bc"] = inp("qd_bc", [16, 128, 256], F32)
    io["kd"] = inp("kd", [16, 128, 2], F32)
    io["bd"] = inp("bd", [16, 128, 1], F32)
    io["ddT"] = inp("ddT", [16, 256, 256], F32)
    io["off_p4"] = inp("off_p4", [1, 1], I32)
    io["off_p8"] = inp("off_p8", [1, 1], I32)

    io["n0_full"] = mid("n0_full", [S, D], BF16)
    io["wdeq_my"] = mid("wdeq_my", [512, D], BF16)
    wdeq_ag = mid("wdeq_ag", [D, D], BF16)
    io["qk0"] = mid("qk0", [4096, S], BF16)
    io["v0fm"] = mid("v0fm", [2048, S], BF16)
    io["g0fm"] = mid("g0fm", [2048, S], BF16)
    io["attn0"] = mid("attn0", [S, 2048], BF16)
    io["ss_my"] = mid("ss_my", [128, 32], F32)
    io["ss_ag"] = mid("ss_ag", [256, 32], F32)
    io["gn0"] = mid("gn0", [S, 2048], BF16)
    gn_ag = mid("gn_ag", [2 * S, 2048], BF16)
    x1 = mid("x1", [TOK, D], F32)
    n1 = mid("n1", [TOK, D], BF16)
    x2 = mid("x2", [TOK, D], F32)
    n2 = mid("n2", [TOK, D], BF16)
    io["n2_full"] = mid("n2_ag", [S, D], BF16)
    io["qk1"] = mid("qk1", [4096, S], BF16)
    io["v1fm"] = mid("v1fm", [2048, S], BF16)
    io["ctx1"] = mid("ctx1", [2048, S], BF16)
    ctx_ag = mid("ctx_ag", [D, S], BF16)
    x3 = mid("x3", [TOK, D], F32)
    io["n3_my"] = mid("n3", [TOK, D], BF16)
    io["x3p_my"] = mid("x3p", [TOK, D], F32)
    io["z_my"] = mid("z_my", [32, TOK], F32)
    io["h8_my"] = mid("h8_my", [128, 64], F32)
    io["ds_my"] = mid("ds_my", [128, 64], F32)
    io["mx_my"] = mid("mx_my", [2, 1], F32)
    mx_g = mid("mx_g", [2, 1], F32)
    io["x3_my"] = x3
    io["x4_my"] = nc.dram_tensor("x4_my", [TOK, D], F32, kind="ExternalOutput").ap()

    pairs = [[0, 1], [2, 3], [4, 5], [6, 7]]
    all8 = [[0, 1, 2, 3, 4, 5, 6, 7]]

    emit_ln(nc, io["x_full"], io["n0_full"], n_rows=S, tag="p0")
    emit_dequant(nc, io)
    emit_p1(nc, io)
    emit_p2(nc, io)
    emit_coll(nc, "ss", "AllGather", ALU.bypass, pairs, io["ss_my"], io["ss_ag"])
    emit_p3(nc, io, ss_ag=True)
    emit_coll(nc, "gn", "AllGather", ALU.bypass, pairs, io["gn0"], gn_ag)
    gn_loc = mid("gn_loc", [TOK, D], BF16)
    emit_reshard(nc, "gn", [
        (gn_loc[:, 0:2048], gn_ag.tensor, [[2048, 2048], [1, 2048]], 0),
        (gn_loc[:, 2048:4096], gn_ag.tensor, [[2048, 2048], [1, 2048]], S * 2048),
    ], io["off_p4"])
    emit_proj_tm(nc, "p4", gn_loc, io["wo_t"], False, x1, res=io["x_my"])
    emit_ln(nc, x1, n1, n_rows=TOK, tag="p4l")
    emit_coll(nc, "wq", "AllGather", ALU.bypass, all8, io["wdeq_my"], wdeq_ag)
    emit_proj_tm(nc, "p5", n1, wdeq_ag, True, x2, res=x1, bias=io["fbias"])
    emit_ln(nc, x2, n2, n_rows=TOK, tag="p5l")
    emit_coll(nc, "n2", "AllGather", ALU.bypass, pairs, n2, io["n2_full"])
    emit_p6(nc, io)
    emit_p7(nc, io)
    emit_coll(nc, "cx", "AllGather", ALU.bypass, pairs, io["ctx1"], ctx_ag)
    cx_loc = mid("cx_loc", [D, TOK], BF16)
    emit_reshard(nc, "cx", [
        (cx_loc[:, :], ctx_ag.tensor, [[4096, 4096], [1, 2048]], 0),
    ], io["off_p8"])
    emit_proj_tm(nc, "p8", cx_loc, io["w1o"], False, x3, res=x2, bias=io["b1o"],
                 act_mode="fm")
    emit_ln(nc, x3, io["n3_my"], n_rows=TOK, tag="p8l")
    emit_proj_tm(nc, "p9h", io["n3_my"], io["W1T"], False, io["x3p_my"],
                 res=x3, bias=io["b1"])
    emit_p9z(nc, io)
    emit_coll(nc, "mx", "AllReduce", ALU.max, all8, io["mx_my"], mx_g)
    io["mx_all"] = mx_g.rearrange("t o -> o t")
    emit_p9b(nc, io, mx_pre=True)
    return nc


_IN_KEYS = ["x_my", "pack_my", "wsc_my", "wqk_t", "wv_t", "wg_t", "nw_bc", "wo_t",
            "fbias", "w1qk", "w1v", "b1qk", "b1v2", "sl_kpos", "slqp", "dmask",
            "w1o", "b1o", "W1T", "b1", "A1", "LiMEsT", "LiMEs", "Bmat", "identf",
            "ident", "qd_bc", "kd", "bd", "ddT"]


def kernel(**inputs):
    pc = host_prep(inputs)
    cc = np.ascontiguousarray
    x = np.asarray(inputs["L0_attn_hidden_states"])
    nc = build_fused()
    in_maps = []
    for c in range(NCORE):
        b, h = c // 2, c % 2
        d = {k: cc(np.asarray(pc[c][k]).astype(np.float32)
                   if np.asarray(pc[c][k]).dtype == np.float64 else pc[c][k])
             for k in _IN_KEYS}
        d["fbias"] = cc(d["fbias"].astype(np.float32))
        d["b1o"] = cc(d["b1o"].astype(np.float32))
        d["b1"] = cc(d["b1"].astype(np.float32))
        d["x_full"] = cc(x[b])
        d["off_p4"] = np.array([[h * 2048 * 2048]], np.int32)
        d["off_p8"] = np.array([[h * 2048]], np.int32)
        in_maps.append(d)
    res = run_bass_kernel_spmd(nc, in_maps, core_ids=list(range(NCORE)))
    out = np.zeros((B, S, D), np.float32)
    for c in range(NCORE):
        b, h = c // 2, c % 2
        out[b, h * TOK:(h + 1) * TOK] = res.results[c]["x4_my"]
    return out



def _run(nc, in_maps):
    res = run_bass_kernel_spmd(nc, in_maps, core_ids=list(range(NCORE)))
    return res.results


def kernel_unfused(**inputs):
    pc = host_prep(inputs)
    cc = np.ascontiguousarray

    r0 = _run(build_p0(), [{k: cc(pc[c][k]) for k in ["x_my", "pack_my", "wsc_my"]}
                           for c in range(NCORE)])
    n0_full = [np.concatenate([r0[2 * b]["n0_my"], r0[2 * b + 1]["n0_my"]], 0)
               for b in range(4)]
    wdeq_all = np.concatenate([r0[c]["wdeq_my"] for c in range(NCORE)], 0)

    r1 = _run(build_p1(), [dict(n0_full=n0_full[c // 2],
                                **{k: cc(pc[c][k]) for k in ["wqk_t", "wv_t", "wg_t"]})
                           for c in range(NCORE)])

    r2 = _run(build_p2(), [dict(qk0=r1[c]["qk0"], v0fm=r1[c]["v0fm"],
                                **{k: cc(pc[c][k]) for k in ["qd_bc", "kd", "bd", "ddT"]})
                           for c in range(NCORE)])

    r3 = _run(build_p3(), [dict(attn0=r2[c]["attn0"], g0fm=r1[c]["g0fm"],
                                nw_bc=cc(pc[c]["nw_bc"]), ss_my=r2[c]["ss_my"],
                                ss_pt=r2[c ^ 1]["ss_my"]) for c in range(NCORE)])

    gn_my = []
    for c in range(NCORE):
        b, h = c // 2, c % 2
        gn_all = np.concatenate([r3[2 * b]["gn0"], r3[2 * b + 1]["gn0"]], 1)
        gn_my.append(cc(gn_all[h * TOK:(h + 1) * TOK]))
    r4 = _run(build_p4(), [dict(gn_my=gn_my[c], wo_t=cc(pc[c]["wo_t"]),
                                x_my=cc(pc[c]["x_my"])) for c in range(NCORE)])

    r5 = _run(build_p5(), [dict(n1_my=r4[c]["n1_my"], wdeq=wdeq_all,
                                fbias=cc(pc[c]["fbias"].astype(np.float32)),
                                x1_my=r4[c]["x1_my"]) for c in range(NCORE)])
    n2_full = [np.concatenate([r5[2 * b]["n2_my"], r5[2 * b + 1]["n2_my"]], 0)
               for b in range(4)]

    r6 = _run(build_p6(), [dict(n2_full=n2_full[c // 2],
                                **{k: cc(pc[c][k]) for k in ["w1qk", "w1v", "b1qk", "b1v2"]})
                           for c in range(NCORE)])

    r7 = _run(build_p7(), [dict(qk1=r6[c]["qk1"], v1fm=r6[c]["v1fm"],
                                **{k: cc(pc[c][k]) for k in ["sl_kpos", "slqp", "dmask"]})
                           for c in range(NCORE)])

    ctx_my = []
    for c in range(NCORE):
        b, h = c // 2, c % 2
        ctx_all = np.concatenate([r7[2 * b]["ctx1"], r7[2 * b + 1]["ctx1"]], 0)
        ctx_my.append(cc(ctx_all.T[h * TOK:(h + 1) * TOK]))
    r8 = _run(build_p8(), [dict(ctx_my=ctx_my[c], w1o=cc(pc[c]["w1o"].astype(np.float32)),
                                b1o=cc(pc[c]["b1o"].astype(np.float32)),
                                x2_my=r5[c]["x2_my"]) for c in range(NCORE)])

    r9a = _run(build_p9a(), [dict(n3_my=r8[c]["n3_my"], x3_my=r8[c]["x3_my"],
                                  **{k: cc(np.asarray(pc[c][k]).astype(np.float32))
                                     for k in ["W1T", "b1", "A1", "LiMEsT", "identf"]})
                             for c in range(NCORE)])
    mx_all = np.concatenate([np.array([r9a[c]["mx_my"][0, 0] for c in range(NCORE)]),
                             np.array([r9a[c]["mx_my"][1, 0] for c in range(NCORE)])]
                            ).astype(np.float32)[None, :]

    r9b = _run(build_p9b(), [dict(mx_all=cc(mx_all), h8_my=r9a[c]["h8_my"],
                                  ds_my=r9a[c]["ds_my"], z_my=r9a[c]["z_my"],
                                  LiMEs=cc(pc[c]["LiMEs"].astype(np.float32)),
                                  Bmat=cc(pc[c]["Bmat"].astype(np.float32)),
                                  x3p_my=r9a[c]["x3p_my"], ident=cc(pc[c]["ident"]))
                             for c in range(NCORE)])

    out = np.zeros((B, S, D), np.float32)
    for c in range(NCORE):
        b, h = c // 2, c % 2
        out[b, h * TOK:(h + 1) * TOK] = r9b[c]["x4_my"]
    return out


# version 08
# speedup vs baseline: 1.0175x; 1.0175x over previous
import math
from contextlib import ExitStack

import numpy as np
import ml_dtypes

import concourse.bass as bass
import concourse.mybir as mybir
from concourse.bass_utils import run_bass_kernel_spmd

F32 = mybir.dt.float32
BF16 = mybir.dt.bfloat16
I32 = mybir.dt.int32
AF = mybir.ActivationFunctionType
ALU = mybir.AluOpType
AX = mybir.AxisListType
BF = ml_dtypes.bfloat16

B, S, D = 4, 4096, 4096
TOK = 2048
NHEAD = 32
HD = 128
HPC = 16
BLK = 256
NB = S // BLK
EPS_LN = 1e-5
EPS_RMS = 1e-5
NCORE = 8



def _l0_slopes():
    base = 1.0 / 2.0 ** (8.0 / NHEAD)
    exponent = np.arange(NHEAD, dtype=np.float32) + 1.0
    factor = 1.0 - 1 / (32 - 1 + 1e-5) + 1e-5
    return (base ** exponent * factor).astype(np.float32)


def _l1_slopes():
    n = 32
    cp2 = 2 ** math.floor(math.log2(n))
    base = 2 ** (-(2 ** -(math.log2(cp2) - 3)))
    powers = np.arange(1, 1 + cp2, dtype=np.float32)
    return np.power(base, powers).astype(np.float32)


def host_tables(core):
    h = core % 2
    my_heads = list(range(h * HPC, h * HPC + HPC))
    sl0 = _l0_slopes()
    r = np.arange(BLK, dtype=np.float32) + 1.0
    t = {}
    qd = np.exp(-sl0[my_heads][:, None] * r[None, :]).astype(np.float32)
    kd = np.exp(-sl0[my_heads][:, None] * (BLK - r)[None, :]).astype(np.float32)
    bd = np.exp(-sl0[my_heads] * BLK).astype(np.float32)
    diff = r[:, None] - r[None, :]
    ddT = []
    for hh in my_heads:
        dg = sl0[hh] * (-diff)
        dmat = np.where(diff >= 0, np.exp(-sl0[hh] * diff, dtype=np.float32), 0.0)
        ddT.append(dmat.T.astype(np.float32))
    t["qd"] = qd
    t["kd"] = np.ascontiguousarray(kd.reshape(16, 2, 128).transpose(0, 2, 1))
    t["bd"] = np.repeat(bd[:, None], 128, 1)[:, :, None].astype(np.float32)
    t["ddT"] = np.stack(ddT)
    t["qd_bc"] = np.repeat(qd[:, None, :], 128, 1).astype(np.float32)
    heads_l1 = [2 * j + h for j in range(HPC)]
    sl1 = _l1_slopes()[heads_l1]
    kpos = np.arange(S, dtype=np.float32).reshape(32, 128)
    t["sl_kpos"] = np.ascontiguousarray(
        (sl1[:, None, None] * kpos[None]).transpose(0, 2, 1)).astype(np.float32)
    t["slqp"] = (sl1[:, None] * np.arange(S, dtype=np.float32)[None, :]).astype(np.float32)
    masks = np.zeros((4, 128, 512), np.float32)
    for a in range(4):
        kk = a * 128 + np.arange(128)
        qq = np.arange(512)
        masks[a] = np.where(kk[:, None] <= qq[None, :], 0.0, -1.0e8)
    t["dmask"] = masks
    t["ident"] = np.eye(128, dtype=BF)
    par = (np.arange(128) % 2 == 1).astype(np.float32)[:, None]
    t["odd_par"] = par
    return t


def host_prep(inputs):
    x = np.asarray(inputs["L0_attn_hidden_states"])
    qkv_wT = np.ascontiguousarray(np.asarray(inputs["L0_attn_qkv_w"]).T)
    out_wT = np.ascontiguousarray(np.asarray(inputs["L0_attn_out_w"]).T)
    gate_wT = np.ascontiguousarray(np.asarray(inputs["L0_attn_gate_w"]).T)
    norm_w = np.asarray(inputs["L0_attn_norm_w"])
    packed = np.asarray(inputs["L0_ffn_packed_weight"])
    wscale = np.asarray(inputs["L0_ffn_scale"])
    fbias = np.asarray(inputs["L0_ffn_bias"])
    w_qkv1 = np.asarray(inputs["L1_attn_w_qkv"])
    b_qkv1 = np.asarray(inputs["L1_attn_b_qkv"])
    w_out1 = np.asarray(inputs["L1_attn_w_out"])
    b_out1 = np.asarray(inputs["L1_attn_b_out"])
    W1T = np.ascontiguousarray(np.asarray(inputs["L1_ffn_W"]).T)
    b1 = np.asarray(inputs["L1_ffn_b"])
    A1 = np.asarray(inputs["L1_ffn_A"])
    Bmat = np.asarray(inputs["L1_ffn_Bmat"])
    LiMEs = np.asarray(inputs["L1_ffn_LiMEs"])
    LiMEsT = np.ascontiguousarray(LiMEs.T)

    per_core = []
    for c in range(NCORE):
        b = c // 2
        h = c % 2
        heads = list(range(h * HPC, h * HPC + HPC))
        heads_l1 = [2 * j + h for j in range(HPC)]
        qk_cols = np.concatenate([np.arange(hh * 384, hh * 384 + 256) for hh in heads])
        v_cols = np.concatenate([np.arange(hh * 384 + 256, hh * 384 + 384) for hh in heads])
        d = {
            "x_my": np.ascontiguousarray(x[b, h * TOK:(h + 1) * TOK]),
            "wqk_t": np.ascontiguousarray(qkv_wT[:, qk_cols]),
            "wv_t": np.ascontiguousarray(qkv_wT[:, v_cols]),
            "wg_t": np.ascontiguousarray(gate_wT[:, h * 2048:(h + 1) * 2048]),
            "wo_t": out_wT,
            "norm_w": norm_w[None, :].astype(np.float32),
            "nw_bc": np.ascontiguousarray(np.repeat(
                norm_w[h * 2048:(h + 1) * 2048][None, :], 128, 0).astype(np.float32)),
            "pack_my": np.ascontiguousarray(packed[c * 512:(c + 1) * 512]),
            "wsc_my": np.ascontiguousarray(wscale[c * 512:(c + 1) * 512]),
            "fbias": fbias[None, :],
            "w1qk": np.ascontiguousarray(
                w_qkv1[:, np.concatenate([np.concatenate(
                    [np.arange(hh * 128, hh * 128 + 128),
                     np.arange(4096 + hh * 128, 4096 + hh * 128 + 128)]) for hh in heads_l1])]),
            "w1v": np.ascontiguousarray(
                w_qkv1[:, np.concatenate([np.arange(8192 + hh * 128, 8192 + hh * 128 + 128)
                                          for hh in heads_l1])]),
            "b1qk": None,
            "b1v2": None,
            "w1o": np.ascontiguousarray(w_out1[np.concatenate(
                [np.arange((2 * j + p) * 128, (2 * j + p) * 128 + 128)
                 for p in range(2) for j in range(HPC)])]),
            "b1o": b_out1[None, :],
            "W1T": W1T, "b1": b1[None, :], "A1": A1, "Bmat": Bmat,
            "LiMEs": LiMEs, "LiMEsT": LiMEsT,
        }
        qkb = np.concatenate([np.concatenate(
            [b_qkv1[hh * 128:hh * 128 + 128], b_qkv1[4096 + hh * 128:4096 + hh * 128 + 128]])
            for hh in heads_l1])
        vb = np.concatenate([b_qkv1[8192 + hh * 128:8192 + hh * 128 + 128] for hh in heads_l1])
        d["b1qk"] = np.ascontiguousarray(qkb.reshape(32, 128).T).astype(np.float32)
        d["b1v2"] = np.ascontiguousarray(vb.reshape(16, 128).T).astype(np.float32)
        d["identf"] = np.eye(128, dtype=np.float32)
        d.update(host_tables(c))
        per_core.append(d)
    return per_core



class Sem:

    def __init__(self, handle):
        self.h = handle
        self.n = 0

    def inc(self, inst, k=1):
        inst.then_inc(self.h, k)
        self.n += k
        return self.n


def mk_sems(st, nc, names):
    return {n: Sem(st.enter_context(nc.semaphore(n))) for n in names}


def dinc(eng, sem, k=1):
    sem.inc(eng.drain(), k)



def emit_ln(nc, src, dst, raw_dst=None, n_rows=TOK, tag="ln"):
    NT = n_rows // 128
    with ExitStack() as st:
        def sb(name, shape, dt):
            return st.enter_context(nc.sbuf_tensor(tag + name, shape, dt))
        xt = [sb(f"x{j}", [128, D], F32) for j in range(2)]
        n0t = [sb(f"n{j}", [128, D], BF16) for j in range(2)]
        bstat = sb("bst", [128, 48], F32)
        mvc = sb("mvc", [128, 2 * NT], F32)
        lnv = sb("lnv", [128, NT], F32)
        lnh = sb("lnh", [128, NT], F32)
        rstd = sb("rstd", [128, NT], F32)
        epsc = sb("eps", [128, 1], F32)
        sems = mk_sems(st, nc, [tag + k for k in ["ld", "cv", "ca", "so"]])
        ld, cv, ca, so = (sems[tag + k] for k in ["ld", "cv", "ca", "so"])
        blk = st.enter_context(nc.Block())

        @blk.sync
        def _(sy):
            for j in range(NT):
                if j >= 2:
                    sy.wait_ge(cv.h, j - 1)
                sy.dma_start(out=xt[j % 2][:], in_=src[j * 128:(j + 1) * 128, :]).then_inc(ld.h, 16)
            sy.wait_ge(ca.h, 2)
            for j in range(NT):
                if j >= 2:
                    sy.wait_ge(cv.h, NT + 2 + (j - 1))
                sy.dma_start(out=xt[j % 2][:], in_=src[j * 128:(j + 1) * 128, :]).then_inc(ld.h, 16)
            ld.n = 32 * NT

        @blk.vector
        def _(v):
            for j in range(NT):
                v.wait_ge(ld.h, 16 * (j + 1))
                for c8 in range(8):
                    v.bn_stats(out=bstat[:, c8 * 6:(c8 + 1) * 6],
                               in_=xt[j % 2][:, c8 * 512:(c8 + 1) * 512])
                v.drain()
                v.bn_aggr(out=mvc[:, 2 * j:2 * j + 2], in_=bstat[:])
                dinc(v, cv)
            v.wait_ge(ca.h, 1)
            v.tensor_scalar_mul(out=lnh[:], in0=lnv[:], scalar1=-0.5)
            dinc(v, cv)
            dinc(v, cv)
            v.wait_ge(ca.h, 2)
            for j in range(NT):
                v.wait_ge(ld.h, 16 * (NT + j + 1))
                if j >= 2:
                    v.wait_ge(so.h, 16 * (j - 1))
                v.tensor_scalar(out=n0t[j % 2][:], in0=xt[j % 2][:],
                                scalar1=mvc[:, 2 * j:2 * j + 1],
                                scalar2=rstd[:, j:j + 1],
                                op0=ALU.subtract, op1=ALU.mult)
                dinc(v, cv)

        @blk.scalar
        def _(s):
            s.wait_ge(cv.h, NT)
            s.activation(out=lnv[:], in_=mvc[:].rearrange("p (j k) -> p j k", k=2)[:, :, 1:2],
                         func=AF.Ln, bias=epsc[:])
            dinc(s, ca)
            s.wait_ge(cv.h, NT + 1)
            s.activation(out=rstd[:], in_=lnh[:], func=AF.Exp)
            dinc(s, ca)

        @blk.gpsimd
        def _(g):
            g.memset(epsc[:], EPS_LN)
            nr = 0
            if raw_dst is not None:
                q = n_rows // 4
                for j in range(4):
                    g.dma_start(out=raw_dst[j * q:(j + 1) * q, :],
                                in_=src[j * q:(j + 1) * q, :]).then_inc(so.h, 16)
                nr = 4
            for j in range(NT):
                g.wait_ge(cv.h, NT + 2 + j + 1)
                g.dma_start(out=dst[j * 128:(j + 1) * 128, :], in_=n0t[j % 2][:]).then_inc(so.h, 16)
            so.n = 16 * (NT + nr)

        @blk.sync
        def _(sy):
            sy.wait_ge(so.h, so.n)

    return nc


def emit_dequant(nc, io):
    pk = io["pack_my"]
    sc = io["wsc_my"]
    wq = io["wdeq_my"]
    with ExitStack() as st:
        def sb(name, shape, dt):
            return st.enter_context(nc.sbuf_tensor(name, shape, dt))
        pkt = sb("dqpk", [128, 2048], I32)
        sct = sb("dqsc", [128, 128], F32)
        shi = sb("dqshi", [128, 512], I32)
        wf_ = sb("dqwf", [128, 4096], F32)
        wb_ = sb("dqwb", [128, 4096], BF16)
        sems = mk_sems(st, nc, ["dqld", "dq", "dqs"])
        ld, dq, dqs = (sems[k] for k in ["dqld", "dq", "dqs"])
        blk = st.enter_context(nc.Block())

        @blk.sync
        def _(sy):
            sy.dma_start(out=pkt[:].rearrange("p (c n) -> p c n", c=4),
                         in_=pk.rearrange("(c p) n -> p c n", p=128)).then_inc(ld.h, 16)
            sy.dma_start(out=sct[:].rearrange("p (c n) -> p c n", c=4),
                         in_=sc.rearrange("(c p) n -> p c n", p=128)).then_inc(ld.h, 16)

        @blk.vector
        def _(v):
            v.wait_ge(ld.h, 32)
            for rc in range(4):
                pkv = pkt[:, rc * 512:(rc + 1) * 512]
                for e in range(8):
                    v.tensor_scalar(out=shi[:], in0=pkv, scalar1=4 * e, scalar2=28,
                                    op0=ALU.logical_shift_left, op1=ALU.arith_shift_right)
                    v.tensor_copy(out=wf_[:, e:4096:8], in_=shi[:])
                if rc >= 1:
                    v.wait_ge(dqs.h, 16 * rc)
                for gsc in range(32):
                    v.tensor_scalar(out=wb_[:, gsc * 128:(gsc + 1) * 128],
                                    in0=wf_[:, gsc * 128:(gsc + 1) * 128],
                                    scalar1=sct[:, rc * 32 + gsc:rc * 32 + gsc + 1],
                                    scalar2=None, op0=ALU.mult)
                dinc(v, dq)

        @blk.gpsimd
        def _(g):
            for rc in range(4):
                g.wait_ge(dq.h, rc + 1)
                g.dma_start(out=wq[rc * 128:(rc + 1) * 128, :], in_=wb_[:]).then_inc(dqs.h, 16)

        @blk.sync
        def _(sy):
            sy.wait_ge(dqs.h, 64)
    return nc


def emit_p0(nc, io):
    emit_ln(nc, io["x_my"], io["n0_my"], n_rows=TOK, tag="p0")
    emit_dequant(nc, io)
    return nc



def emit_p1(nc, io):
    n0 = io["n0_full"]
    wqk = io["wqk_t"]
    wv = io["wv_t"]
    wg = io["wg_t"]
    qk0 = io["qk0"]
    v0 = io["v0fm"]
    g0 = io["g0fm"]
    NF = 64

    for th in (0, 1):
        with ExitStack() as st:
            def sb(name, shape, dt):
                return st.enter_context(nc.sbuf_tensor(f"p1{th}{name}", shape, dt))
            rhsf = [sb(f"r{i}", [128, 2048], BF16) for i in range(32)]
            wbuf = [sb(f"w{j}", [128, 4096], BF16) for j in range(2)]
            evb = [sb(f"e{j}", [128, 512], BF16) for j in range(8)]
            ps = [st.enter_context(nc.psum_tensor(f"p1{th}ps{j}", [128, 512], F32))
                  for j in range(8)]
            sems = mk_sems(st, nc, [f"p1{th}{k}" for k in ["ldr", "wl", "wf", "ce", "se"]])
            ldr, wl, wf, ce, se = (sems[f"p1{th}{k}"] for k in ["ldr", "wl", "wf", "ce", "se"])
            blk = st.enter_context(nc.Block())
            tsl = slice(th * 2048, (th + 1) * 2048)

            @blk.sync
            def _(sy):
                for i in range(32):
                    sy.dma_start(out=rhsf[i][:], in_=n0[tsl, i * 128:(i + 1) * 128],
                                 transpose=True).then_inc(ldr.h, 16)

            @blk.gpsimd
            def _(g):
                for f in range(NF):
                    if f < 32:
                        src = wqk[:, f * 128:(f + 1) * 128]
                    elif f < 48:
                        src = wv[:, (f - 32) * 128:(f - 31) * 128]
                    else:
                        src = wg[:, (f - 48) * 128:(f - 47) * 128]
                    if f >= 2:
                        g.wait_ge(wf.h, f - 1)
                    s3 = src.rearrange("(i p) c -> p i c", p=128)
                    d3 = wbuf[f % 2][:].rearrange("p (i c) -> p i c", c=128)
                    for q in range(4):
                        g.dma_start(out=d3[:, q * 8:(q + 1) * 8, :],
                                    in_=s3[:, q * 8:(q + 1) * 8, :]).then_inc(wl.h, 16)

            @blk.tensor
            def _(t):
                t.wait_ge(ldr.h, 16 * 32)
                for f in range(NF):
                    t.wait_ge(wl.h, 64 * (f + 1))
                    if f >= 2:
                        t.wait_ge(ce.h, 4 * (f - 1))
                    for i in range(32):
                        lhsT = wbuf[f % 2][:, i * 128:(i + 1) * 128]
                        for tt in range(4):
                            ti = t.matmul(ps[(f % 2) * 4 + tt][:],
                                          lhsT, rhsf[i][:, tt * 512:(tt + 1) * 512],
                                          start=(i == 0), stop=(i == 31))
                    wf.inc(ti)

            @blk.scalar
            def _(s):
                for f in range(NF):
                    s.wait_ge(wf.h, f + 1)
                    if f >= 2:
                        s.wait_ge(se.h, 16 * 4 * (f - 1))
                    func = AF.Silu if f < 48 else AF.Sigmoid
                    for tt in range(4):
                        si = s.activation(out=evb[(f % 2) * 4 + tt][:],
                                          in_=ps[(f % 2) * 4 + tt][:], func=func)
                        ce.inc(si)
                    for tt in range(4):
                        if f < 32:
                            dst = qk0[f * 128:(f + 1) * 128,
                                      th * 2048 + tt * 512: th * 2048 + (tt + 1) * 512]
                        elif f < 48:
                            dst = v0[(f - 32) * 128:(f - 31) * 128,
                                     th * 2048 + tt * 512: th * 2048 + (tt + 1) * 512]
                        else:
                            dst = g0[(f - 48) * 128:(f - 47) * 128,
                                     th * 2048 + tt * 512: th * 2048 + (tt + 1) * 512]
                        s.dma_start(out=dst, in_=evb[(f % 2) * 4 + tt][:]).then_inc(se.h, 16)

            @blk.sync
            def _(sy):
                sy.wait_ge(se.h, 16 * 4 * NF)
    return nc


def build_p1():
    nc = bass.Bass("TRN2", target_bir_lowering=False, debug=False)
    io = {
        "n0_full": nc.dram_tensor("n0_full", [S, D], BF16, kind="ExternalInput").ap(),
        "wqk_t": nc.dram_tensor("wqk_t", [D, 4096], F32, kind="ExternalInput").ap(),
        "wv_t": nc.dram_tensor("wv_t", [D, 2048], F32, kind="ExternalInput").ap(),
        "wg_t": nc.dram_tensor("wg_t", [D, 2048], F32, kind="ExternalInput").ap(),
        "qk0": nc.dram_tensor("qk0", [4096, S], BF16, kind="ExternalOutput").ap(),
        "v0fm": nc.dram_tensor("v0fm", [2048, S], BF16, kind="ExternalOutput").ap(),
        "g0fm": nc.dram_tensor("g0fm", [2048, S], BF16, kind="ExternalOutput").ap(),
    }
    emit_p1(nc, io)
    return nc



def emit_p2(nc, io):
    qk0 = io["qk0"]
    v0 = io["v0fm"]
    attn0 = io["attn0"]
    qdT = io["qd_bc"]
    kdT = io["kd"]
    bdT = io["bd"]
    ddTT = io["ddT"]
    identb = io["ident"]

    with ExitStack() as st:
        def sb(name, shape, dt):
            return st.enter_context(nc.sbuf_tensor(f"p2{name}", shape, dt))
        qsl = [sb(f"q{j}", [128, S], BF16) for j in range(2)]
        ksl = [sb(f"k{j}", [128, S], BF16) for j in range(2)]
        vsl = [sb(f"v{j}", [128, S], BF16) for j in range(2)]
        qdb = [sb(f"qd{j}", [128, 256], F32) for j in range(2)]
        kdc = [sb(f"kd{j}", [128, 2], F32) for j in range(2)]
        bdc = [sb(f"bd{j}", [128, 1], F32) for j in range(2)]
        dds = [sb(f"dd{j}", [128, 512], F32) for j in range(2)]
        idb = sb("idb", [128, 128], BF16)
        qs_ = [sb(f"qs{j}", [128, 256], BF16) for j in range(2)]
        stb = [sb(f"st{j}", [128, 512], BF16) for j in range(2)]
        kkd = [sb(f"kk{j}", [128, 256], BF16) for j in range(2)]
        vtm = [sb(f"vt{j}", [128, 256], BF16) for j in range(2)]
        atb = [sb(f"at{j}", [128, 256], BF16) for j in range(2)]
        kvf = [sb(f"kvf{j}", [128, 128], F32) for j in range(2)]
        kvbf = sb("kvbf", [128, 128], BF16)
        sqw = sb("sqw", [128, 256], F32)
        ssall = sb("ssall", [128, 512], F32)
        ssout = sb("ssout", [128, 32], F32)
        ps_st = [st.enter_context(nc.psum_tensor(f"p2pst{j}", [128, 256], F32)) for j in range(2)]
        ps_a = [st.enter_context(nc.psum_tensor(f"p2a{j}", [128, 128], F32)) for j in range(2)]
        ps_c = st.enter_context(nc.psum_tensor("p2c", [128, 128], F32))
        ps_tk = st.enter_context(nc.psum_tensor("p2tk", [128, 256], BF16))
        ps_tv = st.enter_context(nc.psum_tensor("p2tv", [128, 256], BF16))
        sems = mk_sems(st, nc, [f"p2{k}" for k in ["ld", "tm", "dv", "ca", "se"]])
        ld, tm, dv, ca, se = (sems[f"p2{k}"] for k in ["ld", "tm", "dv", "ca", "se"])
        blk = st.enter_context(nc.Block())

        def ldslot(s_):
            return 16 * (1 + 7 * (s_ + 1))

        @blk.sync
        def _(sy):
            sy.dma_start(out=idb[:], in_=identb).then_inc(ld.h, 16)
            for s_ in range(16):
                sp = s_ % 2
                if s_ >= 2:
                    sy.wait_ge(tm.h, 5 * 16 * (s_ - 1))
                    sy.wait_ge(dv.h, 5 * 16 * (s_ - 1))
                r0 = s_ * 256
                sy.dma_start(out=qsl[sp][:], in_=qk0[r0:r0 + 128, :]).then_inc(ld.h, 16)
                sy.dma_start(out=ksl[sp][:], in_=qk0[r0 + 128:r0 + 256, :]).then_inc(ld.h, 16)
                sy.dma_start(out=vsl[sp][:], in_=v0[s_ * 128:(s_ + 1) * 128, :]).then_inc(ld.h, 16)
                sy.dma_start(out=qdb[sp][:], in_=qdT[s_]).then_inc(ld.h, 16)
                sy.dma_start(out=kdc[sp][:], in_=kdT[s_]).then_inc(ld.h, 16)
                sy.dma_start(out=bdc[sp][:], in_=bdT[s_]).then_inc(ld.h, 16)
                sy.dma_start(out=dds[sp][:].rearrange("p (c n) -> p c n", c=2),
                             in_=ddTT[s_].rearrange("(c p) n -> p c n", p=128)).then_inc(ld.h, 16)

        @blk.vector
        def _(v):
            for s_ in range(16):
                sp = s_ % 2
                if s_ >= 1:
                    v.wait_ge(ca.h, 3 * 16 * s_)
                v.memset(kvf[0][:], 0.0)
                v.memset(kvbf[:], 0.0)
                for n_ in range(16):
                    u = s_ * 16 + n_
                    p = u % 2
                    c0 = n_ * 256
                    v.wait_ge(ld.h, ldslot(s_))
                    v.tensor_tensor(out=qs_[p][:], in0=qsl[sp][:, c0:c0 + 256],
                                    in1=qdb[sp][:], op=ALU.mult)
                    dinc(v, dv)
                    v.wait_ge(tm.h, 5 * u + 3)
                    v.tensor_tensor(out=stb[p][:, 0:256], in0=ps_st[0][:],
                                    in1=dds[sp][:, 0:256], op=ALU.mult)
                    v.tensor_tensor(out=stb[p][:, 256:512], in0=ps_st[1][:],
                                    in1=dds[sp][:, 256:512], op=ALU.mult)
                    dinc(v, dv)
                    v.wait_ge(tm.h, 5 * u + 1)
                    v.tensor_scalar(out=kkd[p][:, 0:128], in0=ps_tk[:, 0:128],
                                    scalar1=kdc[sp][:, 0:1], scalar2=None, op0=ALU.mult)
                    v.tensor_scalar(out=kkd[p][:, 128:256], in0=ps_tk[:, 128:256],
                                    scalar1=kdc[sp][:, 1:2], scalar2=None, op0=ALU.mult)
                    dinc(v, dv)
                    v.wait_ge(tm.h, 5 * u + 5)
                    v.scalar_tensor_tensor(out=kvf[(n_ + 1) % 2][:], in0=kvf[n_ % 2][:],
                                           scalar=bdc[sp][:], in1=ps_c[:],
                                           op0=ALU.mult, op1=ALU.add)
                    dinc(v, dv)
                    v.wait_ge(ca.h, 3 * u + 3)
                    v.scalar_tensor_tensor(out=sqw[:], in0=atb[p][:], scalar=0.0,
                                           in1=atb[p][:], op0=ALU.add, op1=ALU.mult)
                    for nh in range(2):
                        tc = n_ * 2 + nh
                        v.reduce_sum(out=ssall[:, tc * 16 + s_:tc * 16 + s_ + 1],
                                     in_=sqw[:, nh * 128:(nh + 1) * 128], axis=AX.X)
                    dv.inc(v.engine_nop())

        @blk.tensor
        def _(t):
            t.wait_ge(ld.h, 16)
            for s_ in range(16):
                sp = s_ % 2
                for n_ in range(16):
                    u = s_ * 16 + n_
                    p = u % 2
                    c0 = n_ * 256
                    t.wait_ge(ld.h, ldslot(s_))
                    if u >= 1:
                        t.wait_ge(dv.h, 5 * (u - 1) + 3)
                    t.transpose(ps_tk[:, 0:128], ksl[sp][:, c0:c0 + 128], idb[:])
                    ti = t.transpose(ps_tk[:, 128:256], ksl[sp][:, c0 + 128:c0 + 256], idb[:])
                    tm.inc(ti)
                    if u >= 1:
                        t.wait_ge(ca.h, 3 * (u - 1) + 1)
                    t.transpose(ps_tv[:, 0:128], vsl[sp][:, c0:c0 + 128], idb[:])
                    ti = t.transpose(ps_tv[:, 128:256], vsl[sp][:, c0 + 128:c0 + 256], idb[:])
                    tm.inc(ti)
                    if u >= 1:
                        t.wait_ge(dv.h, 5 * (u - 1) + 2)
                    for mh in range(2):
                        ti = t.matmul(ps_st[mh][:], ksl[sp][:, c0 + mh * 128:c0 + (mh + 1) * 128],
                                      qsl[sp][:, c0:c0 + 256], start=True, stop=True)
                    tm.inc(ti)
                    t.wait_ge(dv.h, 5 * u + 2)
                    t.wait_ge(ca.h, 3 * u + 1)
                    if u >= 1:
                        t.wait_ge(ca.h, 3 * (u - 1) + 3)
                    for nh in range(2):
                        t.matmul(ps_a[nh][:], stb[p][:, nh * 128:nh * 128 + 128],
                                 vtm[p][:, 0:128], start=True, stop=False)
                        t.matmul(ps_a[nh][:], stb[p][:, 256 + nh * 128:256 + nh * 128 + 128],
                                 vtm[p][:, 128:256], start=False, stop=False)
                        ti = t.matmul(ps_a[nh][:], qs_[p][:, nh * 128:(nh + 1) * 128],
                                      kvbf[:], start=False, stop=True)
                    tm.inc(ti)
                    t.wait_ge(dv.h, max(5 * u + 3, 5 * (u - 1) + 4) if u >= 1 else 5 * u + 3)
                    t.matmul(ps_c[:], kkd[p][:, 0:128], vtm[p][:, 0:128],
                             start=True, stop=False)
                    ti = t.matmul(ps_c[:], kkd[p][:, 128:256], vtm[p][:, 128:256],
                                  start=False, stop=True)
                    tm.inc(ti)

        @blk.scalar
        def _(s):
            for s_ in range(16):
                for n_ in range(16):
                    u = s_ * 16 + n_
                    p = u % 2
                    s.wait_ge(tm.h, 5 * u + 2)
                    if u >= 2:
                        s.wait_ge(tm.h, 5 * (u - 2) + 5)
                    si = s.activation(out=vtm[p][:], in_=ps_tv[:], func=AF.Copy)
                    dinc(s, ca)
                    s.wait_ge(dv.h, 5 * u + 4)
                    s.activation(out=kvbf[:], in_=kvf[(n_ + 1) % 2][:], func=AF.Copy)
                    dinc(s, ca)
                    s.wait_ge(tm.h, 5 * u + 4)
                    if u >= 1:
                        s.wait_ge(se.h, 32 * u)
                    if u >= 2:
                        s.wait_ge(dv.h, 5 * (u - 2) + 5)
                    s.activation(out=atb[p][:, 0:128], in_=ps_a[0][:], func=AF.Copy)
                    s.activation(out=atb[p][:, 128:256], in_=ps_a[1][:], func=AF.Copy)
                    dinc(s, ca)
                    for nh in range(2):
                        s.dma_start(out=attn0[n_ * 256 + nh * 128:n_ * 256 + (nh + 1) * 128,
                                             s_ * 128:(s_ + 1) * 128],
                                    in_=atb[p][:, nh * 128:(nh + 1) * 128]).then_inc(se.h, 16)

        @blk.vector
        def _(v):
            v.reduce_sum(out=ssout[:].rearrange("p (a o) -> p a o", o=1),
                         in_=ssall[:].rearrange("p (a s) -> p a s", s=16), axis=AX.X)
            dinc(v, dv)

        @blk.gpsimd
        def _(g):
            g.wait_ge(dv.h, 5 * 256 + 1)
            g.dma_start(out=io["ss_my"], in_=ssout[:]).then_inc(se.h, 16)

        @blk.sync
        def _(sy):
            sy.wait_ge(se.h, 32 * 256 + 16)
    return nc


def build_p2():
    nc = bass.Bass("TRN2", target_bir_lowering=False, debug=False)
    io = {
        "qk0": nc.dram_tensor("qk0", [4096, S], BF16, kind="ExternalInput").ap(),
        "v0fm": nc.dram_tensor("v0fm", [2048, S], BF16, kind="ExternalInput").ap(),
        "qd_bc": nc.dram_tensor("qd_bc", [16, 128, 256], F32, kind="ExternalInput").ap(),
        "kd": nc.dram_tensor("kd", [16, 128, 2], F32, kind="ExternalInput").ap(),
        "bd": nc.dram_tensor("bd", [16, 128, 1], F32, kind="ExternalInput").ap(),
        "ddT": nc.dram_tensor("ddT", [16, 256, 256], F32, kind="ExternalInput").ap(),
        "ident": nc.dram_tensor("ident", [128, 128], BF16, kind="ExternalInput").ap(),
        "attn0": nc.dram_tensor("attn0", [S, 2048], BF16, kind="ExternalOutput").ap(),
        "ss_my": nc.dram_tensor("ss_my", [128, 32], F32, kind="ExternalOutput").ap(),
    }
    emit_p2(nc, io)
    return nc



def emit_p3(nc, io, ss_ag=False):
    attn0 = io["attn0"]
    g0 = io["g0fm"]
    nwb = io["nw_bc"]
    if ss_ag:
        ssm = io["ss_ag"][0:128, :]
        ssp = io["ss_ag"][128:256, :]
    else:
        ssm = io["ss_my"]
        ssp = io["ss_pt"]
    gn0 = io["gn0"]

    with ExitStack() as st:
        def sb(name, shape, dt):
            return st.enter_context(nc.sbuf_tensor(f"p3{name}", shape, dt))
        at_t = [sb(f"a{j}", [128, 2048], BF16) for j in range(2)]
        gt_t = [sb(f"g{j}", [128, 2048], BF16) for j in range(2)]
        t2 = sb("t2", [128, 2048], F32)
        gnt = [sb(f"o{j}", [128, 2048], BF16) for j in range(2)]
        nwt = sb("nw", [128, 2048], F32)
        ssa = sb("ssa", [128, 32], F32)
        ssb = sb("ssb", [128, 32], F32)
        w1 = sb("w1", [128, 32], F32)
        lnv = sb("lnv", [128, 32], F32)
        lnh = sb("lnh", [128, 32], F32)
        rstd = sb("rstd", [128, 32], F32)
        epsc = sb("eps", [128, 1], F32)
        sems = mk_sems(st, nc, [f"p3{k}" for k in ["ld", "cv", "ca", "so"]])
        ld, cv, ca, so = (sems[f"p3{k}"] for k in ["ld", "cv", "ca", "so"])
        blk = st.enter_context(nc.Block())

        @blk.sync
        def _(sy):
            sy.dma_start(out=nwt[:], in_=nwb).then_inc(ld.h, 16)
            sy.dma_start(out=ssa[:], in_=ssm).then_inc(ld.h, 16)
            sy.dma_start(out=ssb[:], in_=ssp).then_inc(ld.h, 16)
            for tc in range(32):
                if tc >= 2:
                    sy.wait_ge(cv.h, 4 + 2 * (tc - 2) + 2)
                sy.dma_start(out=at_t[tc % 2][:],
                             in_=attn0[tc * 128:(tc + 1) * 128, :]).then_inc(ld.h, 16)
                sy.dma_start(out=gt_t[tc % 2][:], in_=g0[:, tc * 128:(tc + 1) * 128],
                             transpose=True).then_inc(ld.h, 16)

        @blk.vector
        def _(v):
            v.memset(epsc[:], EPS_RMS)
            v.wait_ge(ld.h, 48)
            v.tensor_tensor(out=w1[:], in0=ssa[:], in1=ssb[:], op=ALU.add)
            dinc(v, cv)
            v.wait_ge(ca.h, 1)
            v.tensor_scalar_mul(out=lnh[:], in0=lnv[:], scalar1=-0.5)
            dinc(v, cv)
            dinc(v, cv)
            v.wait_ge(ca.h, 2)
            dinc(v, cv)
            for tc in range(32):
                v.wait_ge(ld.h, 48 + 32 * (tc + 1))
                v.scalar_tensor_tensor(out=t2[:], in0=at_t[tc % 2][:],
                                       scalar=rstd[:, tc:tc + 1], in1=nwt[:],
                                       op0=ALU.mult, op1=ALU.mult)
                if tc >= 2:
                    v.wait_ge(so.h, 16 * (tc - 1))
                v.tensor_tensor(out=gnt[tc % 2][:], in0=t2[:], in1=gt_t[tc % 2][:], op=ALU.mult)
                dinc(v, cv)
                dinc(v, cv)

        @blk.scalar
        def _(s):
            s.wait_ge(cv.h, 1)
            s.activation(out=lnv[:], in_=w1[:], func=AF.Ln, scale=1.0 / D, bias=epsc[:])
            dinc(s, ca)
            s.wait_ge(cv.h, 2)
            s.activation(out=rstd[:], in_=lnh[:], func=AF.Exp)
            dinc(s, ca)

        @blk.gpsimd
        def _(g):
            for tc in range(32):
                g.wait_ge(cv.h, 4 + 2 * tc + 2)
                g.dma_start(out=gn0[tc * 128:(tc + 1) * 128, :],
                            in_=gnt[tc % 2][:]).then_inc(so.h, 16)

        @blk.sync
        def _(sy):
            sy.wait_ge(so.h, 16 * 32)
    return nc



def emit_proj_tm(nc, tag, act, w_ap, w_is_bf16_fxi, out, res=None, bias=None,
                 act_mode="tm"):
    with ExitStack() as st:
        def sb(name, shape, dt):
            return st.enter_context(nc.sbuf_tensor(f"{tag}{name}", shape, dt))
        wsb = [sb(f"w{i}", [128, 2048], BF16) for i in range(32)]
        gnf = [[sb(f"a{q}_{i}", [128, 512], BF16) for i in range(32)] for q in range(2)]
        xt = sb("x0", [128, 1024], F32)
        xo = sb("o0", [128, 1024], F32)
        ones1 = sb("one", [1, 128], BF16)
        brow = sb("b", [1, 2048], BF16) if bias is not None else None
        ps = [st.enter_context(nc.psum_tensor(f"{tag}ps{j}", [128, 512], F32))
              for j in range(8)]
        sems = mk_sems(st, nc, [f"{tag}{k}" for k in ["lw", "la", "lx", "mm", "cv", "so", "pr"]])
        lw, la, lx, mm, cv, so, pr = (sems[f"{tag}{k}"]
                                      for k in ["lw", "la", "lx", "mm", "cv", "so", "pr"])
        blk = st.enter_context(nc.Block())

        @blk.vector
        def _(v):
            v.memset(ones1[:], 1.0)
            dinc(v, pr)
            g = 0
            for fh in range(2):
                for tq in range(4):
                    for tc in range(4):
                        g += 1
                        v.wait_ge(mm.h, g)
                        p = (g - 1) % 2
                        for half in range(2):
                            v.wait_ge(lx.h, 32 * g - 16 + 16 * half)
                            if g > 1 or half == 1:
                                v.wait_ge(so.h, 32 * (g - 1) + 16 * half)
                            for f2 in range(2):
                                fo = half * 2 + f2
                                v.tensor_tensor(out=xo[:, f2 * 512:(f2 + 1) * 512],
                                                in0=ps[p * 4 + fo][:],
                                                in1=xt[:, f2 * 512:(f2 + 1) * 512], op=ALU.add)
                            dinc(v, cv)

        @blk.gpsimd
        def _(g_):
            g = 0
            for fh in range(2):
                if fh == 1 and (bias is not None or not w_is_bf16_fxi):
                    g_.wait_ge(mm.h, 16)
                if bias is not None:
                    g_.dma_start(out=brow[:],
                                 in_=bias[:, fh * 2048:(fh + 1) * 2048]).then_inc(lw.h, 16)
                if not w_is_bf16_fxi:
                    for i in range(32):
                        g_.dma_start(out=wsb[i][:],
                                     in_=w_ap[i * 128:(i + 1) * 128, fh * 2048:(fh + 1) * 2048]
                                     ).then_inc(lw.h, 16)
                for tq in range(4):
                    for tc in range(4):
                        g += 1
                        for ho in range(2):
                            g_.wait_ge(cv.h, 2 * (g - 1) + ho + 1)
                            g_.dma_start(
                                out=out[tq * 512 + tc * 128:tq * 512 + (tc + 1) * 128,
                                        fh * 2048 + ho * 1024:fh * 2048 + (ho + 1) * 1024],
                                in_=xo[:]).then_inc(so.h, 16)

        @blk.sync
        def _(sy):
            g = 0
            for fh in range(2):
                if w_is_bf16_fxi:
                    if fh == 1:
                        sy.wait_ge(mm.h, 16)
                    for i in range(32):
                        sy.dma_start(out=wsb[i][:],
                                     in_=w_ap[fh * 2048:(fh + 1) * 2048, i * 128:(i + 1) * 128],
                                     transpose=True).then_inc(lw.h, 16)
                for tq in range(4):
                    gq = fh * 4 + tq
                    if gq >= 2:
                        sy.wait_ge(mm.h, 4 * (gq - 1))
                    for i in range(32):
                        if act_mode == "tm":
                            sy.dma_start(out=gnf[gq % 2][i][:],
                                         in_=act[tq * 512:(tq + 1) * 512, i * 128:(i + 1) * 128],
                                         transpose=True).then_inc(la.h, 16)
                        else:
                            sy.dma_start(out=gnf[gq % 2][i][:],
                                         in_=act[i * 128:(i + 1) * 128, tq * 512:(tq + 1) * 512]
                                         ).then_inc(la.h, 16)
                    for tc in range(4):
                        g += 1
                        for half in range(2):
                            if g > 1 or half == 1:
                                sy.wait_ge(cv.h, 2 * (g - 1) + half)
                            sy.dma_start(
                                out=xt[:],
                                in_=res[tq * 512 + tc * 128:tq * 512 + (tc + 1) * 128,
                                        fh * 2048 + half * 1024:fh * 2048 + (half + 1) * 1024]
                            ).then_inc(lx.h, 16)

        @blk.tensor
        def _(t):
            nb = 0 if bias is None else 1
            nwl = 32 + nb if not w_is_bf16_fxi else 32 + nb
            t.wait_ge(pr.h, 1)
            g = 0
            for fh in range(2):
                t.wait_ge(lw.h, 16 * nwl * (fh + 1))
                for tq in range(4):
                    gq = fh * 4 + tq
                    t.wait_ge(la.h, 16 * 32 * (gq + 1))
                    for tc in range(4):
                        g += 1
                        p = (g - 1) % 2
                        if g > 2:
                            t.wait_ge(cv.h, 2 * (g - 2))
                        for i in range(32):
                            for fo in range(4):
                                ti = t.matmul(ps[p * 4 + fo][:],
                                              gnf[gq % 2][i][:, tc * 128:(tc + 1) * 128],
                                              wsb[i][:, fo * 512:(fo + 1) * 512],
                                              start=(i == 0), stop=(i == 31 and bias is None))
                        if bias is not None:
                            for fo in range(4):
                                ti = t.matmul(ps[p * 4 + fo][:], ones1[:],
                                              brow[:, fo * 512:(fo + 1) * 512],
                                              start=False, stop=True)
                        mm.inc(ti)

        @blk.sync
        def _(sy):
            sy.wait_ge(so.h, 32 * 32)
    return nc



def build_p3():
    nc = bass.Bass("TRN2", target_bir_lowering=False, debug=False)
    io = {
        "attn0": nc.dram_tensor("attn0", [S, 2048], BF16, kind="ExternalInput").ap(),
        "g0fm": nc.dram_tensor("g0fm", [2048, S], BF16, kind="ExternalInput").ap(),
        "nw_bc": nc.dram_tensor("nw_bc", [128, 2048], F32, kind="ExternalInput").ap(),
        "ss_my": nc.dram_tensor("ss_my", [128, 32], F32, kind="ExternalInput").ap(),
        "ss_pt": nc.dram_tensor("ss_pt", [128, 32], F32, kind="ExternalInput").ap(),
        "gn0": nc.dram_tensor("gn0", [S, 2048], BF16, kind="ExternalOutput").ap(),
    }
    emit_p3(nc, io)
    return nc


def build_p4():
    nc = bass.Bass("TRN2", target_bir_lowering=False, debug=False)
    gn_my = nc.dram_tensor("gn_my", [TOK, D], BF16, kind="ExternalInput").ap()
    wo_t = nc.dram_tensor("wo_t", [D, D], F32, kind="ExternalInput").ap()
    x_my = nc.dram_tensor("x_my", [TOK, D], F32, kind="ExternalInput").ap()
    x1_my = nc.dram_tensor("x1_my", [TOK, D], F32, kind="ExternalOutput").ap()
    n1_my = nc.dram_tensor("n1_my", [TOK, D], BF16, kind="ExternalOutput").ap()
    emit_proj_tm(nc, "p4", gn_my, wo_t, False, x1_my, res=x_my)
    emit_ln(nc, x1_my, n1_my, n_rows=TOK, tag="p4ln")
    return nc



def build_p0():
    nc = bass.Bass("TRN2", target_bir_lowering=False, debug=False)
    io = {
        "x_my": nc.dram_tensor("x_my", [TOK, D], F32, kind="ExternalInput").ap(),
        "pack_my": nc.dram_tensor("pack_my", [512, 512], I32, kind="ExternalInput").ap(),
        "wsc_my": nc.dram_tensor("wsc_my", [512, 32], F32, kind="ExternalInput").ap(),
        "n0_my": nc.dram_tensor("n0_my", [TOK, D], BF16, kind="ExternalOutput").ap(),
        "wdeq_my": nc.dram_tensor("wdeq_my", [512, D], BF16, kind="ExternalOutput").ap(),
    }
    emit_p0(nc, io)
    return nc



def emit_p6(nc, io):
    n2 = io["n2_full"]
    wqk = io["w1qk"]
    wv = io["w1v"]
    bqk = io["b1qk"]
    bv = io["b1v2"]
    qk1 = io["qk1"]
    v1 = io["v1fm"]
    NF = 48

    for th in (0, 1):
        with ExitStack() as st:
            def sb(name, shape, dt):
                return st.enter_context(nc.sbuf_tensor(f"p6{th}{name}", shape, dt))
            rhsf = [sb(f"r{i}", [128, 2048], BF16) for i in range(32)]
            wbuf = [sb(f"w{j}", [128, 4096], BF16) for j in range(2)]
            evb = [sb(f"e{j}", [128, 512], BF16) for j in range(8)]
            bcol = sb("bc", [128, 48], F32)
            ps = [st.enter_context(nc.psum_tensor(f"p6{th}ps{j}", [128, 512], F32))
                  for j in range(8)]
            sems = mk_sems(st, nc, [f"p6{th}{k}" for k in ["ldr", "wl", "wf", "ce", "se"]])
            ldr, wl, wf, ce, se = (sems[f"p6{th}{k}"] for k in ["ldr", "wl", "wf", "ce", "se"])
            blk = st.enter_context(nc.Block())
            tsl = slice(th * 2048, (th + 1) * 2048)

            @blk.sync
            def _(sy):
                sy.dma_start(out=bcol[:, 0:32], in_=bqk).then_inc(ldr.h, 16)
                sy.dma_start(out=bcol[:, 32:48], in_=bv).then_inc(ldr.h, 16)
                for i in range(32):
                    sy.dma_start(out=rhsf[i][:], in_=n2[tsl, i * 128:(i + 1) * 128],
                                 transpose=True).then_inc(ldr.h, 16)

            @blk.gpsimd
            def _(g):
                for f in range(NF):
                    src_ = wqk[:, f * 128:(f + 1) * 128] if f < 32 else \
                        wv[:, (f - 32) * 128:(f - 31) * 128]
                    if f >= 2:
                        g.wait_ge(wf.h, f - 1)
                    s3 = src_.rearrange("(i p) c -> p i c", p=128)
                    d3 = wbuf[f % 2][:].rearrange("p (i c) -> p i c", c=128)
                    for q in range(4):
                        g.dma_start(out=d3[:, q * 8:(q + 1) * 8, :],
                                    in_=s3[:, q * 8:(q + 1) * 8, :]).then_inc(wl.h, 16)

            @blk.tensor
            def _(t):
                t.wait_ge(ldr.h, 16 * 34)
                for f in range(NF):
                    t.wait_ge(wl.h, 64 * (f + 1))
                    if f >= 2:
                        t.wait_ge(ce.h, 4 * (f - 1))
                    for i in range(32):
                        lhsT = wbuf[f % 2][:, i * 128:(i + 1) * 128]
                        for tt in range(4):
                            ti = t.matmul(ps[(f % 2) * 4 + tt][:],
                                          lhsT, rhsf[i][:, tt * 512:(tt + 1) * 512],
                                          start=(i == 0), stop=(i == 31))
                    wf.inc(ti)

            @blk.scalar
            def _(s):
                for f in range(NF):
                    s.wait_ge(wf.h, f + 1)
                    if f >= 2:
                        s.wait_ge(se.h, 16 * 4 * (f - 1))
                    for tt in range(4):
                        si = s.activation(out=evb[(f % 2) * 4 + tt][:],
                                          in_=ps[(f % 2) * 4 + tt][:], func=AF.Identity,
                                          bias=bcol[:, f:f + 1])
                        ce.inc(si)
                    for tt in range(4):
                        if f < 32:
                            dst = qk1[f * 128:(f + 1) * 128,
                                      th * 2048 + tt * 512: th * 2048 + (tt + 1) * 512]
                        else:
                            dst = v1[(f - 32) * 128:(f - 31) * 128,
                                     th * 2048 + tt * 512: th * 2048 + (tt + 1) * 512]
                        s.dma_start(out=dst, in_=evb[(f % 2) * 4 + tt][:]).then_inc(se.h, 16)

            @blk.sync
            def _(sy):
                sy.wait_ge(se.h, 16 * 4 * NF)
    return nc



def _p7_windows():
    out = []
    for j in range(16):
        slope = float(_l1_slopes()[2 * j + 1])
        W = int(math.ceil(90.0 / slope))
        kept = {}
        for qt in range(8):
            nkc = 4 * (qt + 1)
            kept[qt] = [kc for kc in range(nkc) if (kc + 1) * 128 - 1 >= qt * 512 - W]
        out.append(kept)
    return out


P7_FAST0 = 9
P7_SKEW = 4


def _p7_tiles():
    wins = _p7_windows()
    ld = tm = dv = ca = se = 0
    ms = []
    dv += 1
    ld += 16
    for s_ in range(16):
        m = {"fast": s_ >= P7_FAST0}
        fast = m["fast"]
        ld += 16 * 36
        m["ld_slot"] = ld
        ca += 1; m["ca_ksq"] = ca
        m["tm_krow"] = []; m["ca_krow"] = []
        for j in range(8):
            tm += 1; m["tm_krow"].append(tm)
            ca += 1; m["ca_krow"].append(ca)
        ca += 1; m["ca_qsq"] = ca
        m["tm_qrow"] = []; m["ca_qrow"] = []
        for j in range(8):
            tm += 1; m["tm_qrow"].append(tm)
            ca += 1; m["ca_qrow"].append(ca)
        dv += 1; m["dv_km"] = dv
        ca += 1; m["ca_kms"] = ca
        dv += 1; m["dv_trow"] = dv
        ca += 1; m["ca_rq"] = ca
        if fast:
            dv += 1; m["dv_crow"] = dv
            tm += 1; m["tm_cb"] = tm
            ca += 1; m["ca_cb"] = ca
        tl = []
        m["qt"] = {}
        kept = wins[s_]
        gidx = 0
        for qt in range(8):
            q = {}
            m["qt"][qt] = q
            if fast:
                dv += 1; q["dv_badj"] = dv
            else:
                tm += 1; q["tm_rq"] = tm
                ca += 1; q["ca_rqb"] = ca
                dv += 1; q["dv_msk"] = dv
            kcs = kept[qt]
            nk = len(kcs)
            tiles = []
            for i2, kc in enumerate(kcs):
                t_ = {"qt": qt, "kc": kc, "i2": i2, "p": gidx % 4,
                      "diag": kc >= 4 * qt, "first": i2 == 0, "last": i2 == nk - 1}
                t_["slowpath"] = (not fast) or t_["diag"]
                tiles.append(t_)
                gidx += 1
            q["tiles"] = tiles
            for j in range(nk + P7_SKEW):
                if j < nk:
                    tm += 1; tiles[j]["tm_sc"] = tm
                jj = j - P7_SKEW
                if jj >= 0:
                    tm += 1; tiles[jj]["tm_cd"] = tm
            for t_ in tiles:
                if t_["slowpath"]:
                    dv += 1; t_["dv_tmp"] = dv
                ca += 1; t_["ca_exp"] = ca
            q["tm_den"] = tiles[-1]["tm_cd"]
            dv += 1; q["dv_rec"] = dv
            tm += 1; q["tm_rec"] = tm
            ca += 1; q["ca_recb"] = ca
            dv += 1; q["dv_ctxn"] = dv
            se += 16; q["se_ctx"] = se
            tl.extend(tiles)
        m["tl"] = tl
        m["tm_end"] = tm
        m["ca_end"] = ca
        m["dv_end"] = dv
        ms.append(m)
    return ms


def _psfree_wait(eng, m, t_):
    tl = m["tl"]
    idx = None
    for k, tt in enumerate(tl):
        if tt is t_:
            idx = k
            break
    if idx is None or idx < 4:
        return
    prev = tl[idx - 4]
    if prev["slowpath"]:
        eng.wait_ge.__self__


def emit_p7(nc, io):
    qk1 = io["qk1"]
    v1 = io["v1fm"]
    ctx1 = io["ctx1"]
    slkpT = io["sl_kpos"]
    slqpT = io["slqp"]
    dmaskT = io["dmask"]
    SCAL = float(HD) ** -0.5
    ms = _p7_tiles()

    with ExitStack() as st:
        def sb(name, shape, dt):
            return st.enter_context(nc.sbuf_tensor(f"p7{name}", shape, dt))
        qf = sb("qf", [128, S], BF16)
        kf = sb("kf", [128, S], BF16)
        vts = sb("vts", [128, S], BF16)
        sqb = sb("sqb", [128, S], BF16)
        slkp = sb("slkp", [128, 32], F32)
        slqp = sb("slqp", [1, S], F32)
        dmsk = sb("dmsk", [128, 2048], F32)
        kkrow = sb("kkrow", [1, S], F32)
        qrt = sb("qrt", [1, S], F32)
        km = sb("km", [1, 1], F32)
        kms = sb("kms", [1, 1], F32)
        trow = sb("trow", [1, S], F32)
        rqrow = sb("rqrow", [1, S], F32)
        crow = sb("crow", [1, 8], F32)
        cb = sb("cb", [128, 8], F32)
        badj = sb("badj", [128, 32], F32)
        rqall = sb("rqall", [128, 2560], F32)
        stf = [sb(f"stf{j}", [128, 512], F32) for j in range(4)]
        eb = [sb(f"eb{j}", [128, 512], BF16) for j in range(4)]
        recrow = sb("recrow", [1, 512], F32)
        recb = sb("recb", [128, 512], F32)
        ctxn = [sb(f"ctxn{j}", [128, 512], BF16) for j in range(2)]
        onesb = sb("onesb", [128, 1], BF16)
        onesf = sb("onesf", [1, 128], F32)
        ps_s = [st.enter_context(nc.psum_tensor(f"p7s{j}", [128, 512], F32)) for j in range(4)]
        ps_ctx = st.enter_context(nc.psum_tensor("p7ctx", [128, 512], F32))
        ps_den = st.enter_context(nc.psum_tensor("p7den", [1, 512], F32))
        ps_r = st.enter_context(nc.psum_tensor("p7r", [128, 512], F32))
        sems = mk_sems(st, nc, [f"p7{k}" for k in ["ld", "tm", "dv", "ca", "se"]])
        ld, tm, dv, ca, se = (sems[f"p7{k}"] for k in ["ld", "tm", "dv", "ca", "se"])
        blk = st.enter_context(nc.Block())

        def psfree(m, t_):
            tl = m["tl"]
            idx = tl.index(t_)
            if idx < 4:
                return None
            prev = tl[idx - 4]
            if prev["slowpath"]:
                return ("dv", prev["dv_tmp"])
            return ("ca", prev["ca_exp"])

        def ebfree(m, t_):
            tl = m["tl"]
            idx = tl.index(t_)
            if idx < 4:
                return None
            return ("tm", tl[idx - 4]["tm_cd"])

        @blk.sync
        def _(sy):
            sy.dma_start(out=dmsk[:].rearrange("p (a n) -> p a n", a=4),
                         in_=dmaskT.rearrange("a p n -> p a n")).then_inc(ld.h, 16)
            for s_ in range(16):
                if s_ >= 1:
                    sy.wait_ge(tm.h, ms[s_ - 1]["tm_end"])
                    sy.wait_ge(ca.h, ms[s_ - 1]["ca_end"])
                    sy.wait_ge(dv.h, ms[s_ - 1]["dv_end"])
                r0 = s_ * 256
                sy.dma_start(out=qf[:], in_=qk1[r0:r0 + 128, :]).then_inc(ld.h, 16)
                sy.dma_start(out=kf[:], in_=qk1[r0 + 128:r0 + 256, :]).then_inc(ld.h, 16)
                for kc in range(32):
                    sy.dma_start(out=vts[:, kc * 128:(kc + 1) * 128],
                                 in_=v1[s_ * 128:(s_ + 1) * 128, kc * 128:(kc + 1) * 128],
                                 transpose=True).then_inc(ld.h, 16)
                sy.dma_start(out=slkp[:], in_=slkpT[s_]).then_inc(ld.h, 16)
                sy.dma_start(out=slqp[:], in_=slqpT[s_:s_ + 1, :]).then_inc(ld.h, 16)

        @blk.vector
        def _(v):
            v.memset(onesb[:], 1.0)
            v.memset(onesf[:], 1.0)
            dinc(v, dv)
            for s_ in range(16):
                m = ms[s_]
                fast = m["fast"]
                v.wait_ge(ca.h, m["ca_krow"][7])
                v.reduce_max(out=km[:], in_=kkrow[:], axis=AX.X)
                dinc(v, dv)
                v.wait_ge(ca.h, m["ca_kms"])
                v.wait_ge(ca.h, m["ca_qrow"][7])
                v.scalar_tensor_tensor(out=trow[:], in0=qrt[:], scalar=kms[:],
                                       in1=slqp[:], op0=ALU.mult, op1=ALU.add)
                dinc(v, dv)
                if fast:
                    v.reduce_max(out=crow[:].rearrange("o (a b) -> o a b", b=1),
                                 in_=trow[:].rearrange("o (a b) -> o a b", b=512),
                                 axis=AX.X)
                    dinc(v, dv)
                for qt in range(8):
                    q = m["qt"][qt]
                    tiles = q["tiles"]
                    if fast:
                        v.wait_ge(ca.h, m["ca_cb"])
                        if qt >= 1:
                            v.wait_ge(ca.h, m["qt"][qt - 1]["tiles"][-1]["ca_exp"])
                        v.tensor_scalar(out=badj[:], in0=slkp[:],
                                        scalar1=cb[:, qt:qt + 1], scalar2=None,
                                        op0=ALU.subtract)
                        dinc(v, dv)
                    else:
                        v.wait_ge(ca.h, q["ca_rqb"])
                        for a in range(4):
                            v.tensor_tensor(out=rqall[:, (a + 1) * 512:(a + 2) * 512],
                                            in0=rqall[:, 0:512],
                                            in1=dmsk[:, a * 512:(a + 1) * 512], op=ALU.add)
                        dinc(v, dv)
                    for t_ in tiles:
                        if not t_["slowpath"]:
                            continue
                        v.wait_ge(tm.h, t_["tm_sc"])
                        pf = psfree(m, t_)
                        kc = t_["kc"]
                        if fast:
                            in1 = dmsk[:, (kc - 4 * qt) * 512:(kc - 4 * qt + 1) * 512]
                        else:
                            sel = 0 if kc < 4 * qt else 1 + (kc - 4 * qt)
                            in1 = rqall[:, sel * 512:(sel + 1) * 512]
                        ef = ebfree(m, t_)
                        if ef is not None:
                            pass
                        v.tensor_tensor(out=stf[t_["p"]][:], in0=ps_s[t_["p"]][:],
                                        in1=in1, op=ALU.add)
                        dinc(v, dv)
                    v.wait_ge(tm.h, q["tm_den"])
                    v.reciprocal(out=recrow[:], in_=ps_den[0:1, :])
                    dinc(v, dv)
                    v.wait_ge(ca.h, q["ca_recb"])
                    if qt >= 2:
                        v.wait_ge(se.h, m["qt"][qt - 2]["se_ctx"])
                    v.tensor_tensor(out=ctxn[qt % 2][:], in0=ps_ctx[:], in1=recb[:],
                                    op=ALU.mult)
                    dinc(v, dv)

        @blk.scalar
        def _(s):
            for s_ in range(16):
                m = ms[s_]
                fast = m["fast"]
                s.wait_ge(ld.h, m["ld_slot"])
                s.activation(out=sqb[:], in_=kf[:], func=AF.Square)
                dinc(s, ca)
                for j in range(8):
                    s.wait_ge(tm.h, m["tm_krow"][j])
                    s.activation(out=kkrow[:, j * 512:(j + 1) * 512], in_=ps_den[0:1, :],
                                 func=AF.Copy)
                    dinc(s, ca)
                s.activation(out=sqb[:], in_=qf[:], func=AF.Square)
                dinc(s, ca)
                for j in range(8):
                    s.wait_ge(tm.h, m["tm_qrow"][j])
                    s.activation(out=qrt[:, j * 512:(j + 1) * 512], in_=ps_den[0:1, :],
                                 func=AF.Sqrt)
                    dinc(s, ca)
                s.wait_ge(dv.h, m["dv_km"])
                s.activation(out=kms[:], in_=km[:], func=AF.Sqrt, scale=SCAL * SCAL)
                dinc(s, ca)
                s.wait_ge(dv.h, m["dv_trow"])
                s.activation(out=rqrow[:], in_=trow[:], func=AF.Copy, scale=-1.0 / SCAL)
                dinc(s, ca)
                if fast:
                    s.wait_ge(tm.h, m["tm_cb"])
                    s.activation(out=cb[:], in_=ps_r[:, 0:8], func=AF.Copy)
                    dinc(s, ca)
                for qt in range(8):
                    q = m["qt"][qt]
                    tiles = q["tiles"]
                    if fast:
                        s.wait_ge(dv.h, q["dv_badj"])
                    else:
                        s.wait_ge(tm.h, q["tm_rq"])
                        if qt >= 1:
                            prevq = m["qt"][qt - 1]
                            lastslow = [t2 for t2 in prevq["tiles"] if t2["slowpath"]]
                            if lastslow:
                                s.wait_ge(dv.h, lastslow[-1]["dv_tmp"])
                        s.activation(out=rqall[:, 0:512], in_=ps_r[:], func=AF.Copy)
                        dinc(s, ca)
                    for t_ in tiles:
                        kc = t_["kc"]
                        ef = ebfree(m, t_)
                        if ef is not None:
                            s.wait_ge(tm.h, ef[1])
                        if t_["slowpath"]:
                            s.wait_ge(dv.h, t_["dv_tmp"])
                            bias_ap = badj[:, kc:kc + 1] if fast else slkp[:, kc:kc + 1]
                            s.activation(out=eb[t_["p"]][:], in_=stf[t_["p"]][:],
                                         func=AF.Exp, scale=SCAL, bias=bias_ap)
                        else:
                            s.wait_ge(tm.h, t_["tm_sc"])
                            s.activation(out=eb[t_["p"]][:], in_=ps_s[t_["p"]][:],
                                         func=AF.Exp, scale=SCAL, bias=badj[:, kc:kc + 1])
                        dinc(s, ca)
                    s.wait_ge(tm.h, q["tm_rec"])
                    s.activation(out=recb[:], in_=ps_r[:], func=AF.Copy)
                    dinc(s, ca)

        @blk.tensor
        def _(t):
            t.wait_ge(dv.h, 1)
            for s_ in range(16):
                m = ms[s_]
                fast = m["fast"]
                t.wait_ge(ld.h, m["ld_slot"])
                t.wait_ge(ca.h, m["ca_ksq"])
                for j in range(8):
                    if j >= 1:
                        t.wait_ge(ca.h, m["ca_krow"][j - 1])
                    ti = t.matmul(ps_den[:], onesb[:], sqb[:, j * 512:(j + 1) * 512],
                                  start=True, stop=True)
                    tm.inc(ti)
                t.wait_ge(ca.h, m["ca_qsq"])
                for j in range(8):
                    t.wait_ge(ca.h, m["ca_qrow"][j - 1] if j >= 1 else m["ca_krow"][7])
                    ti = t.matmul(ps_den[:], onesb[:], sqb[:, j * 512:(j + 1) * 512],
                                  start=True, stop=True)
                    tm.inc(ti)
                t.wait_ge(ca.h, m["ca_rq"])
                if fast:
                    t.wait_ge(dv.h, m["dv_crow"])
                    ti = t.matmul(ps_r[:, 0:8], onesf[:], crow[:], start=True, stop=True)
                    tm.inc(ti)
                for qt in range(8):
                    q = m["qt"][qt]
                    tiles = q["tiles"]
                    nk = len(tiles)
                    if not fast:
                        if qt >= 1:
                            t.wait_ge(ca.h, m["qt"][qt - 1]["ca_recb"])
                        ti = t.matmul(ps_r[:], onesf[:], rqrow[:, qt * 512:(qt + 1) * 512],
                                      start=True, stop=True)
                        tm.inc(ti)
                    for j in range(nk + P7_SKEW):
                        if j < nk:
                            t_ = tiles[j]
                            pf = psfree(m, t_)
                            if pf is not None:
                                t.wait_ge(dv.h if pf[0] == "dv" else ca.h, pf[1])
                            kc = t_["kc"]
                            ti = t.matmul(ps_s[t_["p"]][:], kf[:, kc * 128:(kc + 1) * 128],
                                          qf[:, qt * 512:(qt + 1) * 512],
                                          start=True, stop=True)
                            tm.inc(ti)
                        jj = j - P7_SKEW
                        if jj >= 0:
                            t_ = tiles[jj]
                            t.wait_ge(ca.h, t_["ca_exp"])
                            if jj == 0 and qt >= 1:
                                t.wait_ge(dv.h, m["qt"][qt - 1]["dv_ctxn"])
                                t.wait_ge(dv.h, m["qt"][qt - 1]["dv_rec"])
                            kc = t_["kc"]
                            t.matmul(ps_ctx[:], vts[:, kc * 128:(kc + 1) * 128], eb[t_["p"]][:],
                                     start=t_["first"], stop=t_["last"])
                            ti = t.matmul(ps_den[:], onesb[:], eb[t_["p"]][:],
                                          start=t_["first"], stop=t_["last"])
                            tm.inc(ti)
                    t.wait_ge(dv.h, q["dv_rec"])
                    ti = t.matmul(ps_r[:], onesf[:], recrow[:], start=True, stop=True)
                    tm.inc(ti)

        @blk.gpsimd
        def _(g):
            for s_ in range(16):
                m = ms[s_]
                for qt in range(8):
                    g.wait_ge(dv.h, m["qt"][qt]["dv_ctxn"])
                    g.dma_start(out=ctx1[s_ * 128:(s_ + 1) * 128, qt * 512:(qt + 1) * 512],
                                in_=ctxn[qt % 2][:]).then_inc(se.h, 16)

        @blk.sync
        def _(sy):
            sy.wait_ge(se.h, 16 * 8 * 16)
    return nc


def build_p7():
    nc = bass.Bass("TRN2", target_bir_lowering=False, debug=False)
    io = {
        "qk1": nc.dram_tensor("qk1", [4096, S], BF16, kind="ExternalInput").ap(),
        "v1fm": nc.dram_tensor("v1fm", [2048, S], BF16, kind="ExternalInput").ap(),
        "sl_kpos": nc.dram_tensor("sl_kpos", [16, 128, 32], F32, kind="ExternalInput").ap(),
        "slqp": nc.dram_tensor("slqp", [16, S], F32, kind="ExternalInput").ap(),
        "dmask": nc.dram_tensor("dmask", [4, 128, 512], F32, kind="ExternalInput").ap(),
        "ctx1": nc.dram_tensor("ctx1", [2048, S], BF16, kind="ExternalOutput").ap(),
    }
    emit_p7(nc, io)
    return nc



def emit_reshard(nc, tag, copies, off_ap):
    with ExitStack() as st:
        osb = st.enter_context(nc.sbuf_tensor(f"rs{tag}o", [1, 1], I32))
        sem = st.enter_context(nc.semaphore(f"rs{tag}s"))
        blk = st.enter_context(nc.Block())

        @blk.gpsimd
        def _(g):
            g.dma_start(out=osb[:], in_=off_ap).then_inc(sem, 16)
            g.wait_ge(sem, 16)
            rb = nc.gpsimd.alloc_register(f"rs{tag}r")
            r2 = nc.gpsimd.alloc_register(f"rs{tag}r2")
            g.reg_load(rb, osb[0:1, 0:1])
            n = 16
            for dst, srct, pat, soff in copies:
                g.reg_add(r2, rb, soff)
                g.dma_start(out=dst, in_=bass.AP(srct, r2, pat)).then_inc(sem, 16)
                n += 16
            g.wait_ge(sem, n)
    return nc


def emit_coll(nc, tag, kind, op, groups, in_ap, out_ap):
    with ExitStack() as st:
        sem = st.enter_context(nc.semaphore(f"cc{tag}"))
        blk = st.enter_context(nc.Block())

        @blk.gpsimd
        def _(g):
            g.collective_compute(kind, op, replica_groups=groups,
                                 ins=[in_ap.opt()], outs=[out_ap.opt()]).then_inc(sem)
            g.wait_ge(sem, 1)
    return nc


def emit_p9z(nc, io):
    n3 = io["n3_my"]
    W1T = io["W1T"]
    b1 = io["b1"]
    A1 = io["A1"]
    LiT = io["LiMEsT"]
    identf = io["identf"]
    z_out = io["z_my"]
    h8_out = io["h8_my"]
    ds_out = io["ds_my"]
    mx_out = io["mx_my"]

    with ExitStack() as st:
        def sb(name, shape, dt):
            return st.enter_context(nc.sbuf_tensor(f"p9z{name}", shape, dt))
        n3f = [sb(f"n{i}", [128, 2048], BF16) for i in range(32)]
        a1sb = sb("a1", [128, 1024], BF16)
        w8all = sb("w8", [128, 256], BF16)
        litb = sb("lit", [32, 8], BF16)
        b8row = sb("b8", [1, 8], BF16)
        ones1 = sb("one", [1, 128], BF16)
        idf = sb("idf", [128, 128], F32)
        zfm = sb("zfm", [32, 2048], F32)
        zbf = sb("zbf", [32, 2048], BF16)
        h8s = sb("h8s", [128, 64], F32)
        dss = sb("dss", [128, 64], F32)
        mhd = sb("mhd", [128, 2], F32)
        mx2 = sb("mx2", [2, 1], F32)
        psz = st.enter_context(nc.psum_tensor("p9zz", [32, 512], F32))
        ps8 = st.enter_context(nc.psum_tensor("p9z8", [128, 8], F32))
        pst = st.enter_context(nc.psum_tensor("p9zt", [2, 128], F32))
        sems = mk_sems(st, nc, [f"p9z{k}" for k in ["ld", "tm", "dv", "ca", "se"]])
        ld, tm, dv, ca, se = (sems[f"p9z{k}"] for k in ["ld", "tm", "dv", "ca", "se"])
        blk = st.enter_context(nc.Block())

        @blk.sync
        def _(sy):
            for i in range(32):
                sy.dma_start(out=n3f[i][:], in_=n3[:, i * 128:(i + 1) * 128],
                             transpose=True).then_inc(ld.h, 16)
            sy.dma_start(out=idf[:], in_=identf).then_inc(ld.h, 16)

        @blk.gpsimd
        def _(g):
            for i in range(32):
                g.dma_start(out=a1sb[:, i * 32:(i + 1) * 32],
                            in_=A1[i * 128:(i + 1) * 128, :]).then_inc(ld.h, 16)
                g.dma_start(out=w8all[:, i * 8:(i + 1) * 8],
                            in_=W1T[i * 128:(i + 1) * 128, 0:8]).then_inc(ld.h, 16)
            g.dma_start(out=litb[:], in_=LiT).then_inc(ld.h, 16)
            g.dma_start(out=b8row[:], in_=b1[:, 0:8]).then_inc(ld.h, 16)
            g.wait_ge(ca.h, 8)
            g.dma_start(out=z_out, in_=zfm[:]).then_inc(se.h, 16)
            g.wait_ge(ca.h, 16)
            g.dma_start(out=h8_out, in_=h8s[:]).then_inc(se.h, 16)
            g.wait_ge(ca.h, 24)
            g.dma_start(out=ds_out, in_=dss[:]).then_inc(se.h, 16)
            g.wait_ge(dv.h, 3)
            g.dma_start(out=mx_out, in_=mx2[:]).then_inc(se.h, 16)

        @blk.vector
        def _(v):
            v.memset(ones1[:], 1.0)
            dinc(v, dv)
            v.wait_ge(ca.h, 24)
            v.reduce_max(out=mhd[:, 0:1], in_=h8s[:], axis=AX.X, apply_absolute_value=True)
            v.reduce_max(out=mhd[:, 1:2], in_=dss[:], axis=AX.X, apply_absolute_value=True)
            dinc(v, dv)
            v.wait_ge(tm.h, 21)
            v.reduce_max(out=mx2[:], in_=pst[:], axis=AX.X)
            dinc(v, dv)

        @blk.tensor
        def _(t):
            t.wait_ge(ld.h, 16 * 99)
            t.wait_ge(dv.h, 1)
            for t4 in range(4):
                if t4 >= 1:
                    t.wait_ge(ca.h, 2 * t4)
                for i in range(32):
                    ti = t.matmul(psz[:], a1sb[:, i * 32:(i + 1) * 32],
                                  n3f[i][:, t4 * 512:(t4 + 1) * 512],
                                  start=(i == 0), stop=(i == 31))
                tm.inc(ti)
            for ac in range(8):
                if ac >= 1:
                    t.wait_ge(ca.h, 8 + ac)
                for i in range(32):
                    t.matmul(ps8[:], n3f[i][:, 1 + ac * 256:ac * 256 + 256:2],
                             w8all[:, i * 8:(i + 1) * 8], start=(i == 0), stop=False)
                ti = t.matmul(ps8[:], ones1[:], b8row[:], start=False, stop=True)
                tm.inc(ti)
            for ac in range(8):
                t.wait_ge(ca.h, 16 + ac)
                ti = t.matmul(ps8[:], zbf[:, 1 + ac * 256:ac * 256 + 256:2],
                              litb[:], start=True, stop=True)
                tm.inc(ti)
            t.wait_ge(dv.h, 2)
            ti = t.matmul(pst[:], mhd[:], idf[:], start=True, stop=True)
            tm.inc(ti)

        @blk.scalar
        def _(s):
            for t4 in range(4):
                s.wait_ge(tm.h, t4 + 1)
                s.activation(out=zfm[:, t4 * 512:(t4 + 1) * 512], in_=psz[:], func=AF.Copy)
                dinc(s, ca)
                s.activation(out=zbf[:, t4 * 512:(t4 + 1) * 512], in_=psz[:], func=AF.Copy)
                dinc(s, ca)
            for ac in range(8):
                s.wait_ge(tm.h, 5 + ac)
                s.activation(out=h8s[:, ac * 8:(ac + 1) * 8], in_=ps8[:], func=AF.Copy)
                dinc(s, ca)
            for ac in range(8):
                s.wait_ge(tm.h, 13 + ac)
                s.activation(out=dss[:, ac * 8:(ac + 1) * 8], in_=ps8[:], func=AF.Copy)
                dinc(s, ca)

        @blk.sync
        def _(sy):
            sy.wait_ge(se.h, 64)
    return nc


def emit_p9b(nc, io, mx_pre=False):
    mxall = io["mx_all"]
    h8 = io["h8_my"]
    dsi = io["ds_my"]
    z_in = io["z_my"]
    limes = io["LiMEs"]
    bmat = io["Bmat"]
    x3p = io["x3p_my"]
    identb = io["ident"]
    x4 = io["x4_my"]
    GSC = 0.5

    with ExitStack() as st:
        def sb(name, shape, dt):
            return st.enter_context(nc.sbuf_tensor(f"p9b{name}", shape, dt))
        mxs = sb("mx", [1, 16], F32)
        h8s = sb("h8", [128, 64], F32)
        dss = sb("ds", [128, 64], F32)
        zfm = sb("z", [32, 2048], F32)
        limb = sb("lim", [8, 32], BF16)
        bmb = sb("bm", [32, 4096], BF16)
        idb = sb("idb", [128, 128], BF16)
        onesf = sb("onef", [1, 128], F32)
        mx2 = sb("m2", [1, 2], F32)
        cp1 = sb("cp1", [1, 2], F32)
        mx2c = sb("m2c", [1, 2], F32)
        cp2 = sb("cp2", [1, 2], F32)
        rcp = sb("rcp", [1, 2], F32)
        sc2 = sb("sc2", [1, 2], F32)
        scb = sb("scb", [128, 2], F32)
        ds2 = sb("ds2", [128, 64], F32)
        lg = sb("lg", [128, 64], F32)
        elg = sb("elg", [128, 64], F32)
        sm8 = sb("sm8", [128, 8], F32)
        cp8 = sb("cp8", [128, 8], F32)
        rp8 = sb("rp8", [128, 8], F32)
        probs = sb("pr", [128, 64], F32)
        m1 = sb("m1", [128, 8], F32)
        eqb = sb("eq", [128, 64], F32)
        cnt = sb("cnt", [128, 8], F32)
        mskd = sb("mskd", [128, 64], F32)
        m2b = sb("m2b", [128, 8], F32)
        d12 = sb("d12", [128, 8], F32)
        cond = sb("cond", [128, 8], F32)
        t1b = sb("t1b", [128, 8], F32)
        thr = sb("thr", [128, 8], F32)
        pdf = sb("pd", [128, 64], F32)
        sgm = sb("sg", [128, 64], F32)
        wu = sb("wu", [128, 64], F32)
        ws = sb("ws", [128, 8], F32)
        wsp = sb("wsp", [128, 8], F32)
        cpw = sb("cpw", [128, 8], F32)
        rw = sb("rw", [128, 8], F32)
        wgt = sb("wgt", [128, 64], BF16)
        wfm = sb("wfm", [8, 1024], BF16)
        pm = sb("pm", [32, 1024], F32)
        pf = sb("pf", [32, 2048], F32)
        zpb = sb("zpb", [32, 2048], BF16)
        x3t = [sb(f"x{j}", [128, 4096], F32) for j in range(2)]
        x4t = [sb(f"y{j}", [128, 4096], F32) for j in range(2)]
        psc = st.enter_context(nc.psum_tensor("p9bc", [128, 2], F32))
        pwt = st.enter_context(nc.psum_tensor("p9bw", [8, 128], BF16))
        ppm = st.enter_context(nc.psum_tensor("p9bp", [32, 512], F32))
        psd = [st.enter_context(nc.psum_tensor(f"p9bd{j}", [128, 512], F32)) for j in range(4)]
        sems = mk_sems(st, nc, [f"p9b{k}" for k in ["ld", "tm", "dv", "ca", "se", "lx"]])
        ld, tm, dv, ca, se, lx = (sems[f"p9b{k}"]
                                  for k in ["ld", "tm", "dv", "ca", "se", "lx"])
        blk = st.enter_context(nc.Block())

        def b3(ap):
            return ap.rearrange("p (g o) -> p g o", o=1).broadcast_to([128, 8, 8])

        def g3(ap):
            return ap.rearrange("p (g e) -> p g e", e=8)

        @blk.sync
        def _(sy):
            sy.dma_start(out=mxs[:, 0:2] if mx_pre else mxs[:], in_=mxall).then_inc(ld.h, 16)
            sy.dma_start(out=h8s[:], in_=h8).then_inc(ld.h, 16)
            sy.dma_start(out=dss[:], in_=dsi).then_inc(ld.h, 16)
            sy.dma_start(out=zfm[:], in_=z_in).then_inc(ld.h, 16)
            sy.dma_start(out=idb[:], in_=identb).then_inc(ld.h, 16)
            for tc in range(16):
                if tc >= 2:
                    sy.wait_ge(dv.h, 17 + 2 * (tc - 2) + 2)
                sy.dma_start(out=x3t[tc % 2][:],
                             in_=x3p[tc * 128:(tc + 1) * 128, :]).then_inc(lx.h, 16)

        @blk.gpsimd
        def _(g):
            g.dma_start(out=limb[:], in_=limes).then_inc(ld.h, 16)
            g.dma_start(out=bmb[:], in_=bmat).then_inc(ld.h, 16)
            for tc in range(16):
                g.wait_ge(dv.h, 17 + 2 * tc + 2)
                g.dma_start(out=x4[tc * 128:(tc + 1) * 128, :],
                            in_=x4t[tc % 2][:]).then_inc(se.h, 16)

        @blk.vector
        def _(v):
            v.memset(onesf[:], 1.0)
            v.wait_ge(ld.h, 16 * 7)
            if mx_pre:
                v.tensor_copy(out=mx2[:], in_=mxs[:, 0:2])
            else:
                v.reduce_max(out=mx2[:].rearrange("o (g e) -> o g e", e=1),
                             in_=mxs[:].rearrange("o (g e) -> o g e", e=8), axis=AX.X)
            dinc(v, dv)
            v.wait_ge(ca.h, 1)
            v.tensor_scalar_max(out=mx2c[:], in0=cp1[:], scalar1=1e-6)
            dinc(v, dv)
            v.wait_ge(ca.h, 2)
            v.reciprocal(out=rcp[:], in_=cp2[:])
            dinc(v, dv)
            v.wait_ge(ca.h, 4)
            v.tensor_scalar(out=ds2[:], in0=dss[:], scalar1=scb[:, 1:2], scalar2=None,
                            op0=ALU.mult)
            v.drain()
            v.scalar_tensor_tensor(out=lg[:], in0=h8s[:], scalar=scb[:, 0:1], in1=ds2[:],
                                   op0=ALU.mult, op1=ALU.add)
            dinc(v, dv)
            v.wait_ge(ca.h, 5)
            v.reduce_sum(out=sm8[:].rearrange("p (g o) -> p g o", o=1), in_=g3(elg[:]),
                         axis=AX.X)
            dinc(v, dv)
            v.wait_ge(ca.h, 6)
            v.reciprocal(out=rp8[:], in_=cp8[:])
            v.drain()
            v.tensor_tensor(out=g3(probs[:]), in0=g3(elg[:]), in1=b3(rp8[:]), op=ALU.mult)
            v.drain()
            v.reduce_max(out=m1[:].rearrange("p (g o) -> p g o", o=1), in_=g3(probs[:]),
                         axis=AX.X)
            v.drain()
            v.tensor_tensor(out=g3(eqb[:]), in0=g3(probs[:]), in1=b3(m1[:]), op=ALU.is_ge)
            v.drain()
            v.reduce_sum(out=cnt[:].rearrange("p (g o) -> p g o", o=1), in_=g3(eqb[:]),
                         axis=AX.X)
            v.drain()
            v.scalar_tensor_tensor(out=mskd[:], in0=eqb[:], scalar=-10.0, in1=probs[:],
                                   op0=ALU.mult, op1=ALU.add)
            v.drain()
            v.reduce_max(out=m2b[:].rearrange("p (g o) -> p g o", o=1), in_=g3(mskd[:]),
                         axis=AX.X)
            v.drain()
            v.tensor_sub(out=d12[:], in0=m1[:], in1=m2b[:])
            v.drain()
            v.tensor_scalar(out=cond[:], in0=cnt[:], scalar1=1.5, scalar2=None, op0=ALU.is_gt)
            v.drain()
            v.tensor_mul(out=t1b[:], in0=d12[:], in1=cond[:])
            v.drain()
            v.tensor_add(out=thr[:], in0=t1b[:], in1=m2b[:])
            v.drain()
            v.tensor_tensor(out=g3(pdf[:]), in0=g3(probs[:]), in1=b3(thr[:]), op=ALU.subtract)
            dinc(v, dv)
            v.wait_ge(ca.h, 7)
            v.tensor_mul(out=wu[:], in0=probs[:], in1=sgm[:])
            v.drain()
            v.reduce_sum(out=ws[:].rearrange("p (g o) -> p g o", o=1), in_=g3(wu[:]),
                         axis=AX.X)
            v.drain()
            v.tensor_scalar_add(out=wsp[:], in0=ws[:], scalar1=1e-9)
            dinc(v, dv)
            v.wait_ge(ca.h, 8)
            v.reciprocal(out=rw[:], in_=cpw[:])
            v.drain()
            v.tensor_tensor(out=g3(wgt[:]), in0=g3(wu[:]), in1=b3(rw[:]), op=ALU.mult)
            dinc(v, dv)
            for ac in range(8):
                v.wait_ge(tm.h, 2 + ac)
                v.tensor_copy(out=wfm[:, ac * 128:(ac + 1) * 128], in_=pwt[:])
                dinc(v, dv)
            v.wait_ge(ca.h, 10)
            for par in range(2):
                v.tensor_copy(out=pf[:, par:2048:2], in_=pm[:])
            v.drain()
            v.scalar_tensor_tensor(out=zpb[:], in0=zfm[:], scalar=GSC, in1=pf[:],
                                   op0=ALU.mult, op1=ALU.mult)
            dinc(v, dv)
            for tc in range(16):
                v.wait_ge(lx.h, 16 * (tc + 1))
                if tc >= 2:
                    v.wait_ge(se.h, 16 * (tc - 1))
                for g2 in range(2):
                    v.wait_ge(tm.h, 11 + 2 * tc + g2 + 1)
                    for fo in range(4):
                        f = g2 * 4 + fo
                        v.tensor_tensor(out=x4t[tc % 2][:, f * 512:(f + 1) * 512],
                                        in0=psd[fo][:],
                                        in1=x3t[tc % 2][:, f * 512:(f + 1) * 512],
                                        op=ALU.add)
                    dinc(v, dv)

        @blk.scalar
        def _(s):
            s.wait_ge(dv.h, 1)
            s.activation(out=cp1[:], in_=mx2[:], func=AF.Copy)
            dinc(s, ca)
            s.wait_ge(dv.h, 2)
            s.activation(out=cp2[:], in_=mx2c[:], func=AF.Copy)
            dinc(s, ca)
            s.wait_ge(dv.h, 3)
            s.activation(out=sc2[:], in_=rcp[:], func=AF.Copy, scale=GSC)
            dinc(s, ca)
            s.wait_ge(tm.h, 1)
            s.activation(out=scb[:], in_=psc[:], func=AF.Copy)
            dinc(s, ca)
            s.wait_ge(dv.h, 4)
            s.activation(out=elg[:], in_=lg[:], func=AF.Exp)
            dinc(s, ca)
            s.wait_ge(dv.h, 5)
            s.activation(out=cp8[:], in_=sm8[:], func=AF.Copy)
            dinc(s, ca)
            s.wait_ge(dv.h, 6)
            s.activation(out=sgm[:], in_=pdf[:], func=AF.Sigmoid, scale=2.0)
            dinc(s, ca)
            s.wait_ge(dv.h, 7)
            s.activation(out=cpw[:], in_=wsp[:], func=AF.Copy)
            dinc(s, ca)
            for j in range(2):
                s.wait_ge(tm.h, 10 + j)
                s.activation(out=pm[:, j * 512:(j + 1) * 512], in_=ppm[:], func=AF.Copy)
                dinc(s, ca)

        @blk.tensor
        def _(t):
            t.wait_ge(ld.h, 16 * 7)
            t.wait_ge(ca.h, 3)
            ti = t.matmul(psc[:], onesf[:], sc2[:], start=True, stop=True)
            tm.inc(ti)
            t.wait_ge(dv.h, 8)
            for ac in range(8):
                if ac >= 1:
                    t.wait_ge(dv.h, 8 + ac)
                ti = t.transpose(pwt[:], wgt[:, ac * 8:(ac + 1) * 8], idb[:])
                tm.inc(ti)
            t.wait_ge(dv.h, 16)
            for j in range(2):
                if j >= 1:
                    t.wait_ge(ca.h, 9)
                ti = t.matmul(ppm[:], limb[:], wfm[:, j * 512:(j + 1) * 512],
                              start=True, stop=True)
                tm.inc(ti)
            t.wait_ge(dv.h, 17)
            for tc in range(16):
                for g2 in range(2):
                    if not (tc == 0 and g2 == 0):
                        t.wait_ge(dv.h, 17 + 2 * tc + g2)
                    for fo in range(4):
                        f = g2 * 4 + fo
                        ti = t.matmul(psd[fo][:], zpb[:, tc * 128:(tc + 1) * 128],
                                      bmb[:, f * 512:(f + 1) * 512], start=True, stop=True)
                    tm.inc(ti)

        @blk.sync
        def _(sy):
            sy.wait_ge(se.h, 16 * 16)
    return nc


def build_p8():
    nc = bass.Bass("TRN2", target_bir_lowering=False, debug=False)
    ctx_my = nc.dram_tensor("ctx_my", [TOK, D], BF16, kind="ExternalInput").ap()
    w1o = nc.dram_tensor("w1o", [D, D], F32, kind="ExternalInput").ap()
    b1o = nc.dram_tensor("b1o", [1, D], F32, kind="ExternalInput").ap()
    x2_my = nc.dram_tensor("x2_my", [TOK, D], F32, kind="ExternalInput").ap()
    x3_my = nc.dram_tensor("x3_my", [TOK, D], F32, kind="ExternalOutput").ap()
    n3_my = nc.dram_tensor("n3_my", [TOK, D], BF16, kind="ExternalOutput").ap()
    emit_proj_tm(nc, "p8", ctx_my, w1o, False, x3_my, res=x2_my, bias=b1o)
    emit_ln(nc, x3_my, n3_my, n_rows=TOK, tag="p8ln")
    return nc


def build_p9a():
    nc = bass.Bass("TRN2", target_bir_lowering=False, debug=False)
    io = {
        "n3_my": nc.dram_tensor("n3_my", [TOK, D], BF16, kind="ExternalInput").ap(),
        "W1T": nc.dram_tensor("W1T", [D, D], F32, kind="ExternalInput").ap(),
        "b1": nc.dram_tensor("b1", [1, D], F32, kind="ExternalInput").ap(),
        "A1": nc.dram_tensor("A1", [D, 32], F32, kind="ExternalInput").ap(),
        "LiMEsT": nc.dram_tensor("LiMEsT", [32, 8], F32, kind="ExternalInput").ap(),
        "identf": nc.dram_tensor("identf", [128, 128], F32, kind="ExternalInput").ap(),
        "x3_my": nc.dram_tensor("x3_my", [TOK, D], F32, kind="ExternalInput").ap(),
        "x3p_my": nc.dram_tensor("x3p_my", [TOK, D], F32, kind="ExternalOutput").ap(),
        "z_my": nc.dram_tensor("z_my", [32, TOK], F32, kind="ExternalOutput").ap(),
        "h8_my": nc.dram_tensor("h8_my", [128, 64], F32, kind="ExternalOutput").ap(),
        "ds_my": nc.dram_tensor("ds_my", [128, 64], F32, kind="ExternalOutput").ap(),
        "mx_my": nc.dram_tensor("mx_my", [2, 1], F32, kind="ExternalOutput").ap(),
    }
    emit_proj_tm(nc, "p9h", io["n3_my"], io["W1T"], False, io["x3p_my"],
                 res=io["x3_my"], bias=io["b1"])
    emit_p9z(nc, io)
    return nc


def build_p9b():
    nc = bass.Bass("TRN2", target_bir_lowering=False, debug=False)
    io = {
        "mx_all": nc.dram_tensor("mx_all", [1, 16], F32, kind="ExternalInput").ap(),
        "h8_my": nc.dram_tensor("h8_my", [128, 64], F32, kind="ExternalInput").ap(),
        "ds_my": nc.dram_tensor("ds_my", [128, 64], F32, kind="ExternalInput").ap(),
        "z_my": nc.dram_tensor("z_my", [32, TOK], F32, kind="ExternalInput").ap(),
        "LiMEs": nc.dram_tensor("LiMEs", [8, 32], F32, kind="ExternalInput").ap(),
        "Bmat": nc.dram_tensor("Bmat", [32, D], F32, kind="ExternalInput").ap(),
        "x3p_my": nc.dram_tensor("x3p_my", [TOK, D], F32, kind="ExternalInput").ap(),
        "ident": nc.dram_tensor("ident", [128, 128], BF16, kind="ExternalInput").ap(),
        "x4_my": nc.dram_tensor("x4_my", [TOK, D], F32, kind="ExternalOutput").ap(),
    }
    emit_p9b(nc, io)
    return nc


def build_p6():
    nc = bass.Bass("TRN2", target_bir_lowering=False, debug=False)
    io = {
        "n2_full": nc.dram_tensor("n2_full", [S, D], BF16, kind="ExternalInput").ap(),
        "w1qk": nc.dram_tensor("w1qk", [D, 4096], F32, kind="ExternalInput").ap(),
        "w1v": nc.dram_tensor("w1v", [D, 2048], F32, kind="ExternalInput").ap(),
        "b1qk": nc.dram_tensor("b1qk", [128, 32], F32, kind="ExternalInput").ap(),
        "b1v2": nc.dram_tensor("b1v2", [128, 16], F32, kind="ExternalInput").ap(),
        "qk1": nc.dram_tensor("qk1", [4096, S], BF16, kind="ExternalOutput").ap(),
        "v1fm": nc.dram_tensor("v1fm", [2048, S], BF16, kind="ExternalOutput").ap(),
    }
    emit_p6(nc, io)
    return nc


def build_p5():
    nc = bass.Bass("TRN2", target_bir_lowering=False, debug=False)
    n1_my = nc.dram_tensor("n1_my", [TOK, D], BF16, kind="ExternalInput").ap()
    wdeq = nc.dram_tensor("wdeq", [D, D], BF16, kind="ExternalInput").ap()
    fbias = nc.dram_tensor("fbias", [1, D], F32, kind="ExternalInput").ap()
    x1_my = nc.dram_tensor("x1_my", [TOK, D], F32, kind="ExternalInput").ap()
    x2_my = nc.dram_tensor("x2_my", [TOK, D], F32, kind="ExternalOutput").ap()
    n2_my = nc.dram_tensor("n2_my", [TOK, D], BF16, kind="ExternalOutput").ap()
    emit_proj_tm(nc, "p5", n1_my, wdeq, True, x2_my, res=x1_my, bias=fbias)
    emit_ln(nc, x2_my, n2_my, n_rows=TOK, tag="p5ln")
    return nc



def build_fused():
    nc = bass.Bass("TRN2", target_bir_lowering=False, debug=False)

    def inp(name, shape, dt):
        return nc.dram_tensor(name, shape, dt, kind="ExternalInput").ap()

    def mid(name, shape, dt):
        return nc.dram_tensor(name, shape, dt, kind="Internal").ap()

    io = {}
    io["x_full"] = inp("x_full", [S, D], F32)
    io["x_my"] = inp("x_my", [TOK, D], F32)
    io["pack_my"] = inp("pack_my", [512, 512], I32)
    io["wsc_my"] = inp("wsc_my", [512, 32], F32)
    io["wqk_t"] = inp("wqk_t", [D, 4096], F32)
    io["wv_t"] = inp("wv_t", [D, 2048], F32)
    io["wg_t"] = inp("wg_t", [D, 2048], F32)
    io["nw_bc"] = inp("nw_bc", [128, 2048], F32)
    io["wo_t"] = inp("wo_t", [D, D], F32)
    io["fbias"] = inp("fbias", [1, D], F32)
    io["w1qk"] = inp("w1qk", [D, 4096], F32)
    io["w1v"] = inp("w1v", [D, 2048], F32)
    io["b1qk"] = inp("b1qk", [128, 32], F32)
    io["b1v2"] = inp("b1v2", [128, 16], F32)
    io["sl_kpos"] = inp("sl_kpos", [16, 128, 32], F32)
    io["slqp"] = inp("slqp", [16, S], F32)
    io["dmask"] = inp("dmask", [4, 128, 512], F32)
    io["w1o"] = inp("w1o", [D, D], F32)
    io["b1o"] = inp("b1o", [1, D], F32)
    io["W1T"] = inp("W1T", [D, D], F32)
    io["b1"] = inp("b1", [1, D], F32)
    io["A1"] = inp("A1", [D, 32], F32)
    io["LiMEsT"] = inp("LiMEsT", [32, 8], F32)
    io["LiMEs"] = inp("LiMEs", [8, 32], F32)
    io["Bmat"] = inp("Bmat", [32, D], F32)
    io["identf"] = inp("identf", [128, 128], F32)
    io["ident"] = inp("ident", [128, 128], BF16)
    io["qd_bc"] = inp("qd_bc", [16, 128, 256], F32)
    io["kd"] = inp("kd", [16, 128, 2], F32)
    io["bd"] = inp("bd", [16, 128, 1], F32)
    io["ddT"] = inp("ddT", [16, 256, 256], F32)
    io["off_p4"] = inp("off_p4", [1, 1], I32)
    io["off_p8"] = inp("off_p8", [1, 1], I32)

    io["n0_full"] = mid("n0_full", [S, D], BF16)
    io["wdeq_my"] = mid("wdeq_my", [512, D], BF16)
    wdeq_ag = mid("wdeq_ag", [D, D], BF16)
    io["qk0"] = mid("qk0", [4096, S], BF16)
    io["v0fm"] = mid("v0fm", [2048, S], BF16)
    io["g0fm"] = mid("g0fm", [2048, S], BF16)
    io["attn0"] = mid("attn0", [S, 2048], BF16)
    io["ss_my"] = mid("ss_my", [128, 32], F32)
    io["ss_ag"] = mid("ss_ag", [256, 32], F32)
    io["gn0"] = mid("gn0", [S, 2048], BF16)
    gn_ag = mid("gn_ag", [2 * S, 2048], BF16)
    x1 = mid("x1", [TOK, D], F32)
    n1 = mid("n1", [TOK, D], BF16)
    x2 = mid("x2", [TOK, D], F32)
    n2 = mid("n2", [TOK, D], BF16)
    io["n2_full"] = mid("n2_ag", [S, D], BF16)
    io["qk1"] = mid("qk1", [4096, S], BF16)
    io["v1fm"] = mid("v1fm", [2048, S], BF16)
    io["ctx1"] = mid("ctx1", [2048, S], BF16)
    ctx_ag = mid("ctx_ag", [D, S], BF16)
    x3 = mid("x3", [TOK, D], F32)
    io["n3_my"] = mid("n3", [TOK, D], BF16)
    io["x3p_my"] = mid("x3p", [TOK, D], F32)
    io["z_my"] = mid("z_my", [32, TOK], F32)
    io["h8_my"] = mid("h8_my", [128, 64], F32)
    io["ds_my"] = mid("ds_my", [128, 64], F32)
    io["mx_my"] = mid("mx_my", [2, 1], F32)
    mx_g = mid("mx_g", [2, 1], F32)
    io["x3_my"] = x3
    io["x4_my"] = nc.dram_tensor("x4_my", [TOK, D], F32, kind="ExternalOutput").ap()

    pairs = [[0, 1], [2, 3], [4, 5], [6, 7]]
    all8 = [[0, 1, 2, 3, 4, 5, 6, 7]]

    emit_ln(nc, io["x_full"], io["n0_full"], n_rows=S, tag="p0")
    emit_dequant(nc, io)
    emit_p1(nc, io)
    emit_p2(nc, io)
    emit_coll(nc, "ss", "AllGather", ALU.bypass, pairs, io["ss_my"], io["ss_ag"])
    emit_p3(nc, io, ss_ag=True)
    emit_coll(nc, "gn", "AllGather", ALU.bypass, pairs, io["gn0"], gn_ag)
    gn_loc = mid("gn_loc", [TOK, D], BF16)
    emit_reshard(nc, "gn", [
        (gn_loc[:, 0:2048], gn_ag.tensor, [[2048, 2048], [1, 2048]], 0),
        (gn_loc[:, 2048:4096], gn_ag.tensor, [[2048, 2048], [1, 2048]], S * 2048),
    ], io["off_p4"])
    emit_proj_tm(nc, "p4", gn_loc, io["wo_t"], False, x1, res=io["x_my"])
    emit_ln(nc, x1, n1, n_rows=TOK, tag="p4l")
    emit_coll(nc, "wq", "AllGather", ALU.bypass, all8, io["wdeq_my"], wdeq_ag)
    emit_proj_tm(nc, "p5", n1, wdeq_ag, True, x2, res=x1, bias=io["fbias"])
    emit_ln(nc, x2, n2, n_rows=TOK, tag="p5l")
    emit_coll(nc, "n2", "AllGather", ALU.bypass, pairs, n2, io["n2_full"])
    emit_p6(nc, io)
    emit_p7(nc, io)
    emit_coll(nc, "cx", "AllGather", ALU.bypass, pairs, io["ctx1"], ctx_ag)
    cx_loc = mid("cx_loc", [D, TOK], BF16)
    emit_reshard(nc, "cx", [
        (cx_loc[:, :], ctx_ag.tensor, [[4096, 4096], [1, 2048]], 0),
    ], io["off_p8"])
    emit_proj_tm(nc, "p8", cx_loc, io["w1o"], False, x3, res=x2, bias=io["b1o"],
                 act_mode="fm")
    emit_ln(nc, x3, io["n3_my"], n_rows=TOK, tag="p8l")
    emit_proj_tm(nc, "p9h", io["n3_my"], io["W1T"], False, io["x3p_my"],
                 res=x3, bias=io["b1"])
    emit_p9z(nc, io)
    emit_coll(nc, "mx", "AllReduce", ALU.max, all8, io["mx_my"], mx_g)
    io["mx_all"] = mx_g.rearrange("t o -> o t")
    emit_p9b(nc, io, mx_pre=True)
    return nc


_IN_KEYS = ["x_my", "pack_my", "wsc_my", "wqk_t", "wv_t", "wg_t", "nw_bc", "wo_t",
            "fbias", "w1qk", "w1v", "b1qk", "b1v2", "sl_kpos", "slqp", "dmask",
            "w1o", "b1o", "W1T", "b1", "A1", "LiMEsT", "LiMEs", "Bmat", "identf",
            "ident", "qd_bc", "kd", "bd", "ddT"]


def kernel(**inputs):
    pc = host_prep(inputs)
    cc = np.ascontiguousarray
    x = np.asarray(inputs["L0_attn_hidden_states"])
    nc = build_fused()
    in_maps = []
    for c in range(NCORE):
        b, h = c // 2, c % 2
        d = {k: cc(np.asarray(pc[c][k]).astype(np.float32)
                   if np.asarray(pc[c][k]).dtype == np.float64 else pc[c][k])
             for k in _IN_KEYS}
        d["fbias"] = cc(d["fbias"].astype(np.float32))
        d["b1o"] = cc(d["b1o"].astype(np.float32))
        d["b1"] = cc(d["b1"].astype(np.float32))
        d["x_full"] = cc(x[b])
        d["off_p4"] = np.array([[h * 2048 * 2048]], np.int32)
        d["off_p8"] = np.array([[h * 2048]], np.int32)
        in_maps.append(d)
    res = run_bass_kernel_spmd(nc, in_maps, core_ids=list(range(NCORE)))
    out = np.zeros((B, S, D), np.float32)
    for c in range(NCORE):
        b, h = c // 2, c % 2
        out[b, h * TOK:(h + 1) * TOK] = res.results[c]["x4_my"]
    return out



def _run(nc, in_maps):
    res = run_bass_kernel_spmd(nc, in_maps, core_ids=list(range(NCORE)))
    return res.results


def kernel_unfused(**inputs):
    pc = host_prep(inputs)
    cc = np.ascontiguousarray

    r0 = _run(build_p0(), [{k: cc(pc[c][k]) for k in ["x_my", "pack_my", "wsc_my"]}
                           for c in range(NCORE)])
    n0_full = [np.concatenate([r0[2 * b]["n0_my"], r0[2 * b + 1]["n0_my"]], 0)
               for b in range(4)]
    wdeq_all = np.concatenate([r0[c]["wdeq_my"] for c in range(NCORE)], 0)

    r1 = _run(build_p1(), [dict(n0_full=n0_full[c // 2],
                                **{k: cc(pc[c][k]) for k in ["wqk_t", "wv_t", "wg_t"]})
                           for c in range(NCORE)])

    r2 = _run(build_p2(), [dict(qk0=r1[c]["qk0"], v0fm=r1[c]["v0fm"],
                                **{k: cc(pc[c][k]) for k in ["qd_bc", "kd", "bd", "ddT"]})
                           for c in range(NCORE)])

    r3 = _run(build_p3(), [dict(attn0=r2[c]["attn0"], g0fm=r1[c]["g0fm"],
                                nw_bc=cc(pc[c]["nw_bc"]), ss_my=r2[c]["ss_my"],
                                ss_pt=r2[c ^ 1]["ss_my"]) for c in range(NCORE)])

    gn_my = []
    for c in range(NCORE):
        b, h = c // 2, c % 2
        gn_all = np.concatenate([r3[2 * b]["gn0"], r3[2 * b + 1]["gn0"]], 1)
        gn_my.append(cc(gn_all[h * TOK:(h + 1) * TOK]))
    r4 = _run(build_p4(), [dict(gn_my=gn_my[c], wo_t=cc(pc[c]["wo_t"]),
                                x_my=cc(pc[c]["x_my"])) for c in range(NCORE)])

    r5 = _run(build_p5(), [dict(n1_my=r4[c]["n1_my"], wdeq=wdeq_all,
                                fbias=cc(pc[c]["fbias"].astype(np.float32)),
                                x1_my=r4[c]["x1_my"]) for c in range(NCORE)])
    n2_full = [np.concatenate([r5[2 * b]["n2_my"], r5[2 * b + 1]["n2_my"]], 0)
               for b in range(4)]

    r6 = _run(build_p6(), [dict(n2_full=n2_full[c // 2],
                                **{k: cc(pc[c][k]) for k in ["w1qk", "w1v", "b1qk", "b1v2"]})
                           for c in range(NCORE)])

    r7 = _run(build_p7(), [dict(qk1=r6[c]["qk1"], v1fm=r6[c]["v1fm"],
                                **{k: cc(pc[c][k]) for k in ["sl_kpos", "slqp", "dmask"]})
                           for c in range(NCORE)])

    ctx_my = []
    for c in range(NCORE):
        b, h = c // 2, c % 2
        ctx_all = np.concatenate([r7[2 * b]["ctx1"], r7[2 * b + 1]["ctx1"]], 0)
        ctx_my.append(cc(ctx_all.T[h * TOK:(h + 1) * TOK]))
    r8 = _run(build_p8(), [dict(ctx_my=ctx_my[c], w1o=cc(pc[c]["w1o"].astype(np.float32)),
                                b1o=cc(pc[c]["b1o"].astype(np.float32)),
                                x2_my=r5[c]["x2_my"]) for c in range(NCORE)])

    r9a = _run(build_p9a(), [dict(n3_my=r8[c]["n3_my"], x3_my=r8[c]["x3_my"],
                                  **{k: cc(np.asarray(pc[c][k]).astype(np.float32))
                                     for k in ["W1T", "b1", "A1", "LiMEsT", "identf"]})
                             for c in range(NCORE)])
    mx_all = np.concatenate([np.array([r9a[c]["mx_my"][0, 0] for c in range(NCORE)]),
                             np.array([r9a[c]["mx_my"][1, 0] for c in range(NCORE)])]
                            ).astype(np.float32)[None, :]

    r9b = _run(build_p9b(), [dict(mx_all=cc(mx_all), h8_my=r9a[c]["h8_my"],
                                  ds_my=r9a[c]["ds_my"], z_my=r9a[c]["z_my"],
                                  LiMEs=cc(pc[c]["LiMEs"].astype(np.float32)),
                                  Bmat=cc(pc[c]["Bmat"].astype(np.float32)),
                                  x3p_my=r9a[c]["x3p_my"], ident=cc(pc[c]["ident"]))
                             for c in range(NCORE)])

    out = np.zeros((B, S, D), np.float32)
    for c in range(NCORE):
        b, h = c // 2, c % 2
        out[b, h * TOK:(h + 1) * TOK] = r9b[c]["x4_my"]
    return out


# version 09 (final)
# speedup vs baseline: 1.0191x; 1.0016x over previous
import math
from contextlib import ExitStack

import numpy as np
import ml_dtypes

import concourse.bass as bass
import concourse.mybir as mybir
from concourse.bass_utils import run_bass_kernel_spmd

F32 = mybir.dt.float32
BF16 = mybir.dt.bfloat16
I32 = mybir.dt.int32
AF = mybir.ActivationFunctionType
ALU = mybir.AluOpType
AX = mybir.AxisListType
BF = ml_dtypes.bfloat16

B, S, D = 4, 4096, 4096
TOK = 2048
NHEAD = 32
HD = 128
HPC = 16
BLK = 256
NB = S // BLK
EPS_LN = 1e-5
EPS_RMS = 1e-5
NCORE = 8



def _l0_slopes():
    base = 1.0 / 2.0 ** (8.0 / NHEAD)
    exponent = np.arange(NHEAD, dtype=np.float32) + 1.0
    factor = 1.0 - 1 / (32 - 1 + 1e-5) + 1e-5
    return (base ** exponent * factor).astype(np.float32)


def _l1_slopes():
    n = 32
    cp2 = 2 ** math.floor(math.log2(n))
    base = 2 ** (-(2 ** -(math.log2(cp2) - 3)))
    powers = np.arange(1, 1 + cp2, dtype=np.float32)
    return np.power(base, powers).astype(np.float32)


def host_tables(core):
    h = core % 2
    my_heads = list(range(h * HPC, h * HPC + HPC))
    sl0 = _l0_slopes()
    r = np.arange(BLK, dtype=np.float32) + 1.0
    t = {}
    qd = np.exp(-sl0[my_heads][:, None] * r[None, :]).astype(np.float32)
    kd = np.exp(-sl0[my_heads][:, None] * (BLK - r)[None, :]).astype(np.float32)
    bd = np.exp(-sl0[my_heads] * BLK).astype(np.float32)
    diff = r[:, None] - r[None, :]
    ddT = []
    for hh in my_heads:
        dg = sl0[hh] * (-diff)
        dmat = np.where(diff >= 0, np.exp(-sl0[hh] * diff, dtype=np.float32), 0.0)
        ddT.append(dmat.T.astype(np.float32))
    t["qd"] = qd
    t["kd"] = np.ascontiguousarray(kd.reshape(16, 2, 128).transpose(0, 2, 1))
    t["bd"] = np.repeat(bd[:, None], 128, 1)[:, :, None].astype(np.float32)
    t["ddT"] = np.stack(ddT)
    t["qd_bc"] = np.repeat(qd[:, None, :], 128, 1).astype(np.float32)
    heads_l1 = [2 * j + h for j in range(HPC)]
    sl1 = _l1_slopes()[heads_l1]
    kpos = np.arange(S, dtype=np.float32).reshape(32, 128)
    t["sl_kpos"] = np.ascontiguousarray(
        (sl1[:, None, None] * kpos[None]).transpose(0, 2, 1)).astype(np.float32)
    t["slqp"] = (sl1[:, None] * np.arange(S, dtype=np.float32)[None, :]).astype(np.float32)
    masks = np.zeros((4, 128, 512), np.float32)
    for a in range(4):
        kk = a * 128 + np.arange(128)
        qq = np.arange(512)
        masks[a] = np.where(kk[:, None] <= qq[None, :], 0.0, -1.0e8)
    t["dmask"] = masks
    t["ident"] = np.eye(128, dtype=BF)
    par = (np.arange(128) % 2 == 1).astype(np.float32)[:, None]
    t["odd_par"] = par
    return t


def host_prep(inputs):
    x = np.asarray(inputs["L0_attn_hidden_states"])
    qkv_wT = np.ascontiguousarray(np.asarray(inputs["L0_attn_qkv_w"]).T)
    out_wT = np.ascontiguousarray(np.asarray(inputs["L0_attn_out_w"]).T)
    gate_wT = np.ascontiguousarray(np.asarray(inputs["L0_attn_gate_w"]).T)
    norm_w = np.asarray(inputs["L0_attn_norm_w"])
    packed = np.asarray(inputs["L0_ffn_packed_weight"])
    wscale = np.asarray(inputs["L0_ffn_scale"])
    fbias = np.asarray(inputs["L0_ffn_bias"])
    w_qkv1 = np.asarray(inputs["L1_attn_w_qkv"])
    b_qkv1 = np.asarray(inputs["L1_attn_b_qkv"])
    w_out1 = np.asarray(inputs["L1_attn_w_out"])
    b_out1 = np.asarray(inputs["L1_attn_b_out"])
    W1T = np.ascontiguousarray(np.asarray(inputs["L1_ffn_W"]).T)
    b1 = np.asarray(inputs["L1_ffn_b"])
    A1 = np.asarray(inputs["L1_ffn_A"])
    Bmat = np.asarray(inputs["L1_ffn_Bmat"])
    LiMEs = np.asarray(inputs["L1_ffn_LiMEs"])
    LiMEsT = np.ascontiguousarray(LiMEs.T)

    per_core = []
    for c in range(NCORE):
        b = c // 2
        h = c % 2
        heads = list(range(h * HPC, h * HPC + HPC))
        heads_l1 = [2 * j + h for j in range(HPC)]
        qk_cols = np.concatenate([np.arange(hh * 384, hh * 384 + 256) for hh in heads])
        v_cols = np.concatenate([np.arange(hh * 384 + 256, hh * 384 + 384) for hh in heads])
        d = {
            "x_my": np.ascontiguousarray(x[b, h * TOK:(h + 1) * TOK]),
            "wqk_t": np.ascontiguousarray(qkv_wT[:, qk_cols]),
            "wv_t": np.ascontiguousarray(qkv_wT[:, v_cols]),
            "wg_t": np.ascontiguousarray(gate_wT[:, h * 2048:(h + 1) * 2048]),
            "wo_t": out_wT,
            "norm_w": norm_w[None, :].astype(np.float32),
            "nw_bc": np.ascontiguousarray(np.repeat(
                norm_w[h * 2048:(h + 1) * 2048][None, :], 128, 0).astype(np.float32)),
            "pack_my": np.ascontiguousarray(packed[c * 512:(c + 1) * 512]),
            "wsc_my": np.ascontiguousarray(wscale[c * 512:(c + 1) * 512]),
            "fbias": fbias[None, :],
            "w1qk": np.ascontiguousarray(
                w_qkv1[:, np.concatenate([np.concatenate(
                    [np.arange(hh * 128, hh * 128 + 128),
                     np.arange(4096 + hh * 128, 4096 + hh * 128 + 128)]) for hh in heads_l1])]),
            "w1v": np.ascontiguousarray(
                w_qkv1[:, np.concatenate([np.arange(8192 + hh * 128, 8192 + hh * 128 + 128)
                                          for hh in heads_l1])]),
            "b1qk": None,
            "b1v2": None,
            "w1o": np.ascontiguousarray(w_out1[np.concatenate(
                [np.arange((2 * j + p) * 128, (2 * j + p) * 128 + 128)
                 for p in range(2) for j in range(HPC)])]),
            "b1o": b_out1[None, :],
            "W1T": W1T, "b1": b1[None, :], "A1": A1, "Bmat": Bmat,
            "LiMEs": LiMEs, "LiMEsT": LiMEsT,
        }
        qkb = np.concatenate([np.concatenate(
            [b_qkv1[hh * 128:hh * 128 + 128], b_qkv1[4096 + hh * 128:4096 + hh * 128 + 128]])
            for hh in heads_l1])
        vb = np.concatenate([b_qkv1[8192 + hh * 128:8192 + hh * 128 + 128] for hh in heads_l1])
        d["b1qk"] = np.ascontiguousarray(qkb.reshape(32, 128).T).astype(np.float32)
        d["b1v2"] = np.ascontiguousarray(vb.reshape(16, 128).T).astype(np.float32)
        d["identf"] = np.eye(128, dtype=np.float32)
        d.update(host_tables(c))
        per_core.append(d)
    return per_core



class Sem:

    def __init__(self, handle):
        self.h = handle
        self.n = 0

    def inc(self, inst, k=1):
        inst.then_inc(self.h, k)
        self.n += k
        return self.n


def mk_sems(st, nc, names):
    return {n: Sem(st.enter_context(nc.semaphore(n))) for n in names}


def dinc(eng, sem, k=1):
    sem.inc(eng.drain(), k)



def emit_ln(nc, src, dst, raw_dst=None, n_rows=TOK, tag="ln"):
    NT = n_rows // 128
    with ExitStack() as st:
        def sb(name, shape, dt):
            return st.enter_context(nc.sbuf_tensor(tag + name, shape, dt))
        xt = [sb(f"x{j}", [128, D], F32) for j in range(2)]
        n0t = [sb(f"n{j}", [128, D], BF16) for j in range(2)]
        bstat = sb("bst", [128, 48], F32)
        mvc = sb("mvc", [128, 2 * NT], F32)
        lnv = sb("lnv", [128, NT], F32)
        lnh = sb("lnh", [128, NT], F32)
        rstd = sb("rstd", [128, NT], F32)
        epsc = sb("eps", [128, 1], F32)
        sems = mk_sems(st, nc, [tag + k for k in ["ld", "cv", "ca", "so"]])
        ld, cv, ca, so = (sems[tag + k] for k in ["ld", "cv", "ca", "so"])
        blk = st.enter_context(nc.Block())

        @blk.sync
        def _(sy):
            for j in range(NT):
                if j >= 2:
                    sy.wait_ge(cv.h, j - 1)
                sy.dma_start(out=xt[j % 2][:], in_=src[j * 128:(j + 1) * 128, :]).then_inc(ld.h, 16)
            sy.wait_ge(ca.h, 2)
            for j in range(NT):
                if j >= 2:
                    sy.wait_ge(cv.h, NT + 2 + (j - 1))
                sy.dma_start(out=xt[j % 2][:], in_=src[j * 128:(j + 1) * 128, :]).then_inc(ld.h, 16)
            ld.n = 32 * NT

        @blk.vector
        def _(v):
            for j in range(NT):
                v.wait_ge(ld.h, 16 * (j + 1))
                for c8 in range(8):
                    v.bn_stats(out=bstat[:, c8 * 6:(c8 + 1) * 6],
                               in_=xt[j % 2][:, c8 * 512:(c8 + 1) * 512])
                v.drain()
                v.bn_aggr(out=mvc[:, 2 * j:2 * j + 2], in_=bstat[:])
                dinc(v, cv)
            v.wait_ge(ca.h, 1)
            v.tensor_scalar_mul(out=lnh[:], in0=lnv[:], scalar1=-0.5)
            dinc(v, cv)
            dinc(v, cv)
            v.wait_ge(ca.h, 2)
            for j in range(NT):
                v.wait_ge(ld.h, 16 * (NT + j + 1))
                if j >= 2:
                    v.wait_ge(so.h, 16 * (j - 1))
                v.tensor_scalar(out=n0t[j % 2][:], in0=xt[j % 2][:],
                                scalar1=mvc[:, 2 * j:2 * j + 1],
                                scalar2=rstd[:, j:j + 1],
                                op0=ALU.subtract, op1=ALU.mult)
                dinc(v, cv)

        @blk.scalar
        def _(s):
            s.wait_ge(cv.h, NT)
            s.activation(out=lnv[:], in_=mvc[:].rearrange("p (j k) -> p j k", k=2)[:, :, 1:2],
                         func=AF.Ln, bias=epsc[:])
            dinc(s, ca)
            s.wait_ge(cv.h, NT + 1)
            s.activation(out=rstd[:], in_=lnh[:], func=AF.Exp)
            dinc(s, ca)

        @blk.gpsimd
        def _(g):
            g.memset(epsc[:], EPS_LN)
            nr = 0
            if raw_dst is not None:
                q = n_rows // 4
                for j in range(4):
                    g.dma_start(out=raw_dst[j * q:(j + 1) * q, :],
                                in_=src[j * q:(j + 1) * q, :]).then_inc(so.h, 16)
                nr = 4
            for j in range(NT):
                g.wait_ge(cv.h, NT + 2 + j + 1)
                g.dma_start(out=dst[j * 128:(j + 1) * 128, :], in_=n0t[j % 2][:]).then_inc(so.h, 16)
            so.n = 16 * (NT + nr)

        @blk.sync
        def _(sy):
            sy.wait_ge(so.h, so.n)

    return nc


def emit_dequant(nc, io):
    pk = io["pack_my"]
    sc = io["wsc_my"]
    wq = io["wdeq_my"]
    with ExitStack() as st:
        def sb(name, shape, dt):
            return st.enter_context(nc.sbuf_tensor(name, shape, dt))
        pkt = sb("dqpk", [128, 2048], I32)
        sct = sb("dqsc", [128, 128], F32)
        shi = sb("dqshi", [128, 512], I32)
        wf_ = sb("dqwf", [128, 4096], F32)
        wb_ = sb("dqwb", [128, 4096], BF16)
        sems = mk_sems(st, nc, ["dqld", "dq", "dqs"])
        ld, dq, dqs = (sems[k] for k in ["dqld", "dq", "dqs"])
        blk = st.enter_context(nc.Block())

        @blk.sync
        def _(sy):
            sy.dma_start(out=pkt[:].rearrange("p (c n) -> p c n", c=4),
                         in_=pk.rearrange("(c p) n -> p c n", p=128)).then_inc(ld.h, 16)
            sy.dma_start(out=sct[:].rearrange("p (c n) -> p c n", c=4),
                         in_=sc.rearrange("(c p) n -> p c n", p=128)).then_inc(ld.h, 16)

        @blk.vector
        def _(v):
            v.wait_ge(ld.h, 32)
            for rc in range(4):
                pkv = pkt[:, rc * 512:(rc + 1) * 512]
                for e in range(8):
                    v.tensor_scalar(out=shi[:], in0=pkv, scalar1=4 * e, scalar2=28,
                                    op0=ALU.logical_shift_left, op1=ALU.arith_shift_right)
                    v.tensor_copy(out=wf_[:, e:4096:8], in_=shi[:])
                if rc >= 1:
                    v.wait_ge(dqs.h, 16 * rc)
                for gsc in range(32):
                    v.tensor_scalar(out=wb_[:, gsc * 128:(gsc + 1) * 128],
                                    in0=wf_[:, gsc * 128:(gsc + 1) * 128],
                                    scalar1=sct[:, rc * 32 + gsc:rc * 32 + gsc + 1],
                                    scalar2=None, op0=ALU.mult)
                dinc(v, dq)

        @blk.gpsimd
        def _(g):
            for rc in range(4):
                g.wait_ge(dq.h, rc + 1)
                g.dma_start(out=wq[rc * 128:(rc + 1) * 128, :], in_=wb_[:]).then_inc(dqs.h, 16)

        @blk.sync
        def _(sy):
            sy.wait_ge(dqs.h, 64)
    return nc


def emit_p0(nc, io):
    emit_ln(nc, io["x_my"], io["n0_my"], n_rows=TOK, tag="p0")
    emit_dequant(nc, io)
    return nc



def emit_p1(nc, io):
    n0 = io["n0_full"]
    wqk = io["wqk_t"]
    wv = io["wv_t"]
    wg = io["wg_t"]
    qk0 = io["qk0"]
    v0 = io["v0fm"]
    g0 = io["g0fm"]
    NF = 64

    for th in (0, 1):
        with ExitStack() as st:
            def sb(name, shape, dt):
                return st.enter_context(nc.sbuf_tensor(f"p1{th}{name}", shape, dt))
            rhsf = [sb(f"r{i}", [128, 2048], BF16) for i in range(32)]
            wbuf = [sb(f"w{j}", [128, 4096], BF16) for j in range(2)]
            evb = [sb(f"e{j}", [128, 512], BF16) for j in range(8)]
            ps = [st.enter_context(nc.psum_tensor(f"p1{th}ps{j}", [128, 512], F32))
                  for j in range(8)]
            sems = mk_sems(st, nc, [f"p1{th}{k}" for k in ["ldr", "wl", "wf", "ce", "se"]])
            ldr, wl, wf, ce, se = (sems[f"p1{th}{k}"] for k in ["ldr", "wl", "wf", "ce", "se"])
            blk = st.enter_context(nc.Block())
            tsl = slice(th * 2048, (th + 1) * 2048)

            @blk.sync
            def _(sy):
                for i in range(32):
                    sy.dma_start(out=rhsf[i][:], in_=n0[tsl, i * 128:(i + 1) * 128],
                                 transpose=True).then_inc(ldr.h, 16)

            @blk.gpsimd
            def _(g):
                for f in range(NF):
                    if f < 32:
                        src = wqk[:, f * 128:(f + 1) * 128]
                    elif f < 48:
                        src = wv[:, (f - 32) * 128:(f - 31) * 128]
                    else:
                        src = wg[:, (f - 48) * 128:(f - 47) * 128]
                    if f >= 2:
                        g.wait_ge(wf.h, f - 1)
                    s3 = src.rearrange("(i p) c -> p i c", p=128)
                    d3 = wbuf[f % 2][:].rearrange("p (i c) -> p i c", c=128)
                    for q in range(4):
                        g.dma_start(out=d3[:, q * 8:(q + 1) * 8, :],
                                    in_=s3[:, q * 8:(q + 1) * 8, :]).then_inc(wl.h, 16)

            @blk.tensor
            def _(t):
                t.wait_ge(ldr.h, 16 * 32)
                for f in range(NF):
                    t.wait_ge(wl.h, 64 * (f + 1))
                    if f >= 2:
                        t.wait_ge(ce.h, 4 * (f - 1))
                    for i in range(32):
                        lhsT = wbuf[f % 2][:, i * 128:(i + 1) * 128]
                        for tt in range(4):
                            ti = t.matmul(ps[(f % 2) * 4 + tt][:],
                                          lhsT, rhsf[i][:, tt * 512:(tt + 1) * 512],
                                          start=(i == 0), stop=(i == 31))
                    wf.inc(ti)

            @blk.scalar
            def _(s):
                for f in range(NF):
                    s.wait_ge(wf.h, f + 1)
                    if f >= 2:
                        s.wait_ge(se.h, 16 * 4 * (f - 1))
                    func = AF.Silu if f < 48 else AF.Sigmoid
                    for tt in range(4):
                        si = s.activation(out=evb[(f % 2) * 4 + tt][:],
                                          in_=ps[(f % 2) * 4 + tt][:], func=func)
                        ce.inc(si)
                    for tt in range(4):
                        if f < 32:
                            dst = qk0[f * 128:(f + 1) * 128,
                                      th * 2048 + tt * 512: th * 2048 + (tt + 1) * 512]
                        elif f < 48:
                            dst = v0[(f - 32) * 128:(f - 31) * 128,
                                     th * 2048 + tt * 512: th * 2048 + (tt + 1) * 512]
                        else:
                            dst = g0[(f - 48) * 128:(f - 47) * 128,
                                     th * 2048 + tt * 512: th * 2048 + (tt + 1) * 512]
                        s.dma_start(out=dst, in_=evb[(f % 2) * 4 + tt][:]).then_inc(se.h, 16)

            @blk.sync
            def _(sy):
                sy.wait_ge(se.h, 16 * 4 * NF)
    return nc


def build_p1():
    nc = bass.Bass("TRN2", target_bir_lowering=False, debug=False)
    io = {
        "n0_full": nc.dram_tensor("n0_full", [S, D], BF16, kind="ExternalInput").ap(),
        "wqk_t": nc.dram_tensor("wqk_t", [D, 4096], F32, kind="ExternalInput").ap(),
        "wv_t": nc.dram_tensor("wv_t", [D, 2048], F32, kind="ExternalInput").ap(),
        "wg_t": nc.dram_tensor("wg_t", [D, 2048], F32, kind="ExternalInput").ap(),
        "qk0": nc.dram_tensor("qk0", [4096, S], BF16, kind="ExternalOutput").ap(),
        "v0fm": nc.dram_tensor("v0fm", [2048, S], BF16, kind="ExternalOutput").ap(),
        "g0fm": nc.dram_tensor("g0fm", [2048, S], BF16, kind="ExternalOutput").ap(),
    }
    emit_p1(nc, io)
    return nc



def emit_p2(nc, io):
    qk0 = io["qk0"]
    v0 = io["v0fm"]
    attn0 = io["attn0"]
    qdT = io["qd_bc"]
    kdT = io["kd"]
    bdT = io["bd"]
    ddTT = io["ddT"]
    identb = io["ident"]

    with ExitStack() as st:
        def sb(name, shape, dt):
            return st.enter_context(nc.sbuf_tensor(f"p2{name}", shape, dt))
        qsl = [sb(f"q{j}", [128, S], BF16) for j in range(2)]
        ksl = [sb(f"k{j}", [128, S], BF16) for j in range(2)]
        vsl = [sb(f"v{j}", [128, S], BF16) for j in range(2)]
        qdb = [sb(f"qd{j}", [128, 256], F32) for j in range(2)]
        kdc = [sb(f"kd{j}", [128, 2], F32) for j in range(2)]
        bdc = [sb(f"bd{j}", [128, 1], F32) for j in range(2)]
        dds = [sb(f"dd{j}", [128, 512], F32) for j in range(2)]
        idb = sb("idb", [128, 128], BF16)
        qs_ = [sb(f"qs{j}", [128, 256], BF16) for j in range(2)]
        stb = [sb(f"st{j}", [128, 512], BF16) for j in range(2)]
        kkd = [sb(f"kk{j}", [128, 256], BF16) for j in range(2)]
        vtm = [sb(f"vt{j}", [128, 256], BF16) for j in range(2)]
        atb = [sb(f"at{j}", [128, 256], BF16) for j in range(2)]
        kvf = [sb(f"kvf{j}", [128, 128], F32) for j in range(2)]
        kvbf = sb("kvbf", [128, 128], BF16)
        sqw = sb("sqw", [128, 256], F32)
        ssall = sb("ssall", [128, 512], F32)
        ssout = sb("ssout", [128, 32], F32)
        ps_st = [st.enter_context(nc.psum_tensor(f"p2pst{j}", [128, 256], F32)) for j in range(2)]
        ps_a = [st.enter_context(nc.psum_tensor(f"p2a{j}", [128, 128], F32)) for j in range(2)]
        ps_c = st.enter_context(nc.psum_tensor("p2c", [128, 128], F32))
        ps_tk = st.enter_context(nc.psum_tensor("p2tk", [128, 256], BF16))
        ps_tv = st.enter_context(nc.psum_tensor("p2tv", [128, 256], BF16))
        sems = mk_sems(st, nc, [f"p2{k}" for k in ["ld", "tm", "dv", "ca", "se"]])
        ld, tm, dv, ca, se = (sems[f"p2{k}"] for k in ["ld", "tm", "dv", "ca", "se"])
        blk = st.enter_context(nc.Block())

        def ldslot(s_):
            return 16 * (1 + 7 * (s_ + 1))

        @blk.sync
        def _(sy):
            sy.dma_start(out=idb[:], in_=identb).then_inc(ld.h, 16)
            for s_ in range(16):
                sp = s_ % 2
                if s_ >= 2:
                    sy.wait_ge(tm.h, 5 * 16 * (s_ - 1))
                    sy.wait_ge(dv.h, 5 * 16 * (s_ - 1))
                r0 = s_ * 256
                sy.dma_start(out=qsl[sp][:], in_=qk0[r0:r0 + 128, :]).then_inc(ld.h, 16)
                sy.dma_start(out=ksl[sp][:], in_=qk0[r0 + 128:r0 + 256, :]).then_inc(ld.h, 16)
                sy.dma_start(out=vsl[sp][:], in_=v0[s_ * 128:(s_ + 1) * 128, :]).then_inc(ld.h, 16)
                sy.dma_start(out=qdb[sp][:], in_=qdT[s_]).then_inc(ld.h, 16)
                sy.dma_start(out=kdc[sp][:], in_=kdT[s_]).then_inc(ld.h, 16)
                sy.dma_start(out=bdc[sp][:], in_=bdT[s_]).then_inc(ld.h, 16)
                sy.dma_start(out=dds[sp][:].rearrange("p (c n) -> p c n", c=2),
                             in_=ddTT[s_].rearrange("(c p) n -> p c n", p=128)).then_inc(ld.h, 16)

        @blk.vector
        def _(v):
            for s_ in range(16):
                sp = s_ % 2
                if s_ >= 1:
                    v.wait_ge(ca.h, 3 * 16 * s_)
                v.memset(kvf[0][:], 0.0)
                v.memset(kvbf[:], 0.0)
                for n_ in range(16):
                    u = s_ * 16 + n_
                    p = u % 2
                    c0 = n_ * 256
                    v.wait_ge(ld.h, ldslot(s_))
                    v.tensor_tensor(out=qs_[p][:], in0=qsl[sp][:, c0:c0 + 256],
                                    in1=qdb[sp][:], op=ALU.mult)
                    dinc(v, dv)
                    v.wait_ge(tm.h, 5 * u + 3)
                    v.tensor_tensor(out=stb[p][:, 0:256], in0=ps_st[0][:],
                                    in1=dds[sp][:, 0:256], op=ALU.mult)
                    v.tensor_tensor(out=stb[p][:, 256:512], in0=ps_st[1][:],
                                    in1=dds[sp][:, 256:512], op=ALU.mult)
                    dinc(v, dv)
                    v.wait_ge(tm.h, 5 * u + 1)
                    v.tensor_scalar(out=kkd[p][:, 0:128], in0=ps_tk[:, 0:128],
                                    scalar1=kdc[sp][:, 0:1], scalar2=None, op0=ALU.mult)
                    v.tensor_scalar(out=kkd[p][:, 128:256], in0=ps_tk[:, 128:256],
                                    scalar1=kdc[sp][:, 1:2], scalar2=None, op0=ALU.mult)
                    dinc(v, dv)
                    v.wait_ge(tm.h, 5 * u + 5)
                    v.scalar_tensor_tensor(out=kvf[(n_ + 1) % 2][:], in0=kvf[n_ % 2][:],
                                           scalar=bdc[sp][:], in1=ps_c[:],
                                           op0=ALU.mult, op1=ALU.add)
                    dinc(v, dv)
                    v.wait_ge(ca.h, 3 * u + 3)
                    v.scalar_tensor_tensor(out=sqw[:], in0=atb[p][:], scalar=0.0,
                                           in1=atb[p][:], op0=ALU.add, op1=ALU.mult)
                    for nh in range(2):
                        tc = n_ * 2 + nh
                        v.reduce_sum(out=ssall[:, tc * 16 + s_:tc * 16 + s_ + 1],
                                     in_=sqw[:, nh * 128:(nh + 1) * 128], axis=AX.X)
                    dv.inc(v.engine_nop())

        @blk.tensor
        def _(t):
            t.wait_ge(ld.h, 16)
            for s_ in range(16):
                sp = s_ % 2
                for n_ in range(16):
                    u = s_ * 16 + n_
                    p = u % 2
                    c0 = n_ * 256
                    t.wait_ge(ld.h, ldslot(s_))
                    if u >= 1:
                        t.wait_ge(dv.h, 5 * (u - 1) + 3)
                    t.transpose(ps_tk[:, 0:128], ksl[sp][:, c0:c0 + 128], idb[:])
                    ti = t.transpose(ps_tk[:, 128:256], ksl[sp][:, c0 + 128:c0 + 256], idb[:])
                    tm.inc(ti)
                    if u >= 1:
                        t.wait_ge(ca.h, 3 * (u - 1) + 1)
                    t.transpose(ps_tv[:, 0:128], vsl[sp][:, c0:c0 + 128], idb[:])
                    ti = t.transpose(ps_tv[:, 128:256], vsl[sp][:, c0 + 128:c0 + 256], idb[:])
                    tm.inc(ti)
                    if u >= 1:
                        t.wait_ge(dv.h, 5 * (u - 1) + 2)
                    for mh in range(2):
                        ti = t.matmul(ps_st[mh][:], ksl[sp][:, c0 + mh * 128:c0 + (mh + 1) * 128],
                                      qsl[sp][:, c0:c0 + 256], start=True, stop=True)
                    tm.inc(ti)
                    t.wait_ge(dv.h, 5 * u + 2)
                    t.wait_ge(ca.h, 3 * u + 1)
                    if u >= 1:
                        t.wait_ge(ca.h, 3 * (u - 1) + 3)
                    for nh in range(2):
                        t.matmul(ps_a[nh][:], stb[p][:, nh * 128:nh * 128 + 128],
                                 vtm[p][:, 0:128], start=True, stop=False)
                        t.matmul(ps_a[nh][:], stb[p][:, 256 + nh * 128:256 + nh * 128 + 128],
                                 vtm[p][:, 128:256], start=False, stop=False)
                        ti = t.matmul(ps_a[nh][:], qs_[p][:, nh * 128:(nh + 1) * 128],
                                      kvbf[:], start=False, stop=True)
                    tm.inc(ti)
                    t.wait_ge(dv.h, max(5 * u + 3, 5 * (u - 1) + 4) if u >= 1 else 5 * u + 3)
                    t.matmul(ps_c[:], kkd[p][:, 0:128], vtm[p][:, 0:128],
                             start=True, stop=False)
                    ti = t.matmul(ps_c[:], kkd[p][:, 128:256], vtm[p][:, 128:256],
                                  start=False, stop=True)
                    tm.inc(ti)

        @blk.scalar
        def _(s):
            for s_ in range(16):
                for n_ in range(16):
                    u = s_ * 16 + n_
                    p = u % 2
                    s.wait_ge(tm.h, 5 * u + 2)
                    if u >= 2:
                        s.wait_ge(tm.h, 5 * (u - 2) + 5)
                    si = s.activation(out=vtm[p][:], in_=ps_tv[:], func=AF.Copy)
                    dinc(s, ca)
                    s.wait_ge(dv.h, 5 * u + 4)
                    s.activation(out=kvbf[:], in_=kvf[(n_ + 1) % 2][:], func=AF.Copy)
                    dinc(s, ca)
                    s.wait_ge(tm.h, 5 * u + 4)
                    if u >= 1:
                        s.wait_ge(se.h, 32 * u)
                    if u >= 2:
                        s.wait_ge(dv.h, 5 * (u - 2) + 5)
                    s.activation(out=atb[p][:, 0:128], in_=ps_a[0][:], func=AF.Copy)
                    s.activation(out=atb[p][:, 128:256], in_=ps_a[1][:], func=AF.Copy)
                    dinc(s, ca)
                    for nh in range(2):
                        s.dma_start(out=attn0[n_ * 256 + nh * 128:n_ * 256 + (nh + 1) * 128,
                                             s_ * 128:(s_ + 1) * 128],
                                    in_=atb[p][:, nh * 128:(nh + 1) * 128]).then_inc(se.h, 16)

        @blk.vector
        def _(v):
            v.reduce_sum(out=ssout[:].rearrange("p (a o) -> p a o", o=1),
                         in_=ssall[:].rearrange("p (a s) -> p a s", s=16), axis=AX.X)
            dinc(v, dv)

        @blk.gpsimd
        def _(g):
            g.wait_ge(dv.h, 5 * 256 + 1)
            g.dma_start(out=io["ss_my"], in_=ssout[:]).then_inc(se.h, 16)

        @blk.sync
        def _(sy):
            sy.wait_ge(se.h, 32 * 256 + 16)
    return nc


def build_p2():
    nc = bass.Bass("TRN2", target_bir_lowering=False, debug=False)
    io = {
        "qk0": nc.dram_tensor("qk0", [4096, S], BF16, kind="ExternalInput").ap(),
        "v0fm": nc.dram_tensor("v0fm", [2048, S], BF16, kind="ExternalInput").ap(),
        "qd_bc": nc.dram_tensor("qd_bc", [16, 128, 256], F32, kind="ExternalInput").ap(),
        "kd": nc.dram_tensor("kd", [16, 128, 2], F32, kind="ExternalInput").ap(),
        "bd": nc.dram_tensor("bd", [16, 128, 1], F32, kind="ExternalInput").ap(),
        "ddT": nc.dram_tensor("ddT", [16, 256, 256], F32, kind="ExternalInput").ap(),
        "ident": nc.dram_tensor("ident", [128, 128], BF16, kind="ExternalInput").ap(),
        "attn0": nc.dram_tensor("attn0", [S, 2048], BF16, kind="ExternalOutput").ap(),
        "ss_my": nc.dram_tensor("ss_my", [128, 32], F32, kind="ExternalOutput").ap(),
    }
    emit_p2(nc, io)
    return nc



def emit_p3(nc, io, ss_ag=False, ss_coll=None):
    attn0 = io["attn0"]
    g0 = io["g0fm"]
    nwb = io["nw_bc"]
    if ss_ag:
        ssm = io["ss_ag"][0:128, :]
        ssp = io["ss_ag"][128:256, :]
    else:
        ssm = io["ss_my"]
        ssp = io["ss_pt"]
    gn0 = io["gn0"]

    with ExitStack() as st:
        def sb(name, shape, dt):
            return st.enter_context(nc.sbuf_tensor(f"p3{name}", shape, dt))
        at_t = [sb(f"a{j}", [128, 2048], BF16) for j in range(2)]
        gt_t = [sb(f"g{j}", [128, 2048], BF16) for j in range(2)]
        t2 = sb("t2", [128, 2048], F32)
        gnt = [sb(f"o{j}", [128, 2048], BF16) for j in range(2)]
        nwt = sb("nw", [128, 2048], F32)
        ssa = sb("ssa", [128, 32], F32)
        ssb = sb("ssb", [128, 32], F32)
        w1 = sb("w1", [128, 32], F32)
        lnv = sb("lnv", [128, 32], F32)
        lnh = sb("lnh", [128, 32], F32)
        rstd = sb("rstd", [128, 32], F32)
        epsc = sb("eps", [128, 1], F32)
        sems = mk_sems(st, nc, [f"p3{k}" for k in ["ld", "cv", "ca", "so", "cc"]])
        ld, cv, ca, so, ccs = (sems[f"p3{k}"] for k in ["ld", "cv", "ca", "so", "cc"])
        blk = st.enter_context(nc.Block())

        if ss_coll is not None:
            @blk.gpsimd
            def _(g):
                g.collective_compute("AllGather", ALU.bypass,
                                     replica_groups=ss_coll[0],
                                     ins=[ss_coll[1].opt()],
                                     outs=[ss_coll[2].opt()]).then_inc(ccs.h)
                g.wait_ge(ccs.h, 1)

        @blk.sync
        def _(sy):
            if ss_coll is not None:
                sy.wait_ge(ccs.h, 1)
            sy.dma_start(out=nwt[:], in_=nwb).then_inc(ld.h, 16)
            sy.dma_start(out=ssa[:], in_=ssm).then_inc(ld.h, 16)
            sy.dma_start(out=ssb[:], in_=ssp).then_inc(ld.h, 16)
            for tc in range(32):
                if tc >= 2:
                    sy.wait_ge(cv.h, 4 + 2 * (tc - 2) + 2)
                sy.dma_start(out=at_t[tc % 2][:],
                             in_=attn0[tc * 128:(tc + 1) * 128, :]).then_inc(ld.h, 16)
                sy.dma_start(out=gt_t[tc % 2][:], in_=g0[:, tc * 128:(tc + 1) * 128],
                             transpose=True).then_inc(ld.h, 16)

        @blk.vector
        def _(v):
            v.memset(epsc[:], EPS_RMS)
            v.wait_ge(ld.h, 48)
            v.tensor_tensor(out=w1[:], in0=ssa[:], in1=ssb[:], op=ALU.add)
            dinc(v, cv)
            v.wait_ge(ca.h, 1)
            v.tensor_scalar_mul(out=lnh[:], in0=lnv[:], scalar1=-0.5)
            dinc(v, cv)
            dinc(v, cv)
            v.wait_ge(ca.h, 2)
            dinc(v, cv)
            for tc in range(32):
                v.wait_ge(ld.h, 48 + 32 * (tc + 1))
                v.scalar_tensor_tensor(out=t2[:], in0=at_t[tc % 2][:],
                                       scalar=rstd[:, tc:tc + 1], in1=nwt[:],
                                       op0=ALU.mult, op1=ALU.mult)
                if tc >= 2:
                    v.wait_ge(so.h, 16 * (tc - 1))
                v.tensor_tensor(out=gnt[tc % 2][:], in0=t2[:], in1=gt_t[tc % 2][:], op=ALU.mult)
                dinc(v, cv)
                dinc(v, cv)

        @blk.scalar
        def _(s):
            s.wait_ge(cv.h, 1)
            s.activation(out=lnv[:], in_=w1[:], func=AF.Ln, scale=1.0 / D, bias=epsc[:])
            dinc(s, ca)
            s.wait_ge(cv.h, 2)
            s.activation(out=rstd[:], in_=lnh[:], func=AF.Exp)
            dinc(s, ca)

        @blk.gpsimd
        def _(g):
            for tc in range(32):
                g.wait_ge(cv.h, 4 + 2 * tc + 2)
                g.dma_start(out=gn0[tc * 128:(tc + 1) * 128, :],
                            in_=gnt[tc % 2][:]).then_inc(so.h, 16)

        @blk.sync
        def _(sy):
            sy.wait_ge(so.h, 16 * 32)
    return nc



def emit_proj_tm(nc, tag, act, w_ap, w_is_bf16_fxi, out, res=None, bias=None,
                 act_mode="tm"):
    with ExitStack() as st:
        def sb(name, shape, dt):
            return st.enter_context(nc.sbuf_tensor(f"{tag}{name}", shape, dt))
        wsb = [sb(f"w{i}", [128, 2048], BF16) for i in range(32)]
        gnf = [[sb(f"a{q}_{i}", [128, 512], BF16) for i in range(32)] for q in range(2)]
        xt = sb("x0", [128, 1024], F32)
        xo = sb("o0", [128, 1024], F32)
        ones1 = sb("one", [1, 128], BF16)
        brow = sb("b", [1, 2048], BF16) if bias is not None else None
        ps = [st.enter_context(nc.psum_tensor(f"{tag}ps{j}", [128, 512], F32))
              for j in range(8)]
        sems = mk_sems(st, nc, [f"{tag}{k}" for k in ["lw", "la", "lx", "mm", "cv", "so", "pr"]])
        lw, la, lx, mm, cv, so, pr = (sems[f"{tag}{k}"]
                                      for k in ["lw", "la", "lx", "mm", "cv", "so", "pr"])
        blk = st.enter_context(nc.Block())

        @blk.vector
        def _(v):
            v.memset(ones1[:], 1.0)
            dinc(v, pr)
            g = 0
            for fh in range(2):
                for tq in range(4):
                    for tc in range(4):
                        g += 1
                        v.wait_ge(mm.h, g)
                        p = (g - 1) % 2
                        for half in range(2):
                            v.wait_ge(lx.h, 32 * g - 16 + 16 * half)
                            if g > 1 or half == 1:
                                v.wait_ge(so.h, 32 * (g - 1) + 16 * half)
                            for f2 in range(2):
                                fo = half * 2 + f2
                                v.tensor_tensor(out=xo[:, f2 * 512:(f2 + 1) * 512],
                                                in0=ps[p * 4 + fo][:],
                                                in1=xt[:, f2 * 512:(f2 + 1) * 512], op=ALU.add)
                            dinc(v, cv)

        @blk.gpsimd
        def _(g_):
            g = 0
            for fh in range(2):
                if fh == 1 and (bias is not None or not w_is_bf16_fxi):
                    g_.wait_ge(mm.h, 16)
                if bias is not None:
                    g_.dma_start(out=brow[:],
                                 in_=bias[:, fh * 2048:(fh + 1) * 2048]).then_inc(lw.h, 16)
                if not w_is_bf16_fxi:
                    for i in range(32):
                        g_.dma_start(out=wsb[i][:],
                                     in_=w_ap[i * 128:(i + 1) * 128, fh * 2048:(fh + 1) * 2048]
                                     ).then_inc(lw.h, 16)
                for tq in range(4):
                    for tc in range(4):
                        g += 1
                        for ho in range(2):
                            g_.wait_ge(cv.h, 2 * (g - 1) + ho + 1)
                            g_.dma_start(
                                out=out[tq * 512 + tc * 128:tq * 512 + (tc + 1) * 128,
                                        fh * 2048 + ho * 1024:fh * 2048 + (ho + 1) * 1024],
                                in_=xo[:]).then_inc(so.h, 16)

        @blk.sync
        def _(sy):
            g = 0
            for fh in range(2):
                if w_is_bf16_fxi:
                    if fh == 1:
                        sy.wait_ge(mm.h, 16)
                    for i in range(32):
                        sy.dma_start(out=wsb[i][:],
                                     in_=w_ap[fh * 2048:(fh + 1) * 2048, i * 128:(i + 1) * 128],
                                     transpose=True).then_inc(lw.h, 16)
                for tq in range(4):
                    gq = fh * 4 + tq
                    if gq >= 2:
                        sy.wait_ge(mm.h, 4 * (gq - 1))
                    for i in range(32):
                        if act_mode == "tm":
                            sy.dma_start(out=gnf[gq % 2][i][:],
                                         in_=act[tq * 512:(tq + 1) * 512, i * 128:(i + 1) * 128],
                                         transpose=True).then_inc(la.h, 16)
                        else:
                            sy.dma_start(out=gnf[gq % 2][i][:],
                                         in_=act[i * 128:(i + 1) * 128, tq * 512:(tq + 1) * 512]
                                         ).then_inc(la.h, 16)
                    for tc in range(4):
                        g += 1
                        for half in range(2):
                            if g > 1 or half == 1:
                                sy.wait_ge(cv.h, 2 * (g - 1) + half)
                            sy.dma_start(
                                out=xt[:],
                                in_=res[tq * 512 + tc * 128:tq * 512 + (tc + 1) * 128,
                                        fh * 2048 + half * 1024:fh * 2048 + (half + 1) * 1024]
                            ).then_inc(lx.h, 16)

        @blk.tensor
        def _(t):
            nb = 0 if bias is None else 1
            nwl = 32 + nb if not w_is_bf16_fxi else 32 + nb
            t.wait_ge(pr.h, 1)
            g = 0
            for fh in range(2):
                t.wait_ge(lw.h, 16 * nwl * (fh + 1))
                for tq in range(4):
                    gq = fh * 4 + tq
                    t.wait_ge(la.h, 16 * 32 * (gq + 1))
                    for tc in range(4):
                        g += 1
                        p = (g - 1) % 2
                        if g > 2:
                            t.wait_ge(cv.h, 2 * (g - 2))
                        for i in range(32):
                            for fo in range(4):
                                ti = t.matmul(ps[p * 4 + fo][:],
                                              gnf[gq % 2][i][:, tc * 128:(tc + 1) * 128],
                                              wsb[i][:, fo * 512:(fo + 1) * 512],
                                              start=(i == 0), stop=(i == 31 and bias is None))
                        if bias is not None:
                            for fo in range(4):
                                ti = t.matmul(ps[p * 4 + fo][:], ones1[:],
                                              brow[:, fo * 512:(fo + 1) * 512],
                                              start=False, stop=True)
                        mm.inc(ti)

        @blk.sync
        def _(sy):
            sy.wait_ge(so.h, 32 * 32)
    return nc



def build_p3():
    nc = bass.Bass("TRN2", target_bir_lowering=False, debug=False)
    io = {
        "attn0": nc.dram_tensor("attn0", [S, 2048], BF16, kind="ExternalInput").ap(),
        "g0fm": nc.dram_tensor("g0fm", [2048, S], BF16, kind="ExternalInput").ap(),
        "nw_bc": nc.dram_tensor("nw_bc", [128, 2048], F32, kind="ExternalInput").ap(),
        "ss_my": nc.dram_tensor("ss_my", [128, 32], F32, kind="ExternalInput").ap(),
        "ss_pt": nc.dram_tensor("ss_pt", [128, 32], F32, kind="ExternalInput").ap(),
        "gn0": nc.dram_tensor("gn0", [S, 2048], BF16, kind="ExternalOutput").ap(),
    }
    emit_p3(nc, io)
    return nc


def build_p4():
    nc = bass.Bass("TRN2", target_bir_lowering=False, debug=False)
    gn_my = nc.dram_tensor("gn_my", [TOK, D], BF16, kind="ExternalInput").ap()
    wo_t = nc.dram_tensor("wo_t", [D, D], F32, kind="ExternalInput").ap()
    x_my = nc.dram_tensor("x_my", [TOK, D], F32, kind="ExternalInput").ap()
    x1_my = nc.dram_tensor("x1_my", [TOK, D], F32, kind="ExternalOutput").ap()
    n1_my = nc.dram_tensor("n1_my", [TOK, D], BF16, kind="ExternalOutput").ap()
    emit_proj_tm(nc, "p4", gn_my, wo_t, False, x1_my, res=x_my)
    emit_ln(nc, x1_my, n1_my, n_rows=TOK, tag="p4ln")
    return nc



def build_p0():
    nc = bass.Bass("TRN2", target_bir_lowering=False, debug=False)
    io = {
        "x_my": nc.dram_tensor("x_my", [TOK, D], F32, kind="ExternalInput").ap(),
        "pack_my": nc.dram_tensor("pack_my", [512, 512], I32, kind="ExternalInput").ap(),
        "wsc_my": nc.dram_tensor("wsc_my", [512, 32], F32, kind="ExternalInput").ap(),
        "n0_my": nc.dram_tensor("n0_my", [TOK, D], BF16, kind="ExternalOutput").ap(),
        "wdeq_my": nc.dram_tensor("wdeq_my", [512, D], BF16, kind="ExternalOutput").ap(),
    }
    emit_p0(nc, io)
    return nc



def emit_p6(nc, io):
    n2 = io["n2_full"]
    wqk = io["w1qk"]
    wv = io["w1v"]
    bqk = io["b1qk"]
    bv = io["b1v2"]
    qk1 = io["qk1"]
    v1 = io["v1fm"]
    NF = 48

    for th in (0, 1):
        with ExitStack() as st:
            def sb(name, shape, dt):
                return st.enter_context(nc.sbuf_tensor(f"p6{th}{name}", shape, dt))
            rhsf = [sb(f"r{i}", [128, 2048], BF16) for i in range(32)]
            wbuf = [sb(f"w{j}", [128, 4096], BF16) for j in range(2)]
            evb = [sb(f"e{j}", [128, 512], BF16) for j in range(8)]
            bcol = sb("bc", [128, 48], F32)
            ps = [st.enter_context(nc.psum_tensor(f"p6{th}ps{j}", [128, 512], F32))
                  for j in range(8)]
            sems = mk_sems(st, nc, [f"p6{th}{k}" for k in ["ldr", "wl", "wf", "ce", "se"]])
            ldr, wl, wf, ce, se = (sems[f"p6{th}{k}"] for k in ["ldr", "wl", "wf", "ce", "se"])
            blk = st.enter_context(nc.Block())
            tsl = slice(th * 2048, (th + 1) * 2048)

            @blk.sync
            def _(sy):
                sy.dma_start(out=bcol[:, 0:32], in_=bqk).then_inc(ldr.h, 16)
                sy.dma_start(out=bcol[:, 32:48], in_=bv).then_inc(ldr.h, 16)
                for i in range(32):
                    sy.dma_start(out=rhsf[i][:], in_=n2[tsl, i * 128:(i + 1) * 128],
                                 transpose=True).then_inc(ldr.h, 16)

            @blk.gpsimd
            def _(g):
                for f in range(NF):
                    src_ = wqk[:, f * 128:(f + 1) * 128] if f < 32 else \
                        wv[:, (f - 32) * 128:(f - 31) * 128]
                    if f >= 2:
                        g.wait_ge(wf.h, f - 1)
                    s3 = src_.rearrange("(i p) c -> p i c", p=128)
                    d3 = wbuf[f % 2][:].rearrange("p (i c) -> p i c", c=128)
                    for q in range(4):
                        g.dma_start(out=d3[:, q * 8:(q + 1) * 8, :],
                                    in_=s3[:, q * 8:(q + 1) * 8, :]).then_inc(wl.h, 16)

            @blk.tensor
            def _(t):
                t.wait_ge(ldr.h, 16 * 34)
                for f in range(NF):
                    t.wait_ge(wl.h, 64 * (f + 1))
                    if f >= 2:
                        t.wait_ge(ce.h, 4 * (f - 1))
                    for i in range(32):
                        lhsT = wbuf[f % 2][:, i * 128:(i + 1) * 128]
                        for tt in range(4):
                            ti = t.matmul(ps[(f % 2) * 4 + tt][:],
                                          lhsT, rhsf[i][:, tt * 512:(tt + 1) * 512],
                                          start=(i == 0), stop=(i == 31))
                    wf.inc(ti)

            @blk.scalar
            def _(s):
                for f in range(NF):
                    s.wait_ge(wf.h, f + 1)
                    if f >= 2:
                        s.wait_ge(se.h, 16 * 4 * (f - 1))
                    for tt in range(4):
                        si = s.activation(out=evb[(f % 2) * 4 + tt][:],
                                          in_=ps[(f % 2) * 4 + tt][:], func=AF.Identity,
                                          bias=bcol[:, f:f + 1])
                        ce.inc(si)
                    for tt in range(4):
                        if f < 32:
                            dst = qk1[f * 128:(f + 1) * 128,
                                      th * 2048 + tt * 512: th * 2048 + (tt + 1) * 512]
                        else:
                            dst = v1[(f - 32) * 128:(f - 31) * 128,
                                     th * 2048 + tt * 512: th * 2048 + (tt + 1) * 512]
                        s.dma_start(out=dst, in_=evb[(f % 2) * 4 + tt][:]).then_inc(se.h, 16)

            @blk.sync
            def _(sy):
                sy.wait_ge(se.h, 16 * 4 * NF)
    return nc



def _p7_windows():
    out = []
    for j in range(16):
        slope = float(_l1_slopes()[2 * j + 1])
        W = int(math.ceil(90.0 / slope))
        kept = {}
        for qt in range(8):
            nkc = 4 * (qt + 1)
            kept[qt] = [kc for kc in range(nkc) if (kc + 1) * 128 - 1 >= qt * 512 - W]
        out.append(kept)
    return out


P7_FAST0 = 9
P7_SKEW = 4


def _p7_tiles():
    wins = _p7_windows()
    ld = tm = dv = ca = se = 0
    ms = []
    dv += 1
    ld += 16
    for s_ in range(16):
        m = {"fast": s_ >= P7_FAST0}
        fast = m["fast"]
        ld += 16 * 36
        m["ld_slot"] = ld
        ca += 1; m["ca_ksq"] = ca
        m["tm_krow"] = []; m["ca_krow"] = []
        for j in range(8):
            tm += 1; m["tm_krow"].append(tm)
            ca += 1; m["ca_krow"].append(ca)
        ca += 1; m["ca_qsq"] = ca
        m["tm_qrow"] = []; m["ca_qrow"] = []
        for j in range(8):
            tm += 1; m["tm_qrow"].append(tm)
            ca += 1; m["ca_qrow"].append(ca)
        dv += 1; m["dv_km"] = dv
        ca += 1; m["ca_kms"] = ca
        dv += 1; m["dv_trow"] = dv
        ca += 1; m["ca_rq"] = ca
        if fast:
            dv += 1; m["dv_crow"] = dv
            tm += 1; m["tm_cb"] = tm
            ca += 1; m["ca_cb"] = ca
        tl = []
        m["qt"] = {}
        kept = wins[s_]
        gidx = 0
        for qt in range(8):
            q = {}
            m["qt"][qt] = q
            if fast:
                dv += 1; q["dv_badj"] = dv
            else:
                tm += 1; q["tm_rq"] = tm
                ca += 1; q["ca_rqb"] = ca
                dv += 1; q["dv_msk"] = dv
            kcs = kept[qt]
            nk = len(kcs)
            tiles = []
            for i2, kc in enumerate(kcs):
                t_ = {"qt": qt, "kc": kc, "i2": i2, "p": gidx % 4,
                      "diag": kc >= 4 * qt, "first": i2 == 0, "last": i2 == nk - 1}
                t_["slowpath"] = (not fast) or t_["diag"]
                tiles.append(t_)
                gidx += 1
            q["tiles"] = tiles
            for j in range(nk + P7_SKEW):
                if j < nk:
                    tm += 1; tiles[j]["tm_sc"] = tm
                jj = j - P7_SKEW
                if jj >= 0:
                    tm += 1; tiles[jj]["tm_cd"] = tm
            for t_ in tiles:
                if t_["slowpath"]:
                    dv += 1; t_["dv_tmp"] = dv
                ca += 1; t_["ca_exp"] = ca
            q["tm_den"] = tiles[-1]["tm_cd"]
            dv += 1; q["dv_rec"] = dv
            tm += 1; q["tm_rec"] = tm
            ca += 1; q["ca_recb"] = ca
            dv += 1; q["dv_ctxn"] = dv
            se += 16; q["se_ctx"] = se
            tl.extend(tiles)
        m["tl"] = tl
        m["tm_end"] = tm
        m["ca_end"] = ca
        m["dv_end"] = dv
        ms.append(m)
    return ms


def _psfree_wait(eng, m, t_):
    tl = m["tl"]
    idx = None
    for k, tt in enumerate(tl):
        if tt is t_:
            idx = k
            break
    if idx is None or idx < 4:
        return
    prev = tl[idx - 4]
    if prev["slowpath"]:
        eng.wait_ge.__self__


def emit_p7(nc, io):
    qk1 = io["qk1"]
    v1 = io["v1fm"]
    ctx1 = io["ctx1"]
    slkpT = io["sl_kpos"]
    slqpT = io["slqp"]
    dmaskT = io["dmask"]
    SCAL = float(HD) ** -0.5
    ms = _p7_tiles()

    with ExitStack() as st:
        def sb(name, shape, dt):
            return st.enter_context(nc.sbuf_tensor(f"p7{name}", shape, dt))
        qf = sb("qf", [128, S], BF16)
        kf = sb("kf", [128, S], BF16)
        vts = sb("vts", [128, S], BF16)
        sqb = sb("sqb", [128, S], BF16)
        slkp = sb("slkp", [128, 32], F32)
        slqp = sb("slqp", [1, S], F32)
        dmsk = sb("dmsk", [128, 2048], F32)
        kkrow = sb("kkrow", [1, S], F32)
        qrt = sb("qrt", [1, S], F32)
        km = sb("km", [1, 1], F32)
        kms = sb("kms", [1, 1], F32)
        trow = sb("trow", [1, S], F32)
        rqrow = sb("rqrow", [1, S], F32)
        crow = sb("crow", [1, 8], F32)
        cb = sb("cb", [128, 8], F32)
        badj = sb("badj", [128, 32], F32)
        rqall = sb("rqall", [128, 2560], F32)
        stf = [sb(f"stf{j}", [128, 512], F32) for j in range(4)]
        eb = [sb(f"eb{j}", [128, 512], BF16) for j in range(4)]
        recrow = sb("recrow", [1, 512], F32)
        recb = sb("recb", [128, 512], F32)
        ctxn = [sb(f"ctxn{j}", [128, 512], BF16) for j in range(2)]
        onesb = sb("onesb", [128, 1], BF16)
        onesf = sb("onesf", [1, 128], F32)
        ps_s = [st.enter_context(nc.psum_tensor(f"p7s{j}", [128, 512], F32)) for j in range(4)]
        ps_ctx = st.enter_context(nc.psum_tensor("p7ctx", [128, 512], F32))
        ps_den = st.enter_context(nc.psum_tensor("p7den", [1, 512], F32))
        ps_r = st.enter_context(nc.psum_tensor("p7r", [128, 512], F32))
        sems = mk_sems(st, nc, [f"p7{k}" for k in ["ld", "tm", "dv", "ca", "se"]])
        ld, tm, dv, ca, se = (sems[f"p7{k}"] for k in ["ld", "tm", "dv", "ca", "se"])
        blk = st.enter_context(nc.Block())

        def psfree(m, t_):
            tl = m["tl"]
            idx = tl.index(t_)
            if idx < 4:
                return None
            prev = tl[idx - 4]
            if prev["slowpath"]:
                return ("dv", prev["dv_tmp"])
            return ("ca", prev["ca_exp"])

        def ebfree(m, t_):
            tl = m["tl"]
            idx = tl.index(t_)
            if idx < 4:
                return None
            return ("tm", tl[idx - 4]["tm_cd"])

        @blk.sync
        def _(sy):
            sy.dma_start(out=dmsk[:].rearrange("p (a n) -> p a n", a=4),
                         in_=dmaskT.rearrange("a p n -> p a n")).then_inc(ld.h, 16)
            for s_ in range(16):
                if s_ >= 1:
                    sy.wait_ge(tm.h, ms[s_ - 1]["tm_end"])
                    sy.wait_ge(ca.h, ms[s_ - 1]["ca_end"])
                    sy.wait_ge(dv.h, ms[s_ - 1]["dv_end"])
                r0 = s_ * 256
                sy.dma_start(out=qf[:], in_=qk1[r0:r0 + 128, :]).then_inc(ld.h, 16)
                sy.dma_start(out=kf[:], in_=qk1[r0 + 128:r0 + 256, :]).then_inc(ld.h, 16)
                for kc in range(32):
                    sy.dma_start(out=vts[:, kc * 128:(kc + 1) * 128],
                                 in_=v1[s_ * 128:(s_ + 1) * 128, kc * 128:(kc + 1) * 128],
                                 transpose=True).then_inc(ld.h, 16)
                sy.dma_start(out=slkp[:], in_=slkpT[s_]).then_inc(ld.h, 16)
                sy.dma_start(out=slqp[:], in_=slqpT[s_:s_ + 1, :]).then_inc(ld.h, 16)

        @blk.vector
        def _(v):
            v.memset(onesb[:], 1.0)
            v.memset(onesf[:], 1.0)
            dinc(v, dv)
            for s_ in range(16):
                m = ms[s_]
                fast = m["fast"]
                v.wait_ge(ca.h, m["ca_krow"][7])
                v.reduce_max(out=km[:], in_=kkrow[:], axis=AX.X)
                dinc(v, dv)
                v.wait_ge(ca.h, m["ca_kms"])
                v.wait_ge(ca.h, m["ca_qrow"][7])
                v.scalar_tensor_tensor(out=trow[:], in0=qrt[:], scalar=kms[:],
                                       in1=slqp[:], op0=ALU.mult, op1=ALU.add)
                dinc(v, dv)
                if fast:
                    v.reduce_max(out=crow[:].rearrange("o (a b) -> o a b", b=1),
                                 in_=trow[:].rearrange("o (a b) -> o a b", b=512),
                                 axis=AX.X)
                    dinc(v, dv)
                for qt in range(8):
                    q = m["qt"][qt]
                    tiles = q["tiles"]
                    if fast:
                        v.wait_ge(ca.h, m["ca_cb"])
                        if qt >= 1:
                            v.wait_ge(ca.h, m["qt"][qt - 1]["tiles"][-1]["ca_exp"])
                        v.tensor_scalar(out=badj[:], in0=slkp[:],
                                        scalar1=cb[:, qt:qt + 1], scalar2=None,
                                        op0=ALU.subtract)
                        dinc(v, dv)
                    else:
                        v.wait_ge(ca.h, q["ca_rqb"])
                        for a in range(4):
                            v.tensor_tensor(out=rqall[:, (a + 1) * 512:(a + 2) * 512],
                                            in0=rqall[:, 0:512],
                                            in1=dmsk[:, a * 512:(a + 1) * 512], op=ALU.add)
                        dinc(v, dv)
                    for t_ in tiles:
                        if not t_["slowpath"]:
                            continue
                        v.wait_ge(tm.h, t_["tm_sc"])
                        pf = psfree(m, t_)
                        kc = t_["kc"]
                        if fast:
                            in1 = dmsk[:, (kc - 4 * qt) * 512:(kc - 4 * qt + 1) * 512]
                        else:
                            sel = 0 if kc < 4 * qt else 1 + (kc - 4 * qt)
                            in1 = rqall[:, sel * 512:(sel + 1) * 512]
                        ef = ebfree(m, t_)
                        if ef is not None:
                            pass
                        v.tensor_tensor(out=stf[t_["p"]][:], in0=ps_s[t_["p"]][:],
                                        in1=in1, op=ALU.add)
                        dinc(v, dv)
                    v.wait_ge(tm.h, q["tm_den"])
                    v.reciprocal(out=recrow[:], in_=ps_den[0:1, :])
                    dinc(v, dv)
                    v.wait_ge(ca.h, q["ca_recb"])
                    if qt >= 2:
                        v.wait_ge(se.h, m["qt"][qt - 2]["se_ctx"])
                    v.tensor_tensor(out=ctxn[qt % 2][:], in0=ps_ctx[:], in1=recb[:],
                                    op=ALU.mult)
                    dinc(v, dv)

        @blk.scalar
        def _(s):
            for s_ in range(16):
                m = ms[s_]
                fast = m["fast"]
                s.wait_ge(ld.h, m["ld_slot"])
                s.activation(out=sqb[:], in_=kf[:], func=AF.Square)
                dinc(s, ca)
                for j in range(8):
                    s.wait_ge(tm.h, m["tm_krow"][j])
                    s.activation(out=kkrow[:, j * 512:(j + 1) * 512], in_=ps_den[0:1, :],
                                 func=AF.Copy)
                    dinc(s, ca)
                s.activation(out=sqb[:], in_=qf[:], func=AF.Square)
                dinc(s, ca)
                for j in range(8):
                    s.wait_ge(tm.h, m["tm_qrow"][j])
                    s.activation(out=qrt[:, j * 512:(j + 1) * 512], in_=ps_den[0:1, :],
                                 func=AF.Sqrt)
                    dinc(s, ca)
                s.wait_ge(dv.h, m["dv_km"])
                s.activation(out=kms[:], in_=km[:], func=AF.Sqrt, scale=SCAL * SCAL)
                dinc(s, ca)
                s.wait_ge(dv.h, m["dv_trow"])
                s.activation(out=rqrow[:], in_=trow[:], func=AF.Copy, scale=-1.0 / SCAL)
                dinc(s, ca)
                if fast:
                    s.wait_ge(tm.h, m["tm_cb"])
                    s.activation(out=cb[:], in_=ps_r[:, 0:8], func=AF.Copy)
                    dinc(s, ca)
                for qt in range(8):
                    q = m["qt"][qt]
                    tiles = q["tiles"]
                    if fast:
                        s.wait_ge(dv.h, q["dv_badj"])
                    else:
                        s.wait_ge(tm.h, q["tm_rq"])
                        if qt >= 1:
                            prevq = m["qt"][qt - 1]
                            lastslow = [t2 for t2 in prevq["tiles"] if t2["slowpath"]]
                            if lastslow:
                                s.wait_ge(dv.h, lastslow[-1]["dv_tmp"])
                        s.activation(out=rqall[:, 0:512], in_=ps_r[:], func=AF.Copy)
                        dinc(s, ca)
                    for t_ in tiles:
                        kc = t_["kc"]
                        ef = ebfree(m, t_)
                        if ef is not None:
                            s.wait_ge(tm.h, ef[1])
                        if t_["slowpath"]:
                            s.wait_ge(dv.h, t_["dv_tmp"])
                            bias_ap = badj[:, kc:kc + 1] if fast else slkp[:, kc:kc + 1]
                            s.activation(out=eb[t_["p"]][:], in_=stf[t_["p"]][:],
                                         func=AF.Exp, scale=SCAL, bias=bias_ap)
                        else:
                            s.wait_ge(tm.h, t_["tm_sc"])
                            s.activation(out=eb[t_["p"]][:], in_=ps_s[t_["p"]][:],
                                         func=AF.Exp, scale=SCAL, bias=badj[:, kc:kc + 1])
                        dinc(s, ca)
                    s.wait_ge(tm.h, q["tm_rec"])
                    s.activation(out=recb[:], in_=ps_r[:], func=AF.Copy)
                    dinc(s, ca)

        @blk.tensor
        def _(t):
            t.wait_ge(dv.h, 1)
            for s_ in range(16):
                m = ms[s_]
                fast = m["fast"]
                t.wait_ge(ld.h, m["ld_slot"])
                t.wait_ge(ca.h, m["ca_ksq"])
                for j in range(8):
                    if j >= 1:
                        t.wait_ge(ca.h, m["ca_krow"][j - 1])
                    ti = t.matmul(ps_den[:], onesb[:], sqb[:, j * 512:(j + 1) * 512],
                                  start=True, stop=True)
                    tm.inc(ti)
                t.wait_ge(ca.h, m["ca_qsq"])
                for j in range(8):
                    t.wait_ge(ca.h, m["ca_qrow"][j - 1] if j >= 1 else m["ca_krow"][7])
                    ti = t.matmul(ps_den[:], onesb[:], sqb[:, j * 512:(j + 1) * 512],
                                  start=True, stop=True)
                    tm.inc(ti)
                t.wait_ge(ca.h, m["ca_rq"])
                if fast:
                    t.wait_ge(dv.h, m["dv_crow"])
                    ti = t.matmul(ps_r[:, 0:8], onesf[:], crow[:], start=True, stop=True)
                    tm.inc(ti)
                for qt in range(8):
                    q = m["qt"][qt]
                    tiles = q["tiles"]
                    nk = len(tiles)
                    if not fast:
                        if qt >= 1:
                            t.wait_ge(ca.h, m["qt"][qt - 1]["ca_recb"])
                        ti = t.matmul(ps_r[:], onesf[:], rqrow[:, qt * 512:(qt + 1) * 512],
                                      start=True, stop=True)
                        tm.inc(ti)
                    for j in range(nk + P7_SKEW):
                        if j < nk:
                            t_ = tiles[j]
                            pf = psfree(m, t_)
                            if pf is not None:
                                t.wait_ge(dv.h if pf[0] == "dv" else ca.h, pf[1])
                            kc = t_["kc"]
                            ti = t.matmul(ps_s[t_["p"]][:], kf[:, kc * 128:(kc + 1) * 128],
                                          qf[:, qt * 512:(qt + 1) * 512],
                                          start=True, stop=True)
                            tm.inc(ti)
                        jj = j - P7_SKEW
                        if jj >= 0:
                            t_ = tiles[jj]
                            t.wait_ge(ca.h, t_["ca_exp"])
                            if jj == 0 and qt >= 1:
                                t.wait_ge(dv.h, m["qt"][qt - 1]["dv_ctxn"])
                                t.wait_ge(dv.h, m["qt"][qt - 1]["dv_rec"])
                            kc = t_["kc"]
                            t.matmul(ps_ctx[:], vts[:, kc * 128:(kc + 1) * 128], eb[t_["p"]][:],
                                     start=t_["first"], stop=t_["last"])
                            ti = t.matmul(ps_den[:], onesb[:], eb[t_["p"]][:],
                                          start=t_["first"], stop=t_["last"])
                            tm.inc(ti)
                    t.wait_ge(dv.h, q["dv_rec"])
                    ti = t.matmul(ps_r[:], onesf[:], recrow[:], start=True, stop=True)
                    tm.inc(ti)

        @blk.gpsimd
        def _(g):
            for s_ in range(16):
                m = ms[s_]
                for qt in range(8):
                    g.wait_ge(dv.h, m["qt"][qt]["dv_ctxn"])
                    g.dma_start(out=ctx1[s_ * 128:(s_ + 1) * 128, qt * 512:(qt + 1) * 512],
                                in_=ctxn[qt % 2][:]).then_inc(se.h, 16)

        @blk.sync
        def _(sy):
            sy.wait_ge(se.h, 16 * 8 * 16)
    return nc


def build_p7():
    nc = bass.Bass("TRN2", target_bir_lowering=False, debug=False)
    io = {
        "qk1": nc.dram_tensor("qk1", [4096, S], BF16, kind="ExternalInput").ap(),
        "v1fm": nc.dram_tensor("v1fm", [2048, S], BF16, kind="ExternalInput").ap(),
        "sl_kpos": nc.dram_tensor("sl_kpos", [16, 128, 32], F32, kind="ExternalInput").ap(),
        "slqp": nc.dram_tensor("slqp", [16, S], F32, kind="ExternalInput").ap(),
        "dmask": nc.dram_tensor("dmask", [4, 128, 512], F32, kind="ExternalInput").ap(),
        "ctx1": nc.dram_tensor("ctx1", [2048, S], BF16, kind="ExternalOutput").ap(),
    }
    emit_p7(nc, io)
    return nc



def emit_reshard(nc, tag, copies, off_ap):
    with ExitStack() as st:
        osb = st.enter_context(nc.sbuf_tensor(f"rs{tag}o", [1, 1], I32))
        sem = st.enter_context(nc.semaphore(f"rs{tag}s"))
        blk = st.enter_context(nc.Block())

        @blk.gpsimd
        def _(g):
            g.dma_start(out=osb[:], in_=off_ap).then_inc(sem, 16)
            g.wait_ge(sem, 16)
            rb = nc.gpsimd.alloc_register(f"rs{tag}r")
            r2 = nc.gpsimd.alloc_register(f"rs{tag}r2")
            g.reg_load(rb, osb[0:1, 0:1])
            n = 16
            for dst, srct, pat, soff in copies:
                g.reg_add(r2, rb, soff)
                g.dma_start(out=dst, in_=bass.AP(srct, r2, pat)).then_inc(sem, 16)
                n += 16
            g.wait_ge(sem, n)
    return nc


def emit_coll_reshard(nc, tag, kind, op, groups, in_ap, out_ap, copies, off_ap):
    with ExitStack() as st:
        osb = st.enter_context(nc.sbuf_tensor(f"cr{tag}o", [1, 1], I32))
        sem = st.enter_context(nc.semaphore(f"cr{tag}s"))
        blk = st.enter_context(nc.Block())

        @blk.gpsimd
        def _(g):
            g.dma_start(out=osb[:], in_=off_ap).then_inc(sem, 16)
            g.collective_compute(kind, op, replica_groups=groups,
                                 ins=[in_ap.opt()], outs=[out_ap.opt()]).then_inc(sem)
            g.wait_ge(sem, 17)
            rb = nc.gpsimd.alloc_register(f"cr{tag}r")
            r2 = nc.gpsimd.alloc_register(f"cr{tag}r2")
            g.reg_load(rb, osb[0:1, 0:1])
            n = 17
            for dst, srct, pat, soff in copies:
                g.reg_add(r2, rb, soff)
                g.dma_start(out=dst, in_=bass.AP(srct, r2, pat)).then_inc(sem, 16)
                n += 16
            g.wait_ge(sem, n)
    return nc


def emit_coll(nc, tag, kind, op, groups, in_ap, out_ap):
    with ExitStack() as st:
        sem = st.enter_context(nc.semaphore(f"cc{tag}"))
        blk = st.enter_context(nc.Block())

        @blk.gpsimd
        def _(g):
            g.collective_compute(kind, op, replica_groups=groups,
                                 ins=[in_ap.opt()], outs=[out_ap.opt()]).then_inc(sem)
            g.wait_ge(sem, 1)
    return nc


def emit_p9z(nc, io):
    n3 = io["n3_my"]
    W1T = io["W1T"]
    b1 = io["b1"]
    A1 = io["A1"]
    LiT = io["LiMEsT"]
    identf = io["identf"]
    z_out = io["z_my"]
    h8_out = io["h8_my"]
    ds_out = io["ds_my"]
    mx_out = io["mx_my"]

    with ExitStack() as st:
        def sb(name, shape, dt):
            return st.enter_context(nc.sbuf_tensor(f"p9z{name}", shape, dt))
        n3f = [sb(f"n{i}", [128, 2048], BF16) for i in range(32)]
        a1sb = sb("a1", [128, 1024], BF16)
        w8all = sb("w8", [128, 256], BF16)
        litb = sb("lit", [32, 8], BF16)
        b8row = sb("b8", [1, 8], BF16)
        ones1 = sb("one", [1, 128], BF16)
        idf = sb("idf", [128, 128], F32)
        zfm = sb("zfm", [32, 2048], F32)
        zbf = sb("zbf", [32, 2048], BF16)
        h8s = sb("h8s", [128, 64], F32)
        dss = sb("dss", [128, 64], F32)
        mhd = sb("mhd", [128, 2], F32)
        mx2 = sb("mx2", [2, 1], F32)
        psz = st.enter_context(nc.psum_tensor("p9zz", [32, 512], F32))
        ps8 = st.enter_context(nc.psum_tensor("p9z8", [128, 8], F32))
        pst = st.enter_context(nc.psum_tensor("p9zt", [2, 128], F32))
        sems = mk_sems(st, nc, [f"p9z{k}" for k in ["ld", "tm", "dv", "ca", "se"]])
        ld, tm, dv, ca, se = (sems[f"p9z{k}"] for k in ["ld", "tm", "dv", "ca", "se"])
        blk = st.enter_context(nc.Block())

        @blk.sync
        def _(sy):
            for i in range(32):
                sy.dma_start(out=n3f[i][:], in_=n3[:, i * 128:(i + 1) * 128],
                             transpose=True).then_inc(ld.h, 16)
            sy.dma_start(out=idf[:], in_=identf).then_inc(ld.h, 16)

        @blk.gpsimd
        def _(g):
            for i in range(32):
                g.dma_start(out=a1sb[:, i * 32:(i + 1) * 32],
                            in_=A1[i * 128:(i + 1) * 128, :]).then_inc(ld.h, 16)
                g.dma_start(out=w8all[:, i * 8:(i + 1) * 8],
                            in_=W1T[i * 128:(i + 1) * 128, 0:8]).then_inc(ld.h, 16)
            g.dma_start(out=litb[:], in_=LiT).then_inc(ld.h, 16)
            g.dma_start(out=b8row[:], in_=b1[:, 0:8]).then_inc(ld.h, 16)
            g.wait_ge(ca.h, 8)
            g.dma_start(out=z_out, in_=zfm[:]).then_inc(se.h, 16)
            g.wait_ge(ca.h, 16)
            g.dma_start(out=h8_out, in_=h8s[:]).then_inc(se.h, 16)
            g.wait_ge(ca.h, 24)
            g.dma_start(out=ds_out, in_=dss[:]).then_inc(se.h, 16)
            g.wait_ge(dv.h, 3)
            g.dma_start(out=mx_out, in_=mx2[:]).then_inc(se.h, 16)

        @blk.vector
        def _(v):
            v.memset(ones1[:], 1.0)
            dinc(v, dv)
            v.wait_ge(ca.h, 24)
            v.reduce_max(out=mhd[:, 0:1], in_=h8s[:], axis=AX.X, apply_absolute_value=True)
            v.reduce_max(out=mhd[:, 1:2], in_=dss[:], axis=AX.X, apply_absolute_value=True)
            dinc(v, dv)
            v.wait_ge(tm.h, 21)
            v.reduce_max(out=mx2[:], in_=pst[:], axis=AX.X)
            dinc(v, dv)

        @blk.tensor
        def _(t):
            t.wait_ge(ld.h, 16 * 99)
            t.wait_ge(dv.h, 1)
            for t4 in range(4):
                if t4 >= 1:
                    t.wait_ge(ca.h, 2 * t4)
                for i in range(32):
                    ti = t.matmul(psz[:], a1sb[:, i * 32:(i + 1) * 32],
                                  n3f[i][:, t4 * 512:(t4 + 1) * 512],
                                  start=(i == 0), stop=(i == 31))
                tm.inc(ti)
            for ac in range(8):
                if ac >= 1:
                    t.wait_ge(ca.h, 8 + ac)
                for i in range(32):
                    t.matmul(ps8[:], n3f[i][:, 1 + ac * 256:ac * 256 + 256:2],
                             w8all[:, i * 8:(i + 1) * 8], start=(i == 0), stop=False)
                ti = t.matmul(ps8[:], ones1[:], b8row[:], start=False, stop=True)
                tm.inc(ti)
            for ac in range(8):
                t.wait_ge(ca.h, 16 + ac)
                ti = t.matmul(ps8[:], zbf[:, 1 + ac * 256:ac * 256 + 256:2],
                              litb[:], start=True, stop=True)
                tm.inc(ti)
            t.wait_ge(dv.h, 2)
            ti = t.matmul(pst[:], mhd[:], idf[:], start=True, stop=True)
            tm.inc(ti)

        @blk.scalar
        def _(s):
            for t4 in range(4):
                s.wait_ge(tm.h, t4 + 1)
                s.activation(out=zfm[:, t4 * 512:(t4 + 1) * 512], in_=psz[:], func=AF.Copy)
                dinc(s, ca)
                s.activation(out=zbf[:, t4 * 512:(t4 + 1) * 512], in_=psz[:], func=AF.Copy)
                dinc(s, ca)
            for ac in range(8):
                s.wait_ge(tm.h, 5 + ac)
                s.activation(out=h8s[:, ac * 8:(ac + 1) * 8], in_=ps8[:], func=AF.Copy)
                dinc(s, ca)
            for ac in range(8):
                s.wait_ge(tm.h, 13 + ac)
                s.activation(out=dss[:, ac * 8:(ac + 1) * 8], in_=ps8[:], func=AF.Copy)
                dinc(s, ca)

        @blk.sync
        def _(sy):
            sy.wait_ge(se.h, 64)
    return nc


def emit_p9b(nc, io, mx_pre=False, mx_coll=None):
    mxall = io["mx_all"]
    h8 = io["h8_my"]
    dsi = io["ds_my"]
    z_in = io["z_my"]
    limes = io["LiMEs"]
    bmat = io["Bmat"]
    x3p = io["x3p_my"]
    identb = io["ident"]
    x4 = io["x4_my"]
    GSC = 0.5

    with ExitStack() as st:
        def sb(name, shape, dt):
            return st.enter_context(nc.sbuf_tensor(f"p9b{name}", shape, dt))
        mxs = sb("mx", [1, 16], F32)
        h8s = sb("h8", [128, 64], F32)
        dss = sb("ds", [128, 64], F32)
        zfm = sb("z", [32, 2048], F32)
        limb = sb("lim", [8, 32], BF16)
        bmb = sb("bm", [32, 4096], BF16)
        idb = sb("idb", [128, 128], BF16)
        onesf = sb("onef", [1, 128], F32)
        mx2 = sb("m2", [1, 2], F32)
        cp1 = sb("cp1", [1, 2], F32)
        mx2c = sb("m2c", [1, 2], F32)
        cp2 = sb("cp2", [1, 2], F32)
        rcp = sb("rcp", [1, 2], F32)
        sc2 = sb("sc2", [1, 2], F32)
        scb = sb("scb", [128, 2], F32)
        ds2 = sb("ds2", [128, 64], F32)
        lg = sb("lg", [128, 64], F32)
        elg = sb("elg", [128, 64], F32)
        sm8 = sb("sm8", [128, 8], F32)
        cp8 = sb("cp8", [128, 8], F32)
        rp8 = sb("rp8", [128, 8], F32)
        probs = sb("pr", [128, 64], F32)
        m1 = sb("m1", [128, 8], F32)
        eqb = sb("eq", [128, 64], F32)
        cnt = sb("cnt", [128, 8], F32)
        mskd = sb("mskd", [128, 64], F32)
        m2b = sb("m2b", [128, 8], F32)
        d12 = sb("d12", [128, 8], F32)
        cond = sb("cond", [128, 8], F32)
        t1b = sb("t1b", [128, 8], F32)
        thr = sb("thr", [128, 8], F32)
        pdf = sb("pd", [128, 64], F32)
        sgm = sb("sg", [128, 64], F32)
        wu = sb("wu", [128, 64], F32)
        ws = sb("ws", [128, 8], F32)
        wsp = sb("wsp", [128, 8], F32)
        cpw = sb("cpw", [128, 8], F32)
        rw = sb("rw", [128, 8], F32)
        wgt = sb("wgt", [128, 64], BF16)
        wfm = sb("wfm", [8, 1024], BF16)
        pm = sb("pm", [32, 1024], F32)
        pf = sb("pf", [32, 2048], F32)
        zpb = sb("zpb", [32, 2048], BF16)
        x3t = [sb(f"x{j}", [128, 4096], F32) for j in range(2)]
        x4t = [sb(f"y{j}", [128, 4096], F32) for j in range(2)]
        psc = st.enter_context(nc.psum_tensor("p9bc", [128, 2], F32))
        pwt = st.enter_context(nc.psum_tensor("p9bw", [8, 128], BF16))
        ppm = st.enter_context(nc.psum_tensor("p9bp", [32, 512], F32))
        psd = [st.enter_context(nc.psum_tensor(f"p9bd{j}", [128, 512], F32)) for j in range(4)]
        sems = mk_sems(st, nc, [f"p9b{k}" for k in ["ld", "tm", "dv", "ca", "se", "lx", "cc"]])
        ld, tm, dv, ca, se, lx, ccs = (sems[f"p9b{k}"]
                                       for k in ["ld", "tm", "dv", "ca", "se", "lx", "cc"])
        blk = st.enter_context(nc.Block())

        if mx_coll is not None:
            @blk.scalar
            def _(s_):
                pass

            @blk.gpsimd
            def _(g_):
                g_.collective_compute("AllReduce", ALU.max,
                                      replica_groups=mx_coll[0],
                                      ins=[mx_coll[1].opt()],
                                      outs=[mx_coll[2].opt()]).then_inc(ccs.h)
                g_.wait_ge(ccs.h, 1)

        def b3(ap):
            return ap.rearrange("p (g o) -> p g o", o=1).broadcast_to([128, 8, 8])

        def g3(ap):
            return ap.rearrange("p (g e) -> p g e", e=8)

        @blk.sync
        def _(sy):
            if mx_coll is not None:
                sy.wait_ge(ccs.h, 1)
            sy.dma_start(out=mxs[:, 0:2] if mx_pre else mxs[:], in_=mxall).then_inc(ld.h, 16)
            sy.dma_start(out=h8s[:], in_=h8).then_inc(ld.h, 16)
            sy.dma_start(out=dss[:], in_=dsi).then_inc(ld.h, 16)
            sy.dma_start(out=zfm[:], in_=z_in).then_inc(ld.h, 16)
            sy.dma_start(out=idb[:], in_=identb).then_inc(ld.h, 16)
            for tc in range(16):
                if tc >= 2:
                    sy.wait_ge(dv.h, 17 + 2 * (tc - 2) + 2)
                sy.dma_start(out=x3t[tc % 2][:],
                             in_=x3p[tc * 128:(tc + 1) * 128, :]).then_inc(lx.h, 16)

        @blk.gpsimd
        def _(g):
            g.dma_start(out=limb[:], in_=limes).then_inc(ld.h, 16)
            g.dma_start(out=bmb[:], in_=bmat).then_inc(ld.h, 16)
            for tc in range(16):
                g.wait_ge(dv.h, 17 + 2 * tc + 2)
                g.dma_start(out=x4[tc * 128:(tc + 1) * 128, :],
                            in_=x4t[tc % 2][:]).then_inc(se.h, 16)

        @blk.vector
        def _(v):
            v.memset(onesf[:], 1.0)
            v.wait_ge(ld.h, 16 * 7)
            if mx_pre:
                v.tensor_copy(out=mx2[:], in_=mxs[:, 0:2])
            else:
                v.reduce_max(out=mx2[:].rearrange("o (g e) -> o g e", e=1),
                             in_=mxs[:].rearrange("o (g e) -> o g e", e=8), axis=AX.X)
            dinc(v, dv)
            v.wait_ge(ca.h, 1)
            v.tensor_scalar_max(out=mx2c[:], in0=cp1[:], scalar1=1e-6)
            dinc(v, dv)
            v.wait_ge(ca.h, 2)
            v.reciprocal(out=rcp[:], in_=cp2[:])
            dinc(v, dv)
            v.wait_ge(ca.h, 4)
            v.tensor_scalar(out=ds2[:], in0=dss[:], scalar1=scb[:, 1:2], scalar2=None,
                            op0=ALU.mult)
            v.drain()
            v.scalar_tensor_tensor(out=lg[:], in0=h8s[:], scalar=scb[:, 0:1], in1=ds2[:],
                                   op0=ALU.mult, op1=ALU.add)
            dinc(v, dv)
            v.wait_ge(ca.h, 5)
            v.reduce_sum(out=sm8[:].rearrange("p (g o) -> p g o", o=1), in_=g3(elg[:]),
                         axis=AX.X)
            dinc(v, dv)
            v.wait_ge(ca.h, 6)
            v.reciprocal(out=rp8[:], in_=cp8[:])
            v.drain()
            v.tensor_tensor(out=g3(probs[:]), in0=g3(elg[:]), in1=b3(rp8[:]), op=ALU.mult)
            v.drain()
            v.reduce_max(out=m1[:].rearrange("p (g o) -> p g o", o=1), in_=g3(probs[:]),
                         axis=AX.X)
            v.drain()
            v.tensor_tensor(out=g3(eqb[:]), in0=g3(probs[:]), in1=b3(m1[:]), op=ALU.is_ge)
            v.drain()
            v.reduce_sum(out=cnt[:].rearrange("p (g o) -> p g o", o=1), in_=g3(eqb[:]),
                         axis=AX.X)
            v.drain()
            v.scalar_tensor_tensor(out=mskd[:], in0=eqb[:], scalar=-10.0, in1=probs[:],
                                   op0=ALU.mult, op1=ALU.add)
            v.drain()
            v.reduce_max(out=m2b[:].rearrange("p (g o) -> p g o", o=1), in_=g3(mskd[:]),
                         axis=AX.X)
            v.drain()
            v.tensor_sub(out=d12[:], in0=m1[:], in1=m2b[:])
            v.drain()
            v.tensor_scalar(out=cond[:], in0=cnt[:], scalar1=1.5, scalar2=None, op0=ALU.is_gt)
            v.drain()
            v.tensor_mul(out=t1b[:], in0=d12[:], in1=cond[:])
            v.drain()
            v.tensor_add(out=thr[:], in0=t1b[:], in1=m2b[:])
            v.drain()
            v.tensor_tensor(out=g3(pdf[:]), in0=g3(probs[:]), in1=b3(thr[:]), op=ALU.subtract)
            dinc(v, dv)
            v.wait_ge(ca.h, 7)
            v.tensor_mul(out=wu[:], in0=probs[:], in1=sgm[:])
            v.drain()
            v.reduce_sum(out=ws[:].rearrange("p (g o) -> p g o", o=1), in_=g3(wu[:]),
                         axis=AX.X)
            v.drain()
            v.tensor_scalar_add(out=wsp[:], in0=ws[:], scalar1=1e-9)
            dinc(v, dv)
            v.wait_ge(ca.h, 8)
            v.reciprocal(out=rw[:], in_=cpw[:])
            v.drain()
            v.tensor_tensor(out=g3(wgt[:]), in0=g3(wu[:]), in1=b3(rw[:]), op=ALU.mult)
            dinc(v, dv)
            for ac in range(8):
                v.wait_ge(tm.h, 2 + ac)
                v.tensor_copy(out=wfm[:, ac * 128:(ac + 1) * 128], in_=pwt[:])
                dinc(v, dv)
            v.wait_ge(ca.h, 10)
            for par in range(2):
                v.tensor_copy(out=pf[:, par:2048:2], in_=pm[:])
            v.drain()
            v.scalar_tensor_tensor(out=zpb[:], in0=zfm[:], scalar=GSC, in1=pf[:],
                                   op0=ALU.mult, op1=ALU.mult)
            dinc(v, dv)
            for tc in range(16):
                v.wait_ge(lx.h, 16 * (tc + 1))
                if tc >= 2:
                    v.wait_ge(se.h, 16 * (tc - 1))
                for g2 in range(2):
                    v.wait_ge(tm.h, 11 + 2 * tc + g2 + 1)
                    for fo in range(4):
                        f = g2 * 4 + fo
                        v.tensor_tensor(out=x4t[tc % 2][:, f * 512:(f + 1) * 512],
                                        in0=psd[fo][:],
                                        in1=x3t[tc % 2][:, f * 512:(f + 1) * 512],
                                        op=ALU.add)
                    dinc(v, dv)

        @blk.scalar
        def _(s):
            s.wait_ge(dv.h, 1)
            s.activation(out=cp1[:], in_=mx2[:], func=AF.Copy)
            dinc(s, ca)
            s.wait_ge(dv.h, 2)
            s.activation(out=cp2[:], in_=mx2c[:], func=AF.Copy)
            dinc(s, ca)
            s.wait_ge(dv.h, 3)
            s.activation(out=sc2[:], in_=rcp[:], func=AF.Copy, scale=GSC)
            dinc(s, ca)
            s.wait_ge(tm.h, 1)
            s.activation(out=scb[:], in_=psc[:], func=AF.Copy)
            dinc(s, ca)
            s.wait_ge(dv.h, 4)
            s.activation(out=elg[:], in_=lg[:], func=AF.Exp)
            dinc(s, ca)
            s.wait_ge(dv.h, 5)
            s.activation(out=cp8[:], in_=sm8[:], func=AF.Copy)
            dinc(s, ca)
            s.wait_ge(dv.h, 6)
            s.activation(out=sgm[:], in_=pdf[:], func=AF.Sigmoid, scale=2.0)
            dinc(s, ca)
            s.wait_ge(dv.h, 7)
            s.activation(out=cpw[:], in_=wsp[:], func=AF.Copy)
            dinc(s, ca)
            for j in range(2):
                s.wait_ge(tm.h, 10 + j)
                s.activation(out=pm[:, j * 512:(j + 1) * 512], in_=ppm[:], func=AF.Copy)
                dinc(s, ca)

        @blk.tensor
        def _(t):
            t.wait_ge(ld.h, 16 * 7)
            t.wait_ge(ca.h, 3)
            ti = t.matmul(psc[:], onesf[:], sc2[:], start=True, stop=True)
            tm.inc(ti)
            t.wait_ge(dv.h, 8)
            for ac in range(8):
                if ac >= 1:
                    t.wait_ge(dv.h, 8 + ac)
                ti = t.transpose(pwt[:], wgt[:, ac * 8:(ac + 1) * 8], idb[:])
                tm.inc(ti)
            t.wait_ge(dv.h, 16)
            for j in range(2):
                if j >= 1:
                    t.wait_ge(ca.h, 9)
                ti = t.matmul(ppm[:], limb[:], wfm[:, j * 512:(j + 1) * 512],
                              start=True, stop=True)
                tm.inc(ti)
            t.wait_ge(dv.h, 17)
            for tc in range(16):
                for g2 in range(2):
                    if not (tc == 0 and g2 == 0):
                        t.wait_ge(dv.h, 17 + 2 * tc + g2)
                    for fo in range(4):
                        f = g2 * 4 + fo
                        ti = t.matmul(psd[fo][:], zpb[:, tc * 128:(tc + 1) * 128],
                                      bmb[:, f * 512:(f + 1) * 512], start=True, stop=True)
                    tm.inc(ti)

        @blk.sync
        def _(sy):
            sy.wait_ge(se.h, 16 * 16)
    return nc


def build_p8():
    nc = bass.Bass("TRN2", target_bir_lowering=False, debug=False)
    ctx_my = nc.dram_tensor("ctx_my", [TOK, D], BF16, kind="ExternalInput").ap()
    w1o = nc.dram_tensor("w1o", [D, D], F32, kind="ExternalInput").ap()
    b1o = nc.dram_tensor("b1o", [1, D], F32, kind="ExternalInput").ap()
    x2_my = nc.dram_tensor("x2_my", [TOK, D], F32, kind="ExternalInput").ap()
    x3_my = nc.dram_tensor("x3_my", [TOK, D], F32, kind="ExternalOutput").ap()
    n3_my = nc.dram_tensor("n3_my", [TOK, D], BF16, kind="ExternalOutput").ap()
    emit_proj_tm(nc, "p8", ctx_my, w1o, False, x3_my, res=x2_my, bias=b1o)
    emit_ln(nc, x3_my, n3_my, n_rows=TOK, tag="p8ln")
    return nc


def build_p9a():
    nc = bass.Bass("TRN2", target_bir_lowering=False, debug=False)
    io = {
        "n3_my": nc.dram_tensor("n3_my", [TOK, D], BF16, kind="ExternalInput").ap(),
        "W1T": nc.dram_tensor("W1T", [D, D], F32, kind="ExternalInput").ap(),
        "b1": nc.dram_tensor("b1", [1, D], F32, kind="ExternalInput").ap(),
        "A1": nc.dram_tensor("A1", [D, 32], F32, kind="ExternalInput").ap(),
        "LiMEsT": nc.dram_tensor("LiMEsT", [32, 8], F32, kind="ExternalInput").ap(),
        "identf": nc.dram_tensor("identf", [128, 128], F32, kind="ExternalInput").ap(),
        "x3_my": nc.dram_tensor("x3_my", [TOK, D], F32, kind="ExternalInput").ap(),
        "x3p_my": nc.dram_tensor("x3p_my", [TOK, D], F32, kind="ExternalOutput").ap(),
        "z_my": nc.dram_tensor("z_my", [32, TOK], F32, kind="ExternalOutput").ap(),
        "h8_my": nc.dram_tensor("h8_my", [128, 64], F32, kind="ExternalOutput").ap(),
        "ds_my": nc.dram_tensor("ds_my", [128, 64], F32, kind="ExternalOutput").ap(),
        "mx_my": nc.dram_tensor("mx_my", [2, 1], F32, kind="ExternalOutput").ap(),
    }
    emit_proj_tm(nc, "p9h", io["n3_my"], io["W1T"], False, io["x3p_my"],
                 res=io["x3_my"], bias=io["b1"])
    emit_p9z(nc, io)
    return nc


def build_p9b():
    nc = bass.Bass("TRN2", target_bir_lowering=False, debug=False)
    io = {
        "mx_all": nc.dram_tensor("mx_all", [1, 16], F32, kind="ExternalInput").ap(),
        "h8_my": nc.dram_tensor("h8_my", [128, 64], F32, kind="ExternalInput").ap(),
        "ds_my": nc.dram_tensor("ds_my", [128, 64], F32, kind="ExternalInput").ap(),
        "z_my": nc.dram_tensor("z_my", [32, TOK], F32, kind="ExternalInput").ap(),
        "LiMEs": nc.dram_tensor("LiMEs", [8, 32], F32, kind="ExternalInput").ap(),
        "Bmat": nc.dram_tensor("Bmat", [32, D], F32, kind="ExternalInput").ap(),
        "x3p_my": nc.dram_tensor("x3p_my", [TOK, D], F32, kind="ExternalInput").ap(),
        "ident": nc.dram_tensor("ident", [128, 128], BF16, kind="ExternalInput").ap(),
        "x4_my": nc.dram_tensor("x4_my", [TOK, D], F32, kind="ExternalOutput").ap(),
    }
    emit_p9b(nc, io)
    return nc


def build_p6():
    nc = bass.Bass("TRN2", target_bir_lowering=False, debug=False)
    io = {
        "n2_full": nc.dram_tensor("n2_full", [S, D], BF16, kind="ExternalInput").ap(),
        "w1qk": nc.dram_tensor("w1qk", [D, 4096], F32, kind="ExternalInput").ap(),
        "w1v": nc.dram_tensor("w1v", [D, 2048], F32, kind="ExternalInput").ap(),
        "b1qk": nc.dram_tensor("b1qk", [128, 32], F32, kind="ExternalInput").ap(),
        "b1v2": nc.dram_tensor("b1v2", [128, 16], F32, kind="ExternalInput").ap(),
        "qk1": nc.dram_tensor("qk1", [4096, S], BF16, kind="ExternalOutput").ap(),
        "v1fm": nc.dram_tensor("v1fm", [2048, S], BF16, kind="ExternalOutput").ap(),
    }
    emit_p6(nc, io)
    return nc


def build_p5():
    nc = bass.Bass("TRN2", target_bir_lowering=False, debug=False)
    n1_my = nc.dram_tensor("n1_my", [TOK, D], BF16, kind="ExternalInput").ap()
    wdeq = nc.dram_tensor("wdeq", [D, D], BF16, kind="ExternalInput").ap()
    fbias = nc.dram_tensor("fbias", [1, D], F32, kind="ExternalInput").ap()
    x1_my = nc.dram_tensor("x1_my", [TOK, D], F32, kind="ExternalInput").ap()
    x2_my = nc.dram_tensor("x2_my", [TOK, D], F32, kind="ExternalOutput").ap()
    n2_my = nc.dram_tensor("n2_my", [TOK, D], BF16, kind="ExternalOutput").ap()
    emit_proj_tm(nc, "p5", n1_my, wdeq, True, x2_my, res=x1_my, bias=fbias)
    emit_ln(nc, x2_my, n2_my, n_rows=TOK, tag="p5ln")
    return nc



def build_fused():
    nc = bass.Bass("TRN2", target_bir_lowering=False, debug=False)

    def inp(name, shape, dt):
        return nc.dram_tensor(name, shape, dt, kind="ExternalInput").ap()

    def mid(name, shape, dt):
        return nc.dram_tensor(name, shape, dt, kind="Internal").ap()

    io = {}
    io["x_full"] = inp("x_full", [S, D], F32)
    io["x_my"] = inp("x_my", [TOK, D], F32)
    io["pack_my"] = inp("pack_my", [512, 512], I32)
    io["wsc_my"] = inp("wsc_my", [512, 32], F32)
    io["wqk_t"] = inp("wqk_t", [D, 4096], F32)
    io["wv_t"] = inp("wv_t", [D, 2048], F32)
    io["wg_t"] = inp("wg_t", [D, 2048], F32)
    io["nw_bc"] = inp("nw_bc", [128, 2048], F32)
    io["wo_t"] = inp("wo_t", [D, D], F32)
    io["fbias"] = inp("fbias", [1, D], F32)
    io["w1qk"] = inp("w1qk", [D, 4096], F32)
    io["w1v"] = inp("w1v", [D, 2048], F32)
    io["b1qk"] = inp("b1qk", [128, 32], F32)
    io["b1v2"] = inp("b1v2", [128, 16], F32)
    io["sl_kpos"] = inp("sl_kpos", [16, 128, 32], F32)
    io["slqp"] = inp("slqp", [16, S], F32)
    io["dmask"] = inp("dmask", [4, 128, 512], F32)
    io["w1o"] = inp("w1o", [D, D], F32)
    io["b1o"] = inp("b1o", [1, D], F32)
    io["W1T"] = inp("W1T", [D, D], F32)
    io["b1"] = inp("b1", [1, D], F32)
    io["A1"] = inp("A1", [D, 32], F32)
    io["LiMEsT"] = inp("LiMEsT", [32, 8], F32)
    io["LiMEs"] = inp("LiMEs", [8, 32], F32)
    io["Bmat"] = inp("Bmat", [32, D], F32)
    io["identf"] = inp("identf", [128, 128], F32)
    io["ident"] = inp("ident", [128, 128], BF16)
    io["qd_bc"] = inp("qd_bc", [16, 128, 256], F32)
    io["kd"] = inp("kd", [16, 128, 2], F32)
    io["bd"] = inp("bd", [16, 128, 1], F32)
    io["ddT"] = inp("ddT", [16, 256, 256], F32)
    io["off_p4"] = inp("off_p4", [1, 1], I32)
    io["off_p8"] = inp("off_p8", [1, 1], I32)

    io["n0_full"] = mid("n0_full", [S, D], BF16)
    io["wdeq_my"] = mid("wdeq_my", [512, D], BF16)
    wdeq_ag = mid("wdeq_ag", [D, D], BF16)
    io["qk0"] = mid("qk0", [4096, S], BF16)
    io["v0fm"] = mid("v0fm", [2048, S], BF16)
    io["g0fm"] = mid("g0fm", [2048, S], BF16)
    io["attn0"] = mid("attn0", [S, 2048], BF16)
    io["ss_my"] = mid("ss_my", [128, 32], F32)
    io["ss_ag"] = mid("ss_ag", [256, 32], F32)
    io["gn0"] = mid("gn0", [S, 2048], BF16)
    gn_ag = mid("gn_ag", [2 * S, 2048], BF16)
    x1 = mid("x1", [TOK, D], F32)
    n1 = mid("n1", [TOK, D], BF16)
    x2 = mid("x2", [TOK, D], F32)
    n2 = mid("n2", [TOK, D], BF16)
    io["n2_full"] = mid("n2_ag", [S, D], BF16)
    io["qk1"] = mid("qk1", [4096, S], BF16)
    io["v1fm"] = mid("v1fm", [2048, S], BF16)
    io["ctx1"] = mid("ctx1", [2048, S], BF16)
    ctx_ag = mid("ctx_ag", [D, S], BF16)
    x3 = mid("x3", [TOK, D], F32)
    io["n3_my"] = mid("n3", [TOK, D], BF16)
    io["x3p_my"] = mid("x3p", [TOK, D], F32)
    io["z_my"] = mid("z_my", [32, TOK], F32)
    io["h8_my"] = mid("h8_my", [128, 64], F32)
    io["ds_my"] = mid("ds_my", [128, 64], F32)
    io["mx_my"] = mid("mx_my", [2, 1], F32)
    mx_g = mid("mx_g", [2, 1], F32)
    io["x3_my"] = x3
    io["x4_my"] = nc.dram_tensor("x4_my", [TOK, D], F32, kind="ExternalOutput").ap()

    pairs = [[0, 1], [2, 3], [4, 5], [6, 7]]
    all8 = [[0, 1, 2, 3, 4, 5, 6, 7]]

    emit_ln(nc, io["x_full"], io["n0_full"], n_rows=S, tag="p0")
    emit_dequant(nc, io)
    emit_p1(nc, io)
    emit_p2(nc, io)
    emit_p3(nc, io, ss_ag=True, ss_coll=(pairs, io["ss_my"], io["ss_ag"]))
    gn_loc = mid("gn_loc", [TOK, D], BF16)
    emit_coll_reshard(nc, "gn", "AllGather", ALU.bypass, pairs, io["gn0"], gn_ag, [
        (gn_loc[:, 0:2048], gn_ag.tensor, [[2048, 2048], [1, 2048]], 0),
        (gn_loc[:, 2048:4096], gn_ag.tensor, [[2048, 2048], [1, 2048]], S * 2048),
    ], io["off_p4"])
    emit_proj_tm(nc, "p4", gn_loc, io["wo_t"], False, x1, res=io["x_my"])
    emit_ln(nc, x1, n1, n_rows=TOK, tag="p4l")
    emit_coll(nc, "wq", "AllGather", ALU.bypass, all8, io["wdeq_my"], wdeq_ag)
    emit_proj_tm(nc, "p5", n1, wdeq_ag, True, x2, res=x1, bias=io["fbias"])
    emit_ln(nc, x2, n2, n_rows=TOK, tag="p5l")
    emit_coll(nc, "n2", "AllGather", ALU.bypass, pairs, n2, io["n2_full"])
    emit_p6(nc, io)
    emit_p7(nc, io)
    cx_loc = mid("cx_loc", [D, TOK], BF16)
    emit_coll_reshard(nc, "cx", "AllGather", ALU.bypass, pairs, io["ctx1"], ctx_ag, [
        (cx_loc[:, :], ctx_ag.tensor, [[4096, 4096], [1, 2048]], 0),
    ], io["off_p8"])
    emit_proj_tm(nc, "p8", cx_loc, io["w1o"], False, x3, res=x2, bias=io["b1o"],
                 act_mode="fm")
    emit_ln(nc, x3, io["n3_my"], n_rows=TOK, tag="p8l")
    emit_proj_tm(nc, "p9h", io["n3_my"], io["W1T"], False, io["x3p_my"],
                 res=x3, bias=io["b1"])
    emit_p9z(nc, io)
    io["mx_all"] = mx_g.rearrange("t o -> o t")
    emit_p9b(nc, io, mx_pre=True, mx_coll=(all8, io["mx_my"], mx_g))
    return nc


_IN_KEYS = ["x_my", "pack_my", "wsc_my", "wqk_t", "wv_t", "wg_t", "nw_bc", "wo_t",
            "fbias", "w1qk", "w1v", "b1qk", "b1v2", "sl_kpos", "slqp", "dmask",
            "w1o", "b1o", "W1T", "b1", "A1", "LiMEsT", "LiMEs", "Bmat", "identf",
            "ident", "qd_bc", "kd", "bd", "ddT"]


def kernel(**inputs):
    pc = host_prep(inputs)
    cc = np.ascontiguousarray
    x = np.asarray(inputs["L0_attn_hidden_states"])
    nc = build_fused()
    in_maps = []
    for c in range(NCORE):
        b, h = c // 2, c % 2
        d = {k: cc(np.asarray(pc[c][k]).astype(np.float32)
                   if np.asarray(pc[c][k]).dtype == np.float64 else pc[c][k])
             for k in _IN_KEYS}
        d["fbias"] = cc(d["fbias"].astype(np.float32))
        d["b1o"] = cc(d["b1o"].astype(np.float32))
        d["b1"] = cc(d["b1"].astype(np.float32))
        d["x_full"] = cc(x[b])
        d["off_p4"] = np.array([[h * 2048 * 2048]], np.int32)
        d["off_p8"] = np.array([[h * 2048]], np.int32)
        in_maps.append(d)
    res = run_bass_kernel_spmd(nc, in_maps, core_ids=list(range(NCORE)))
    out = np.zeros((B, S, D), np.float32)
    for c in range(NCORE):
        b, h = c // 2, c % 2
        out[b, h * TOK:(h + 1) * TOK] = res.results[c]["x4_my"]
    return out



def _run(nc, in_maps):
    res = run_bass_kernel_spmd(nc, in_maps, core_ids=list(range(NCORE)))
    return res.results


def kernel_unfused(**inputs):
    pc = host_prep(inputs)
    cc = np.ascontiguousarray

    r0 = _run(build_p0(), [{k: cc(pc[c][k]) for k in ["x_my", "pack_my", "wsc_my"]}
                           for c in range(NCORE)])
    n0_full = [np.concatenate([r0[2 * b]["n0_my"], r0[2 * b + 1]["n0_my"]], 0)
               for b in range(4)]
    wdeq_all = np.concatenate([r0[c]["wdeq_my"] for c in range(NCORE)], 0)

    r1 = _run(build_p1(), [dict(n0_full=n0_full[c // 2],
                                **{k: cc(pc[c][k]) for k in ["wqk_t", "wv_t", "wg_t"]})
                           for c in range(NCORE)])

    r2 = _run(build_p2(), [dict(qk0=r1[c]["qk0"], v0fm=r1[c]["v0fm"],
                                **{k: cc(pc[c][k]) for k in ["qd_bc", "kd", "bd", "ddT"]})
                           for c in range(NCORE)])

    r3 = _run(build_p3(), [dict(attn0=r2[c]["attn0"], g0fm=r1[c]["g0fm"],
                                nw_bc=cc(pc[c]["nw_bc"]), ss_my=r2[c]["ss_my"],
                                ss_pt=r2[c ^ 1]["ss_my"]) for c in range(NCORE)])

    gn_my = []
    for c in range(NCORE):
        b, h = c // 2, c % 2
        gn_all = np.concatenate([r3[2 * b]["gn0"], r3[2 * b + 1]["gn0"]], 1)
        gn_my.append(cc(gn_all[h * TOK:(h + 1) * TOK]))
    r4 = _run(build_p4(), [dict(gn_my=gn_my[c], wo_t=cc(pc[c]["wo_t"]),
                                x_my=cc(pc[c]["x_my"])) for c in range(NCORE)])

    r5 = _run(build_p5(), [dict(n1_my=r4[c]["n1_my"], wdeq=wdeq_all,
                                fbias=cc(pc[c]["fbias"].astype(np.float32)),
                                x1_my=r4[c]["x1_my"]) for c in range(NCORE)])
    n2_full = [np.concatenate([r5[2 * b]["n2_my"], r5[2 * b + 1]["n2_my"]], 0)
               for b in range(4)]

    r6 = _run(build_p6(), [dict(n2_full=n2_full[c // 2],
                                **{k: cc(pc[c][k]) for k in ["w1qk", "w1v", "b1qk", "b1v2"]})
                           for c in range(NCORE)])

    r7 = _run(build_p7(), [dict(qk1=r6[c]["qk1"], v1fm=r6[c]["v1fm"],
                                **{k: cc(pc[c][k]) for k in ["sl_kpos", "slqp", "dmask"]})
                           for c in range(NCORE)])

    ctx_my = []
    for c in range(NCORE):
        b, h = c // 2, c % 2
        ctx_all = np.concatenate([r7[2 * b]["ctx1"], r7[2 * b + 1]["ctx1"]], 0)
        ctx_my.append(cc(ctx_all.T[h * TOK:(h + 1) * TOK]))
    r8 = _run(build_p8(), [dict(ctx_my=ctx_my[c], w1o=cc(pc[c]["w1o"].astype(np.float32)),
                                b1o=cc(pc[c]["b1o"].astype(np.float32)),
                                x2_my=r5[c]["x2_my"]) for c in range(NCORE)])

    r9a = _run(build_p9a(), [dict(n3_my=r8[c]["n3_my"], x3_my=r8[c]["x3_my"],
                                  **{k: cc(np.asarray(pc[c][k]).astype(np.float32))
                                     for k in ["W1T", "b1", "A1", "LiMEsT", "identf"]})
                             for c in range(NCORE)])
    mx_all = np.concatenate([np.array([r9a[c]["mx_my"][0, 0] for c in range(NCORE)]),
                             np.array([r9a[c]["mx_my"][1, 0] for c in range(NCORE)])]
                            ).astype(np.float32)[None, :]

    r9b = _run(build_p9b(), [dict(mx_all=cc(mx_all), h8_my=r9a[c]["h8_my"],
                                  ds_my=r9a[c]["ds_my"], z_my=r9a[c]["z_my"],
                                  LiMEs=cc(pc[c]["LiMEs"].astype(np.float32)),
                                  Bmat=cc(pc[c]["Bmat"].astype(np.float32)),
                                  x3p_my=r9a[c]["x3p_my"], ident=cc(pc[c]["ident"]))
                             for c in range(NCORE)])

    out = np.zeros((B, S, D), np.float32)
    for c in range(NCORE):
        b, h = c // 2, c % 2
        out[b, h * TOK:(h + 1) * TOK] = r9b[c]["x4_my"]
    return out
